# version 12
# baseline (speedup 1.0000x reference)
import numpy as np
import ml_dtypes
import concourse.bass as bass
import concourse.mybir as mybir
from concourse.bass_utils import run_bass_kernel_spmd

F32 = mybir.dt.float32
BF16 = mybir.dt.bfloat16
I8 = mybir.dt.int8
AF = mybir.ActivationFunctionType
ALU = mybir.AluOpType
AX = mybir.AxisListType
bfdt = ml_dtypes.bfloat16

B, S, H = 2, 8192, 2048
NH, D = 16, 128
T = 2048
NCH = 16
C = 128
INTER = 8192
SCALE_BASE = 512.0
GN_EPS = 1e-5
LN_EPS = 1e-5
N_CORES = 8
NQ = 4
NTB = 32
NIB = 256
NSTRIP = 4
STRIP = 512
DBGH = 8
DBG4 = False
SERIALIZE = False



def _xpos_tables_f64(d, positions):
    half = d // 2
    xscale = (np.arange(0, d, 2, dtype=np.float64) + 0.4 * d) / (1.4 * d)
    pos = positions.astype(np.float64)
    scale = xscale[None, :] ** (pos[:, None] / SCALE_BASE)
    inv_freq = 1.0 / 10000.0 ** (np.arange(half, dtype=np.float64) / half)
    sinu = pos[:, None] * inv_freq[None, :]
    sin, cos = np.sin(sinu), np.cos(sinu)
    dup = lambda t: np.repeat(t, 2, axis=-1)
    return dup(cos * scale), dup(sin * scale), dup(cos / scale), dup(sin / scale)


def _build_host_constants():
    gammas = 1.0 - np.exp(np.linspace(np.log(1 / 32), np.log(1 / 512), NH))
    p = np.arange(T)
    cq, sq, ck, sk = _xpos_tables_f64(D, p)
    sgn = np.ones(D)
    sgn[0::2] = -1.0
    sq = sq * sgn[None, :]
    sk = sk * sgn[None, :]

    def to_col(tab):
        return np.ascontiguousarray(
            tab.reshape(NCH, C, D).transpose(1, 0, 2).reshape(C, NCH * D))

    cq_t = to_col(cq).astype(bfdt)
    sq_t = to_col(sq).astype(bfdt)
    ck_t = to_col(ck).astype(bfdt)
    sk_t = to_col(sk).astype(bfdt)

    pc = (np.arange(NCH)[None, :] * C + np.arange(C)[:, None] + 1.0)
    gq = np.empty((C, NH * NCH), np.float64)
    gk = np.empty((C, NH * NCH), np.float64)
    for n in range(NH):
        gq[:, n * NCH:(n + 1) * NCH] = gammas[n] ** pc
        gk[:, n * NCH:(n + 1) * NCH] = gammas[n] ** (-pc)
    gq = gq.astype(np.float32)
    gk = gk.astype(np.float32)

    causal = (np.arange(C)[None, :] >= np.arange(C)[:, None]).astype(np.float64)
    causal_rep = np.ascontiguousarray(causal.astype(bfdt))
    ident = np.ascontiguousarray(np.eye(C, dtype=bfdt))

    half = D // 2
    xscale = (np.arange(0, D, 2, dtype=np.float64) + 0.4 * D) / (1.4 * D)
    inv_freq = 1.0 / 10000.0 ** (np.arange(half, dtype=np.float64) / half)

    def that_lhsT(D_off, n):
        sig = xscale ** (D_off / SCALE_BASE)
        th = D_off * inv_freq
        g = float(np.float32(gammas[n] ** D_off))
        M = np.zeros((D, D))
        for i in range(half):
            cth, sth = np.cos(th[i]), np.sin(th[i])
            M[2 * i, 2 * i] = cth * sig[i] * g
            M[2 * i, 2 * i + 1] = sth * sig[i] * g
            M[2 * i + 1, 2 * i] = -sth * sig[i] * g
            M[2 * i + 1, 2 * i + 1] = cth * sig[i] * g
        return M.T

    that_by_core = []
    for core in range(N_CORES):
        bq, qq = core // NQ, core % NQ
        that = np.zeros((N_CORES, NH, D, D), np.float64)
        for r in range(N_CORES):
            rb, rq = r // NQ, r % NQ
            if rb == bq and rq < qq:
                for n in range(NH):
                    that[r, n] = that_lhsT(T * (qq - rq), n)
        that_by_core.append(np.ascontiguousarray(that.astype(bfdt)))

    return dict(cq=cq_t, sq=sq_t, ck=ck_t, sk=sk_t, gq=gq, gk=gk,
                causal=causal_rep, ident=ident, that_by_core=that_by_core)



class Buf:
    __slots__ = ("writer", "readers", "sem", "group", "writers")

    def __init__(self, sem=None, group=False):
        self.writer = None
        self.writers = []
        self.readers = []
        self.sem = sem
        self.group = group


class Sched:
    ENGINES = ("te", "act", "dve", "pool", "sp")

    def __init__(self):
        self.ops = []
        self.last_dma = {}

    def add(self, eng, fn, reads=(), writes=(), deps=(), dma=False, dsem=None,
            chain=True):
        idx = len(self.ops)
        dep = set(deps)
        if SERIALIZE and idx > 0:
            dep.add(idx - 1)
        for b in reads:
            if b.group:
                dep.update(b.writers)
            elif b.writer is not None:
                dep.add(b.writer)
        for b in writes:
            if b.group:
                dep.update(b.readers)
                b.writers.append(idx)
            else:
                if b.writer is not None:
                    dep.add(b.writer)
                dep.update(b.readers)
                b.writer = idx
                b.readers = []
        dep.discard(idx)
        for d in dep:
            assert d < idx
        if dma:
            if dsem is None and writes:
                dsem = writes[0].sem
            assert dsem is not None, "dma op needs a dsem"
            if chain and dsem in self.last_dma:
                dep.add(self.last_dma[dsem])
            self.last_dma[dsem] = idx
        self.ops.append(dict(i=idx, eng=eng, fn=fn, deps=sorted(dep), dma=dma,
                             dsem=dsem))
        for b in reads:
            b.readers.append(idx)
        for b in writes:
            if not b.group:
                pass
        return idx

    def finalize(self):
        needed = set()
        for op in self.ops:
            needed.update(op["deps"])
        counters = {}
        pos = {}
        for op in self.ops:
            op["needed"] = op["i"] in needed or op["dma"]
            if op["needed"]:
                key = ("d", op["dsem"]) if op["dma"] else ("c", op["eng"])
                inc = 16 if op["dma"] else 1
                counters[key] = counters.get(key, 0) + inc
                pos[op["i"]] = (key, counters[key])
                op["inc"] = inc
        clocks = {e: {} for e in self.ENGINES}
        for op in self.ops:
            wmap = {}
            for d in op["deps"]:
                dop = self.ops[d]
                key, tgt = pos[d]
                if (not dop["dma"]) and dop["eng"] == op["eng"] == "te":
                    continue
                if wmap.get(key, 0) < tgt:
                    wmap[key] = tgt
            clk = clocks[op["eng"]]
            ww = []
            for key in sorted(wmap):
                tgt = wmap[key]
                if clk.get(key, 0) >= tgt:
                    continue
                clk[key] = tgt
                ww.append((key, tgt))
            op["waits"] = ww

    def sem_keys(self):
        ks = set()
        for op in self.ops:
            if op.get("needed"):
                ks.add(("d", op["dsem"]) if op["dma"] else ("c", op["eng"]))
        return sorted(ks)

    def emit_engine(self, eng_name, eng_obj, sems):
        for op in self.ops:
            if op["eng"] != eng_name:
                continue
            for key, tgt in op["waits"]:
                eng_obj.wait_ge(sems[key], tgt)
            inst = op["fn"](eng_obj)
            if op["needed"]:
                assert inst is not None, f"op {op['i']} on {eng_name} returned None"
                key = ("d", op["dsem"]) if op["dma"] else ("c", op["eng"])
                inst.then_inc(sems[key], op["inc"])



def build_program(stop_after=None):
    nc = bass.Bass("TRN2", target_bir_lowering=False)

    dr = {}
    ei = "ExternalInput"
    dr["x"] = nc.dram_tensor("x", [T, H], F32, kind=ei)
    for nm in ("wq", "wk", "wv", "wg"):
        dr[nm] = nc.dram_tensor(nm, [16, C, 16, D], F32, kind=ei)
    dr["wo"] = nc.dram_tensor("wo", [H, H], F32, kind=ei)
    dr["gnw"] = nc.dram_tensor("gnw", [C, H], F32, kind=ei)
    dr["gnb"] = nc.dram_tensor("gnb", [C, H], F32, kind=ei)
    for nm in ("cq", "sq", "ck", "sk"):
        dr[nm] = nc.dram_tensor(nm, [C, T], BF16, kind=ei)
    dr["causal"] = nc.dram_tensor("causal", [C, C], BF16, kind=ei)
    dr["gq"] = nc.dram_tensor("gq", [C, NH * NCH], F32, kind=ei)
    dr["gk"] = nc.dram_tensor("gk", [C, NH * NCH], F32, kind=ei)
    dr["ident"] = nc.dram_tensor("ident", [C, C], BF16, kind=ei)
    dr["that"] = nc.dram_tensor("that", [N_CORES, NH, D, D], BF16, kind=ei)
    dr["wgt"] = nc.dram_tensor("wgt", [NTB, 2, C, 16, 128], I8, kind=ei)
    dr["wut"] = nc.dram_tensor("wut", [NTB, 2, C, 16, 128], I8, kind=ei)
    dr["wdt"] = nc.dram_tensor("wdt", [2, 16, C, 4, 1024], I8, kind=ei)
    dr["sgate"] = nc.dram_tensor("sgate", [C, INTER // C], F32, kind=ei)
    dr["sup"] = nc.dram_tensor("sup", [C, INTER // C], F32, kind=ei)
    dr["sdown"] = nc.dram_tensor("sdown", [C, H], F32, kind=ei)
    dr["y"] = nc.dram_tensor("y", [T, H], F32, kind="ExternalOutput")
    dr["vqg"] = nc.dram_tensor("vqg", [NH, 3, C, T], BF16, kind="Internal")
    dr["ztsp"] = nc.dram_tensor("ztsp", [NH, C, T], BF16, kind="Internal")
    dr["y1sp"] = nc.dram_tensor("y1sp", [NCH, C, H], F32, kind="Internal")
    dr["ccin"] = nc.dram_tensor("ccin", [C, NH * D + 0], BF16, kind="Internal")
    dr["ccout"] = nc.dram_tensor("ccout", [N_CORES * C, NH * D], BF16, kind="Internal")

    A = {k: v.ap() for k, v in dr.items()}

    XT = nc.alloc_sbuf_tensor("XTa", [128, 16 * T], BF16)
    KT = nc.alloc_sbuf_tensor("KTa", [128, 16 * T], BF16)
    COLS = nc.alloc_sbuf_tensor("COLSa", [128, 8 * T], BF16)
    F32A = nc.alloc_sbuf_tensor("F32Aa", [128, 3 * H], F32)
    TAB = nc.alloc_sbuf_tensor("TABa", [128, 2 * T], BF16)
    CAU = nc.alloc_sbuf_tensor("CAUa", [128, C], BF16)
    GNBB = nc.alloc_sbuf_tensor("GNBBa", [128, H], BF16)
    GAM = nc.alloc_sbuf_tensor("GAMa", [128, 2 * NH * NCH], F32)
    IDN = nc.alloc_sbuf_tensor("IDNa", [128, C], BF16)
    GUB = nc.alloc_sbuf_tensor("GUBa", [128, 2 * STRIP], BF16)
    ST = nc.alloc_sbuf_tensor("STa", [128, 180], F32)
    SNB = nc.alloc_sbuf_tensor("SNBa", [128, T], BF16)
    SGC = nc.alloc_sbuf_tensor("SGCa", [128, 2 * (INTER // C)], F32)

    xt = XT.ap(); kt = KT.ap(); cols = COLS.ap(); f32a = F32A.ap()
    tab = TAB.ap(); cau = CAU.ap(); gnbb = GNBB.ap(); gam = GAM.ap()
    idn = IDN.ap(); gub = GUB.ap(); st = ST.ap(); snb = SNB.ap(); sgc = SGC.ap()
    tabf = tab.bitcast(F32)

    PA = nc.alloc_psum_tensor("PAa", [128, 2048], F32)
    PB = nc.alloc_psum_tensor("PBa", [128, 2048], F32)
    pa = PA.ap(); pb = PB.ap()
    pbb = pb.bitcast(BF16)

    Sc = Sched()
    bXT = [Buf(sem="xt") for _ in range(16)]
    bKT = [Buf() for _ in range(16)]
    bCOL = [Buf(sem=f"cols{i}") for i in range(8)]
    bF32 = [Buf(sem=f"f32{i}") for i in range(3)]
    bTAB = Buf(sem="tab"); bCAU = Buf(sem="cau"); bGNB = Buf(sem="gnb")
    bGAM = Buf(sem="gam"); bIDN = Buf(sem="idn")
    bGUB = [Buf(), Buf()]; bST = Buf(); bSNB = Buf(sem="snb"); bSGC = Buf(sem="sgc")
    bKTT = [Buf() for _ in range(16)]
    bBank = [Buf() for _ in range(8)]
    bDR = {k: Buf(sem=k, group=True)
           for k in ("vqg", "ztsp", "y1sp")}
    bDR["ccin"] = Buf(sem="ccin")
    bDR["ccout"] = Buf()

    banksA = [bBank[i] for i in range(4)]
    banksB = [bBank[i] for i in range(4, 8)]
    bTRhi = [bBank[6], bBank[7]]
    bTRlo = [bBank[4], bBank[5]]

    cs = lambda i: cols[:, i * T:(i + 1) * T]
    xts = lambda i: xt[:, i * T:(i + 1) * T]
    pab = pa.bitcast(BF16)
    fs = lambda i: f32a[:, i * H:(i + 1) * H]
    sti = lambda i: st[:, i * 20:i * 20 + 16]
    st1 = lambda i: st[:, i * 20:i * 20 + 1]

    eps_col = st[:, 176:177]
    Sc_eps_done = []

    def dma(engq, out, in_, reads=(), writes=(), deps=(), dsem=None, chain=True):
        return Sc.add(engq, lambda e, o=out, i=in_: e.dma_start(out=o, in_=i),
                      reads=reads, writes=writes, deps=deps, dma=True, dsem=dsem,
                      chain=chain)

    def finish():
        from contextlib import ExitStack
        Sc.finalize()
        with ExitStack() as stack:
            sems = {}
            for key in Sc.sem_keys():
                nm = ("cs_" + key[1]) if key[0] == "c" else ("ds_" + str(key[1]))
                sems[key] = stack.enter_context(nc.semaphore(nm))
            block = stack.enter_context(nc.Block())

            @block.tensor
            def _(eng):
                Sc.emit_engine("te", eng, sems)

            @block.scalar
            def _(eng):
                Sc.emit_engine("act", eng, sems)

            @block.vector
            def _(eng):
                Sc.emit_engine("dve", eng, sems)

            @block.gpsimd
            def _(eng):
                Sc.emit_engine("pool", eng, sems)

            @block.sync
            def _(eng):
                Sc.emit_engine("sp", eng, sems)
        return nc

    def dbg_sbuf(nm, ap, bufs, dtype=BF16, width=16 * T):
        dt_ = nc.dram_tensor(nm, [128, width], dtype, kind="ExternalOutput")
        dd = dma("sp", dt_.ap()[:, :], ap, reads=bufs, writes=[], dsem="out")
        Sc.add("sp", lambda e: e.nop(), deps=[dd])

    def dbg_dram(nm, key, shape, dtype):
        dt_ = nc.dram_tensor(nm, shape, dtype, kind="ExternalOutput")
        idx = tuple(slice(None) for _ in shape)
        dd = dma("sp", dt_.ap()[idx], A[key][idx], reads=[bDR[key]], writes=[], dsem="out")
        Sc.add("sp", lambda e: e.nop(), deps=[dd])

    Sc.add("dve", lambda e: e.memset(eps_col, 1e-5), writes=[bST])
    dma("sp", cau[:, :], A["causal"][:, :], writes=[bCAU])
    dma("sp", idn[:, :], A["ident"][:, :], writes=[bIDN])
    dma("sp", gam[:, 0:NH * NCH], A["gq"][:, :], writes=[bGAM])
    dma("sp", gam[:, NH * NCH:2 * NH * NCH], A["gk"][:, :], writes=[bGAM])
    dma("sp", tab[:, 0:T], A["ck"][:, :], writes=[bTAB])
    dma("sp", tab[:, T:2 * T], A["sk"][:, :], writes=[bTAB])

    for c_ in range(NCH):
        pp = c_ % 2
        xst = fs(pp)
        dma("sp", xst, A["x"][c_ * C:(c_ + 1) * C, :], reads=[bF32[pp]],
            writes=[bF32[pp]], chain=False)
        xb = cs(pp)
        Sc.add("act", lambda e, o=xb, i=xst: e.copy(o, i),
               reads=[bF32[pp]], writes=[bCOL[pp]])

        def mk_tp(src=xb):
            def f(e):
                last = None
                for ht in range(16):
                    last = nc.tensor.matmul(
                        pbb[:, 2048 + ht * 128:2048 + (ht + 1) * 128],
                        src[:, ht * 128:(ht + 1) * 128], idn[:, :],
                        start=(ht % 8 == 0), stop=(ht % 8 == 7),
                        is_transpose=True, skip_group_check=True)
                return last
            return f
        Sc.add("te", mk_tp(), reads=[bCOL[pp], bIDN], writes=bTRhi)
        dst = xt.rearrange("p (ht t) -> p ht t", ht=16)[:, :, c_ * C:(c_ + 1) * C]
        src = pbb[:, 2048:4096].rearrange("p (ht a) -> p ht a", ht=16)
        Sc.add("act", lambda e, o=dst, i=src: e.copy(o, i),
               reads=bTRhi, writes=list(bXT))

    if stop_after == "p0":
        dbg_sbuf("d_xt", xt[:, :], list(bXT))
        return finish()

    def proj_tile(n, wap, wbuf, wapb, out_ap, out_buf):
        dma("pool", wapb.rearrange("p (ht d) -> p ht d", ht=16),
            wap[n], reads=[wbuf], writes=[wbuf], chain=False)
        for half in range(2):
            def mk_mm(half=half, wapb=wapb):
                def f(e):
                    last = None
                    for ht in range(16):
                        for j in range(2):
                            off = ht * T + half * 1024 + j * 512
                            last = nc.tensor.matmul(
                                pb[:, j * 512:(j + 1) * 512],
                                wapb[:, ht * D:(ht + 1) * D],
                                xt[:, off:off + 512],
                                start=(ht == 0), stop=(ht == 15),
                                skip_group_check=True)
                    return last
                return f
            Sc.add("te", mk_mm(), reads=[wbuf] + list(bXT), writes=bTRlo)
            Sc.add("act",
                   lambda e, o=out_ap[:, half * 1024:(half + 1) * 1024],
                   i=pb[:, 0:1024]: e.copy(o, i),
                   reads=bTRlo, writes=[out_buf])

    def transpose16(src_ap, src_buf, dst_ap, dst_bufs, region="hi"):
        reg_off = 2048 if region == "hi" else 0
        reg_bufs = bTRhi if region == "hi" else bTRlo

        def f(e, src_ap=src_ap, reg_off=reg_off):
            last = None
            for c_ in range(16):
                last = nc.tensor.matmul(
                    pbb[:, reg_off + c_ * 128:reg_off + (c_ + 1) * 128],
                    src_ap[:, c_ * 128:(c_ + 1) * 128], idn[:, :],
                    start=(c_ % 8 == 0), stop=(c_ % 8 == 7),
                    is_transpose=True, skip_group_check=True)
            return last
        Sc.add("te", f, reads=[src_buf, bIDN], writes=reg_bufs)
        Sc.add("act",
               lambda e, o=dst_ap, i=pbb[:, reg_off:reg_off + 2048]: e.copy(o, i),
               reads=reg_bufs, writes=list(dst_bufs))

    def xpos_apply(col_ap, col_buf, gam_off, out_ap, out_bufs):
        tmp = fs(0)
        sw = col_ap.rearrange("p (c k two) -> p c k two", c=NCH, two=2)[:, :, :, ::-1]
        stv = tab[:, T:2 * T].rearrange("p (c k two) -> p c k two", c=NCH, two=2)
        Sc.add("dve", lambda e, o=tmp, i0=sw, i1=stv:
               e.tensor_tensor(o.rearrange("p (c k two) -> p c k two", c=NCH, two=2),
                               i0, i1, ALU.mult),
               reads=[col_buf, bTAB], writes=[bF32[0]])
        Sc.add("dve", lambda e, o=col_ap, i1=tab[:, 0:T]:
               e.tensor_tensor(o, o, i1, ALU.mult),
               reads=[col_buf, bTAB], writes=[col_buf])
        Sc.add("dve", lambda e, o=col_ap, i1=tmp:
               e.tensor_tensor(o, o, i1, ALU.add),
               reads=[col_buf, bF32[0]], writes=[col_buf])
        gview = gam[:, gam_off:gam_off + NCH].unsqueeze(2).broadcast_to([128, NCH, D])
        Sc.add("dve", lambda e, o=out_ap, i0=col_ap, i1=gview:
               e.tensor_tensor(o.rearrange("p (c d) -> p c d", c=NCH),
                               i0.rearrange("p (c d) -> p c d", c=NCH),
                               i1, ALU.mult),
               reads=[col_buf, bGAM], writes=list(out_bufs))

    for n in range(NH):
        wkb, wka = bCOL[3 + (n % 2)], cs(3 + (n % 2))
        wvb, wva = bCOL[5 + (n % 2)], cs(5 + (n % 2))
        proj_tile(n, A["wk"], wkb, wka, cs(0), bCOL[0])
        transpose16(cs(0), bCOL[0], kt[:, n * T:(n + 1) * T], [bKT[n]])
        xpos_apply(kt[:, n * T:(n + 1) * T], bKT[n], NH * NCH + n * NCH,
                   kt[:, n * T:(n + 1) * T], [bKT[n]])
        proj_tile(n, A["wv"], wvb, wva, cs(1), bCOL[1])
        transpose16(cs(1), bCOL[1], cs(2), [bCOL[2]])
        dma("sp", A["vqg"][n, 0], cs(2), reads=[bCOL[2]], writes=[bDR["vqg"]])

        def mk_state(n=n):
            def f(e):
                last = None
                for c_ in range(16):
                    last = nc.tensor.matmul(
                        pa[:, n * D:(n + 1) * D],
                        kt[:, n * T + c_ * 128:n * T + (c_ + 1) * 128],
                        cs(2)[:, c_ * 128:(c_ + 1) * 128],
                        start=(n % 4 == 0 and c_ == 0),
                        stop=(n % 4 == 3 and c_ == 15),
                        skip_group_check=True)
                return last
            return f
        Sc.add("te", mk_state(), reads=[bKT[n], bCOL[2]], writes=[bBank[n // 4]])

    Sc.add("act", lambda e, o=cs(7), i=pa[:, :]: e.copy(o, i),
           reads=banksA, writes=[bCOL[7]])
    dma("sp", A["ccin"][:, :], cs(7), reads=[bCOL[7]], writes=[bDR["ccin"]])
    Sc.add("pool", lambda e: e.collective_compute(
        "AllGather", ALU.bypass, [list(range(N_CORES))],
        [A["ccin"][:, :]], [A["ccout"][:, :]]),
        reads=[bDR["ccin"]], writes=[bDR["ccout"]])

    if stop_after == "p1":
        dbg_sbuf("d_kt", kt[:, :], list(bKT))
        dbg_dram("d_vqg", "vqg", [NH, 3, C, T], BF16)
        dbg_dram("d_ccin", "ccin", [C, NH * D + 0], BF16)
        return finish()

    dma("sp", tab[:, 0:T], A["cq"][:, :], reads=[bTAB], writes=[bTAB])
    dma("sp", tab[:, T:2 * T], A["sq"][:, :], reads=[bTAB], writes=[bTAB])
    for n in range(NH):
        wqb, wqa = bCOL[3 + (n % 2)], cs(3 + (n % 2))
        wgb, wga = bCOL[5 + (n % 2)], cs(5 + (n % 2))
        proj_tile(n, A["wq"], wqb, wqa, cs(0), bCOL[0])
        transpose16(cs(0), bCOL[0], cs(1), [bCOL[1]])
        xpos_apply(cs(1), bCOL[1], n * NCH, cs(1), [bCOL[1]])
        dma("sp", A["vqg"][n, 1], cs(1), reads=[bCOL[1]], writes=[bDR["vqg"]])
        proj_tile(n, A["wg"], wgb, wga, cs(0), bCOL[0])
        transpose16(cs(0), bCOL[0], cs(2), [bCOL[2]])
        Sc.add("act", lambda e, o=cs(2): e.activation(o, o, AF.Silu),
               reads=[bCOL[2]], writes=[bCOL[2]])
        dma("sp", A["vqg"][n, 2], cs(2), reads=[bCOL[2]], writes=[bDR["vqg"]])

    for r in range(N_CORES):
        dma("sp", cs(r), A["ccout"][r * C:(r + 1) * C, :],
            reads=[bDR["ccout"], bCOL[r]], writes=[bCOL[r]], dsem="cco")
    for r in range(N_CORES):
        dma("sp", snb[:, :].rearrange("p (n d) -> p n d", n=NH),
            A["that"][r].rearrange("n p d -> p n d"),
            reads=[bSNB], writes=[bSNB], chain=False)

        def mk_that(r=r):
            def f(e):
                last = None
                for n in range(NH):
                    last = nc.tensor.matmul(
                        pa[:, n * D:(n + 1) * D],
                        snb[:, n * D:(n + 1) * D],
                        cs(r)[:, n * D:(n + 1) * D],
                        start=(r == 0 and n % 4 == 0),
                        stop=(r == N_CORES - 1 and n % 4 == 3),
                        skip_group_check=True)
                return last
            return f
        Sc.add("te", mk_that(), reads=[bSNB, bCOL[r]], writes=banksA)
    Sc.add("act", lambda e, o=xts(12), i=pa[:, :]: e.copy(o, i),
           reads=banksA + [bXT[12]], writes=[bXT[12]])

    if stop_after == "exch":
        dbg_sbuf("d_sin", xts(12), [bXT[12]], dtype=BF16, width=T)
        dbg_dram("d_vqg", "vqg", [NH, 3, C, T], BF16)
        return finish()

    dma("sp", tabf[:, :], A["gnw"][:, :], reads=[bTAB], writes=[bTAB])
    dma("pool", gnbb[:, :], A["gnb"][:, :], writes=[bGNB])
    bSTp = [Buf(), Buf()]

    for n in range(NH):
        P_ = n % 2
        reg = (pa, pb)[P_]
        regb = (pab, pbb)[P_]
        reg3 = reg[:, :].rearrange("p (c d) -> p c d", c=NCH)
        rbanks = [bBank[P_ * 4 + i] for i in range(4)]
        rb01 = rbanks[0:2]
        sb0 = P_ * 6
        ktT, qtT, atm = xts(sb0), xts(sb0 + 1), xts(sb0 + 2)
        zc, ztc, dsB = xts(sb0 + 3), xts(sb0 + 4), xts(sb0 + 5)
        bktT, bqtT, batm = bXT[sb0], bXT[sb0 + 1], bXT[sb0 + 2]
        bzc, bztc, bdsB = bXT[sb0 + 3], bXT[sb0 + 4], bXT[sb0 + 5]
        snbP, bsnbP = xts(13 + P_), bXT[13 + P_]
        vc, qc, gc = cs(3 * P_), cs(3 * P_ + 1), cs(3 * P_ + 2)
        bvqg = [bCOL[3 * P_], bCOL[3 * P_ + 1], bCOL[3 * P_ + 2]]
        syF, bsyF = fs(1 + P_), bF32[1 + P_]
        so = P_ * 4
        stS = lambda i: st[:, (so + i) * 20:(so + i) * 20 + 16]
        bst = bSTp[P_]

        for k, q_ in ((0, "sp"), (1, "act"), (2, "pool")):
            dma(q_, cs(3 * P_ + k), A["vqg"][n, k],
                reads=[bDR["vqg"], bvqg[k]], writes=[bvqg[k]],
                dsem=f"vqg{P_}{k}", chain=False)

        def tp16(src_ap, src_buf, dst_ap, dst_buf, regb=regb, rb01=rb01):
            def f(e, src_ap=src_ap, regb=regb):
                last = None
                for c_ in range(16):
                    last = nc.tensor.matmul(
                        regb[:, c_ * 128:(c_ + 1) * 128],
                        src_ap[:, c_ * 128:(c_ + 1) * 128], idn[:, :],
                        start=(c_ % 8 == 0), stop=(c_ % 8 == 7),
                        is_transpose=True, skip_group_check=True)
                return last
            Sc.add("te", f, reads=[src_buf, bIDN], writes=rb01)
            Sc.add("act", lambda e, o=dst_ap, i=regb[:, 0:2048]: e.copy(o, i),
                   reads=rb01, writes=[dst_buf])

        tp16(kt[:, n * T:(n + 1) * T], bKT[n], ktT, bktT)
        tp16(qc, bvqg[1], qtT, bqtT)

        def mk_amm(ktT=ktT, qtT=qtT, reg=reg):
            def f(e):
                last = None
                for c_ in range(16):
                    last = nc.tensor.matmul(
                        reg[:, c_ * 128:(c_ + 1) * 128],
                        ktT[:, c_ * 128:(c_ + 1) * 128],
                        qtT[:, c_ * 128:(c_ + 1) * 128],
                        start=(c_ % 4 == 0), stop=(c_ % 4 == 3),
                        skip_group_check=True)
                return last
            return f
        Sc.add("te", mk_amm(), reads=[bktT, bqtT], writes=rbanks)
        cau_b = cau[:, :].unsqueeze(1).broadcast_to([128, NCH, C])
        Sc.add("dve", lambda e, o=atm, i0=reg3, i1=cau_b: e.tensor_tensor(
            o.rearrange("p (c a) -> p c a", c=NCH), i0, i1, ALU.mult),
               reads=rbanks + [bCAU], writes=[batm])

        def mk_ds(n=n, vc=vc, reg=reg):
            def f(e):
                last = None
                for c_ in range(16):
                    last = nc.tensor.matmul(
                        reg[:, c_ * 128:(c_ + 1) * 128],
                        kt[:, n * T + c_ * 128:n * T + (c_ + 1) * 128],
                        vc[:, c_ * 128:(c_ + 1) * 128],
                        start=(c_ % 4 == 0), stop=(c_ % 4 == 3),
                        skip_group_check=True)
                return last
            return f
        Sc.add("te", mk_ds(), reads=[bKT[n], bvqg[0]], writes=rbanks)
        Sc.add("act", lambda e, o=dsB, i=reg[:, :]: e.copy(o, i),
               reads=rbanks, writes=[bdsB])
        Sc.add("dve", lambda e, o=reg[:, 0:128], i=xts(12)[:, n * D:(n + 1) * D]:
               e.tensor_copy(o, i),
               reads=[bXT[12], bdsB] + rbanks, writes=[rbanks[0]])
        for c_ in range(15):
            Sc.add("dve", lambda e, c_=c_, reg=reg, dsB=dsB: e.tensor_add(
                reg[:, (c_ + 1) * 128:(c_ + 2) * 128],
                reg[:, c_ * 128:(c_ + 1) * 128],
                dsB[:, c_ * 128:(c_ + 1) * 128]),
                reads=[bdsB, rbanks[c_ // 4], rbanks[(c_ + 1) // 4]],
                writes=[rbanks[(c_ + 1) // 4]])
        Sc.add("act", lambda e, o=snbP, i=reg[:, :]: e.copy(o, i),
               reads=rbanks, writes=[bsnbP])

        def mk_y(atm=atm, vc=vc, qtT=qtT, snbP=snbP, reg=reg):
            def f(e):
                last = None
                for c_ in range(16):
                    nc.tensor.matmul(
                        reg[:, c_ * 128:(c_ + 1) * 128],
                        atm[:, c_ * 128:(c_ + 1) * 128],
                        vc[:, c_ * 128:(c_ + 1) * 128],
                        start=(c_ % 4 == 0), stop=False, skip_group_check=True)
                    last = nc.tensor.matmul(
                        reg[:, c_ * 128:(c_ + 1) * 128],
                        qtT[:, c_ * 128:(c_ + 1) * 128],
                        snbP[:, c_ * 128:(c_ + 1) * 128],
                        start=False, stop=(c_ % 4 == 3), skip_group_check=True)
                return last
            return f
        Sc.add("te", mk_y(), reads=[batm, bvqg[0], bqtT, bsnbP], writes=rbanks)
        Sc.add("act", lambda e, o=syF, i=reg[:, :]: e.copy(o, i),
               reads=rbanks, writes=[bsyF])

        sy3 = syF.rearrange("p (c d) -> p c d", c=NCH)
        Sc.add("dve", lambda e, o=stS(0), i=sy3: e.reduce_sum(o, i, axis=AX.X),
               reads=[bsyF], writes=[bst])
        Sc.add("act", lambda e, o=reg[:, :], i=syF: e.activation(o, i, AF.Square),
               reads=[bsyF] + rbanks, writes=rbanks)
        Sc.add("dve", lambda e, o=stS(1), i=reg3: e.reduce_sum(o, i, axis=AX.X),
               reads=rbanks, writes=[bst])
        Sc.add("dve", lambda e, a=stS(0): e.tensor_scalar_mul(a, a, 1.0 / D),
               reads=[bst], writes=[bst])
        Sc.add("dve", lambda e, a=stS(1): e.tensor_scalar_mul(a, a, 1.0 / D),
               reads=[bst], writes=[bst])
        Sc.add("dve", lambda e, a=stS(2), b=stS(0): e.tensor_mul(a, b, b),
               reads=[bst], writes=[bst])
        Sc.add("dve", lambda e, a=stS(1), b=stS(2): e.tensor_sub(a, a, b),
               reads=[bst], writes=[bst])
        Sc.add("act", lambda e, a=stS(2), b=stS(1): e.activation(a, b, AF.Sqrt,
                                                                 bias=eps_col),
               reads=[bst], writes=[bst])
        Sc.add("dve", lambda e, a=stS(3), b=stS(2): e.reciprocal(a, b),
               reads=[bst], writes=[bst])
        mu_b = stS(0).unsqueeze(2).broadcast_to([128, NCH, D])
        rs_b = stS(3).unsqueeze(2).broadcast_to([128, NCH, D])
        gnw_b = tabf[:, n * D:(n + 1) * D].unsqueeze(1).broadcast_to([128, NCH, D])
        gnb_b = gnbb[:, n * D:(n + 1) * D].unsqueeze(1).broadcast_to([128, NCH, D])
        Sc.add("dve", lambda e, o=reg3, i0=sy3, i1=mu_b:
               e.tensor_tensor(o, i0, i1, ALU.subtract),
               reads=[bsyF, bst] + rbanks, writes=rbanks)
        Sc.add("dve", lambda e, o=reg3, i1=rs_b: e.tensor_tensor(o, o, i1,
                                                                 ALU.mult),
               reads=[bst] + rbanks, writes=rbanks)
        Sc.add("dve", lambda e, o=reg3, i1=gnw_b: e.tensor_tensor(o, o, i1,
                                                                  ALU.mult),
               reads=[bTAB] + rbanks, writes=rbanks)
        Sc.add("dve", lambda e, o=reg3, i1=gnb_b: e.tensor_tensor(o, o, i1,
                                                                  ALU.add),
               reads=[bGNB] + rbanks, writes=rbanks)
        gc3 = gc.rearrange("p (c d) -> p c d", c=NCH)
        Sc.add("dve", lambda e, o=zc, i0=reg3, i1=gc3: e.tensor_tensor(
            o.rearrange("p (c d) -> p c d", c=NCH), i0, i1, ALU.mult),
               reads=[bvqg[2]] + rbanks, writes=[bzc])
        tp16(zc, bzc, ztc, bztc)
        dma("sp", A["ztsp"][n], ztc, reads=[bztc], writes=[bDR["ztsp"]])

    if stop_after == "p2b":
        dbg_dram("d_ztsp", "ztsp", [NH, C, T], BF16)
        return finish()

    dma("pool", xt.rearrange("p (ht t) -> p ht t", ht=16),
        A["wo"].rearrange("(ht p) o -> p ht o", p=128),
        reads=list(bXT), writes=list(bXT))
    for c_ in range(NCH):
        ztc = cs(c_ % 2)
        dma("sp", ztc.rearrange("p (n a) -> p n a", n=NH),
            A["ztsp"][:, :, c_ * C:(c_ + 1) * C].rearrange("n p a -> p n a"),
            reads=[bDR["ztsp"], bCOL[c_ % 2]], writes=[bCOL[c_ % 2]], chain=False)

        regP = (pa, pb)[c_ % 2]
        regPb = (pab, pbb)[c_ % 2]
        rbk = [bBank[(c_ % 2) * 4 + i] for i in range(4)]

        def mk_wo(ztc=ztc, regP=regP):
            def f(e):
                last = None
                for ht in range(16):
                    for j in range(4):
                        last = nc.tensor.matmul(
                            regP[:, j * 512:(j + 1) * 512],
                            ztc[:, ht * 128:(ht + 1) * 128],
                            xt[:, ht * T + j * 512:ht * T + (j + 1) * 512],
                            start=(ht == 0), stop=(ht == 15),
                            skip_group_check=True)
                return last
            return f
        Sc.add("te", mk_wo(), reads=[bCOL[c_ % 2]] + list(bXT), writes=rbk)
        xres = fs(0)
        dma("act", xres, A["x"][c_ * C:(c_ + 1) * C, :],
            reads=[bF32[0]], writes=[bF32[0]], chain=False)
        y1 = fs(1)
        Sc.add("dve", lambda e, o=y1, i1=xres, i0=regP: e.tensor_add(o, i0[:, :], i1),
               reads=rbk + [bF32[0]], writes=[bF32[1]])
        Sc.add("dve", lambda e, o=st1(0), i=y1: e.reduce_sum(o, i, axis=AX.X),
               reads=[bF32[1]], writes=[bST])
        Sc.add("act", lambda e, o=fs(2), i=y1: e.activation(o, i, AF.Square),
               reads=[bF32[1]], writes=[bF32[2]])
        Sc.add("dve", lambda e, o=st1(1), i=fs(2): e.reduce_sum(o, i, axis=AX.X),
               reads=[bF32[2]], writes=[bST])

        Sc.add("dve", lambda e: e.tensor_scalar_mul(st1(0), st1(0), 1.0 / H),
               reads=[bST], writes=[bST])
        Sc.add("dve", lambda e: e.tensor_scalar_mul(st1(1), st1(1), 1.0 / H),
               reads=[bST], writes=[bST])
        Sc.add("dve", lambda e: e.tensor_mul(st1(2), st1(0), st1(0)),
               reads=[bST], writes=[bST])
        Sc.add("dve", lambda e: e.tensor_sub(st1(1), st1(1), st1(2)),
               reads=[bST], writes=[bST])
        Sc.add("act", lambda e: e.activation(st1(2), st1(1), AF.Sqrt, bias=eps_col),
               reads=[bST], writes=[bST])
        Sc.add("dve", lambda e: e.reciprocal(st1(3), st1(2)), reads=[bST],
               writes=[bST])
        Sc.add("dve", lambda e, o=y1: e.tensor_scalar(o, o, st1(0), st1(3),
                                                      ALU.subtract, ALU.mult),
               reads=[bF32[1], bST], writes=[bF32[1]])
        dma("act", A["y1sp"][c_], y1, reads=[bF32[1]], writes=[bDR["y1sp"]])
        y1b = cs(2)
        Sc.add("act", lambda e, o=y1b, i=y1: e.copy(o, i),
               reads=[bF32[1]], writes=[bCOL[2]])

        def mk_y1tp(y1b=y1b, regPb=regPb):
            def f(e):
                last = None
                for ht in range(16):
                    last = nc.tensor.matmul(
                        regPb[:, ht * 128:(ht + 1) * 128],
                        y1b[:, ht * 128:(ht + 1) * 128], idn[:, :],
                        start=(ht % 8 == 0), stop=(ht % 8 == 7),
                        is_transpose=True, skip_group_check=True)
                return last
            return f
        Sc.add("te", mk_y1tp(), reads=[bCOL[2], bIDN], writes=rbk[0:2])
        dstk = kt.rearrange("p (ht t) -> p ht t", ht=16)[:, :, c_ * C:(c_ + 1) * C]
        srck = regPb[:, 0:2048].rearrange("p (ht a) -> p ht a", ht=16)
        Sc.add("act", lambda e, o=dstk, i=srck: e.copy(o, i),
               reads=rbk[0:2],
               writes=[bKTT[c_]] + (list(bKT) if c_ == 0 else []))

    if stop_after == "p3":
        dbg_dram("d_y1sp", "y1sp", [NCH, C, H], F32)
        dbg_sbuf("d_y1t", kt[:, :], list(bKT))
        return finish()

    dma("sp", tabf[:, :], A["sdown"][:, :], reads=[bTAB], writes=[bTAB])
    dma("sp", sgc[:, 0:INTER // C], A["sgate"][:, :], writes=[bSGC])
    dma("sp", sgc[:, INTER // C:], A["sup"][:, :], writes=[bSGC])

    xtf = xt.bitcast(F32)
    gnbbf = gnbb.bitcast(F32)
    snbf = snb.bitcast(F32)

    def acc_half(tg, hh):
        if tg < 3:
            return fs(tg)[:, hh * 1024:(hh + 1) * 1024]
        return gnbbf if hh == 0 else snbf

    def acc_bufs(tg):
        return [bF32[tg]] if tg < 3 else [bGNB, bSNB]

    stv = lambda tg, i: st[:, tg * 8 + i:tg * 8 + i + 1]

    out_dmas = []
    for s_ in range(NSTRIP):
        for it in range(64):
            tb, j = it // 2, it % 2
            par = it % 2
            wgs, wus = cs(4 + 2 * par), cs(5 + 2 * par)
            bwg, bwu = bCOL[4 + 2 * par], bCOL[5 + 2 * par]
            dma("pool", wgs.rearrange("p (h i) -> p h i", h=16),
                A["wgt"][tb, j], reads=[bwg], writes=[bwg], chain=False)
            dma("pool", wus.rearrange("p (h i) -> p h i", h=16),
                A["wut"][tb, j], reads=[bwu], writes=[bwu], chain=False)
            bank_g = bBank[par * 2]
            bank_u = bBank[par * 2 + 1]
            pg = pa[:, par * 1024:par * 1024 + 512]
            pu = pa[:, par * 1024 + 512:par * 1024 + 1024]

            def mk_gu(pg=pg, pu=pu, wgs=wgs, wus=wus, s_=s_):
                def f(e):
                    last = None
                    for ht in range(16):
                        rhs = kt[:, ht * T + s_ * STRIP:ht * T + (s_ + 1) * STRIP]
                        nc.tensor.matmul(
                            pg, wgs[:, ht * 128:(ht + 1) * 128], rhs,
                            start=(ht == 0), stop=(ht == 15),
                            skip_group_check=True)
                        last = nc.tensor.matmul(
                            pu, wus[:, ht * 128:(ht + 1) * 128], rhs,
                            start=(ht == 0), stop=(ht == 15),
                            skip_group_check=True)
                    return last
                return f
            Sc.add("te", mk_gu(), reads=[bwg, bwu] + [bKTT[4 * s_ + k]
                                                      for k in range(4)],
                   writes=[bank_g, bank_u])
            gb = gub[:, par * STRIP:(par + 1) * STRIP]
            Sc.add("act", lambda e, o=gb, i=pg, it=it:
                   e.activation(o, i, AF.Silu, scale=sgc[:, it:it + 1]),
                   reads=[bank_g, bSGC], writes=[bGUB[par]])
            fu = xt[:, it * STRIP:(it + 1) * STRIP]
            Sc.add("dve", lambda e, o=fu, i0=pu, it=it, gb=gb:
                   e.scalar_tensor_tensor(o, i0, sgc[:, 64 + it:64 + it + 1],
                                          gb, ALU.mult, ALU.mult),
                   reads=[bank_u, bGUB[par], bSGC], writes=[bXT[it // 4]])
        if DBG4 and s_ == 0:
            dbg_sbuf("d_fuse", xt[:, :], list(bXT), dtype=BF16, width=16 * T)
            dbg_sbuf("d_y1fm", kt[:, :], list(bKTT), dtype=BF16, width=16 * T)
            dbg_dram("d_ztsp", "ztsp", [NH, C, T], BF16)
        for hh in range(2):
            for grp in range(16):
                par = grp % 2
                wds = cols[:, (par * 2) * T:(par * 2 + 2) * T]
                bwd = [bCOL[par * 2], bCOL[par * 2 + 1]]
                dma("pool", wds.rearrange("p (g h) -> p g h", g=4),
                    A["wdt"][hh, grp], reads=bwd, writes=bwd,
                    dsem=f"dwn{par}", chain=False)

                def mk_down(grp=grp, wds=wds):
                    def f(e):
                        last = None
                        for g4 in range(4):
                            it = grp * 4 + g4
                            for tg in range(4):
                                psum = (pa, pb)[tg // 2]
                                po = (tg % 2) * 1024
                                for jn in range(2):
                                    last = nc.tensor.matmul(
                                        psum[:, po + jn * 512:po + (jn + 1) * 512],
                                        xt[:, it * STRIP + tg * 128:
                                           it * STRIP + tg * 128 + 128],
                                        wds[:, g4 * 1024 + jn * 512:
                                            g4 * 1024 + (jn + 1) * 512],
                                        start=(it == 0), stop=(it == 63),
                                        skip_group_check=True)
                        return last
                    return f
                Sc.add("te", mk_down(), reads=bwd + list(bXT),
                       writes=banksA + banksB)
            for tg in range(4):
                psum = (pa, pb)[tg // 2]
                po = (tg % 2) * 1024
                pbanks = [bBank[tg * 2], bBank[tg * 2 + 1]]
                dst = acc_half(tg, hh)
                Sc.add("dve", lambda e, o=dst, i0=psum[:, po:po + 1024],
                       i1=tabf[:, hh * 1024:(hh + 1) * 1024]:
                       e.tensor_tensor(o, i0, i1, ALU.mult),
                       reads=pbanks + [bTAB] + acc_bufs(tg), writes=acc_bufs(tg))
        for tg in range(4):
            c_ = s_ * 4 + tg
            babf = acc_bufs(tg)
            y1h = [xtf[:, tg * 2048 + hh * 1024:tg * 2048 + (hh + 1) * 1024]
                   for hh in range(2)]
            by1 = [bXT[2 * tg], bXT[2 * tg + 1]]
            dma("act", xtf[:, tg * 2048:(tg + 1) * 2048], A["y1sp"][c_],
                reads=[bDR["y1sp"]] + by1, writes=by1, dsem="y1r", chain=False)
            for hh in range(2):
                Sc.add("dve", lambda e, o=acc_half(tg, hh), i1=y1h[hh]:
                       e.tensor_add(o, o, i1),
                       reads=babf + by1, writes=babf)
            Sc.add("dve", lambda e, o=stv(tg, 0), i=acc_half(tg, 0):
                   e.reduce_sum(o, i, axis=AX.X), reads=babf, writes=[bST])
            Sc.add("dve", lambda e, o=stv(tg, 4), i=acc_half(tg, 1):
                   e.reduce_sum(o, i, axis=AX.X), reads=babf, writes=[bST])
            for hh in range(2):
                Sc.add("act", lambda e, o=y1h[hh], i=acc_half(tg, hh):
                       e.activation(o, i, AF.Square),
                       reads=babf + by1, writes=by1)
            Sc.add("dve", lambda e, o=stv(tg, 1), i=y1h[0]:
                   e.reduce_sum(o, i, axis=AX.X), reads=by1, writes=[bST])
            Sc.add("dve", lambda e, o=stv(tg, 5), i=y1h[1]:
                   e.reduce_sum(o, i, axis=AX.X), reads=by1, writes=[bST])
            Sc.add("dve", lambda e, t=tg: e.tensor_add(stv(t, 0), stv(t, 0),
                                                       stv(t, 4)),
                   reads=[bST], writes=[bST])
            Sc.add("dve", lambda e, t=tg: e.tensor_add(stv(t, 1), stv(t, 1),
                                                       stv(t, 5)),
                   reads=[bST], writes=[bST])
            Sc.add("dve", lambda e, t=tg: e.tensor_scalar_mul(stv(t, 0),
                                                              stv(t, 0), 1.0 / H),
                   reads=[bST], writes=[bST])
            Sc.add("dve", lambda e, t=tg: e.tensor_scalar_mul(stv(t, 1),
                                                              stv(t, 1), 1.0 / H),
                   reads=[bST], writes=[bST])
            Sc.add("dve", lambda e, t=tg: e.tensor_mul(stv(t, 2), stv(t, 0),
                                                       stv(t, 0)),
                   reads=[bST], writes=[bST])
            Sc.add("dve", lambda e, t=tg: e.tensor_sub(stv(t, 1), stv(t, 1),
                                                       stv(t, 2)),
                   reads=[bST], writes=[bST])
            Sc.add("act", lambda e, t=tg: e.activation(stv(t, 2), stv(t, 1),
                                                       AF.Sqrt, bias=eps_col),
                   reads=[bST], writes=[bST])
            Sc.add("dve", lambda e, t=tg: e.reciprocal(stv(t, 3), stv(t, 2)),
                   reads=[bST], writes=[bST])
            for hh in range(2):
                Sc.add("dve", lambda e, o=acc_half(tg, hh), t=tg:
                       e.tensor_scalar(o, o, stv(t, 0), stv(t, 3),
                                       ALU.subtract, ALU.mult),
                       reads=babf + [bST], writes=babf)
                out_dmas.append(
                    dma("act", A["y"][c_ * C:(c_ + 1) * C,
                                      hh * 1024:(hh + 1) * 1024],
                        acc_half(tg, hh), reads=babf, writes=[], dsem="out"))

    Sc.add("sp", lambda e: e.nop(), deps=out_dmas)

    return finish()



def kernel(L0_attn_X, L0_attn_W_Q, L0_attn_W_K, L0_attn_W_V, L0_attn_W_G,
           L0_attn_W_O, L0_attn_gn_weight, L0_attn_gn_bias, L0_ffn_w_gate,
           L0_ffn_w_up, L0_ffn_w_down, L0_ffn_s_gate, L0_ffn_s_up, L0_ffn_s_down):
    X = np.asarray(L0_attn_X)
    cst = _build_host_constants()

    def cat(w):
        m = np.asarray(w).transpose(1, 0, 2).reshape(H, H).astype(np.float32)
        return np.ascontiguousarray(
            m.reshape(16, C, 16, D).transpose(2, 1, 0, 3))
    wq, wk, wv = cat(L0_attn_W_Q), cat(L0_attn_W_K), cat(L0_attn_W_V)
    wg = np.ascontiguousarray(np.asarray(L0_attn_W_G, np.float32)
                              .reshape(16, C, 16, D).transpose(2, 1, 0, 3))
    wo = np.ascontiguousarray(np.asarray(L0_attn_W_O, np.float32))
    gnw = np.ascontiguousarray(np.broadcast_to(
        np.asarray(L0_attn_gn_weight, np.float32), (C, H)))
    gnb = np.ascontiguousarray(np.broadcast_to(
        np.asarray(L0_attn_gn_bias, np.float32), (C, H)))
    wgt = np.ascontiguousarray(
        np.asarray(L0_ffn_w_gate).reshape(16, C, NTB, 2, 128)
        .transpose(2, 3, 1, 0, 4))
    wut = np.ascontiguousarray(
        np.asarray(L0_ffn_w_up).reshape(16, C, NTB, 2, 128)
        .transpose(2, 3, 1, 0, 4))
    wdt = np.ascontiguousarray(
        np.asarray(L0_ffn_w_down).reshape(16, 4, C, 2, 1024)
        .transpose(3, 0, 2, 1, 4))
    sgate = np.ascontiguousarray(
        np.asarray(L0_ffn_s_gate, np.float32).reshape(INTER // C, C).T)
    sup = np.ascontiguousarray(
        np.asarray(L0_ffn_s_up, np.float32).reshape(INTER // C, C).T)
    sdown = np.ascontiguousarray(np.broadcast_to(
        np.asarray(L0_ffn_s_down, np.float32), (C, H)))

    shared = dict(wq=wq, wk=wk, wv=wv, wg=wg, wo=wo, gnw=gnw, gnb=gnb,
                  cq=cst["cq"], sq=cst["sq"], ck=cst["ck"], sk=cst["sk"],
                  gq=cst["gq"], gk=cst["gk"], causal=cst["causal"],
                  ident=cst["ident"], wgt=wgt, wut=wut, wdt=wdt,
                  sgate=sgate, sup=sup, sdown=sdown)

    in_maps = []
    for core in range(N_CORES):
        bq, qq = core // NQ, core % NQ
        m = dict(shared)
        m["x"] = np.ascontiguousarray(X[bq, qq * T:(qq + 1) * T, :].astype(np.float32))
        m["that"] = cst["that_by_core"][core]
        in_maps.append(m)

    nc = build_program()
    res = run_bass_kernel_spmd(nc, in_maps, core_ids=list(range(N_CORES)))

    out = np.empty((B, S, H), np.float32)
    for core in range(N_CORES):
        bq, qq = core // NQ, core % NQ
        out[bq, qq * T:(qq + 1) * T, :] = res.results[core]["y"]
    return out


# version 13
# speedup vs baseline: 1.0278x; 1.0278x over previous
import numpy as np
import ml_dtypes
import concourse.bass as bass
import concourse.mybir as mybir
from concourse.bass_utils import run_bass_kernel_spmd

F32 = mybir.dt.float32
BF16 = mybir.dt.bfloat16
I8 = mybir.dt.int8
AF = mybir.ActivationFunctionType
ALU = mybir.AluOpType
AX = mybir.AxisListType
bfdt = ml_dtypes.bfloat16

B, S, H = 2, 8192, 2048
NH, D = 16, 128
T = 2048
NCH = 16
C = 128
INTER = 8192
SCALE_BASE = 512.0
GN_EPS = 1e-5
LN_EPS = 1e-5
N_CORES = 8
NQ = 4
NTB = 32
NIB = 256
NSTRIP = 4
STRIP = 512
DBGH = 8
DBG4 = False
SERIALIZE = False



def _xpos_tables_f64(d, positions):
    half = d // 2
    xscale = (np.arange(0, d, 2, dtype=np.float64) + 0.4 * d) / (1.4 * d)
    pos = positions.astype(np.float64)
    scale = xscale[None, :] ** (pos[:, None] / SCALE_BASE)
    inv_freq = 1.0 / 10000.0 ** (np.arange(half, dtype=np.float64) / half)
    sinu = pos[:, None] * inv_freq[None, :]
    sin, cos = np.sin(sinu), np.cos(sinu)
    dup = lambda t: np.repeat(t, 2, axis=-1)
    return dup(cos * scale), dup(sin * scale), dup(cos / scale), dup(sin / scale)


def _build_host_constants():
    gammas = 1.0 - np.exp(np.linspace(np.log(1 / 32), np.log(1 / 512), NH))
    p = np.arange(T)
    cq, sq, ck, sk = _xpos_tables_f64(D, p)
    sgn = np.ones(D)
    sgn[0::2] = -1.0
    sq = sq * sgn[None, :]
    sk = sk * sgn[None, :]

    def to_col(tab):
        return np.ascontiguousarray(
            tab.reshape(NCH, C, D).transpose(1, 0, 2).reshape(C, NCH * D))

    cq_t = to_col(cq).astype(bfdt)
    sq_t = to_col(sq).astype(bfdt)
    ck_t = to_col(ck).astype(bfdt)
    sk_t = to_col(sk).astype(bfdt)

    pc = (np.arange(NCH)[None, :] * C + np.arange(C)[:, None] + 1.0)
    gq = np.empty((C, NH * NCH), np.float64)
    gk = np.empty((C, NH * NCH), np.float64)
    for n in range(NH):
        gq[:, n * NCH:(n + 1) * NCH] = gammas[n] ** pc
        gk[:, n * NCH:(n + 1) * NCH] = gammas[n] ** (-pc)
    gq = gq.astype(np.float32)
    gk = gk.astype(np.float32)

    causal = (np.arange(C)[None, :] >= np.arange(C)[:, None]).astype(np.float64)
    causal_rep = np.ascontiguousarray(causal.astype(bfdt))
    ident = np.ascontiguousarray(np.eye(C, dtype=bfdt))

    half = D // 2
    xscale = (np.arange(0, D, 2, dtype=np.float64) + 0.4 * D) / (1.4 * D)
    inv_freq = 1.0 / 10000.0 ** (np.arange(half, dtype=np.float64) / half)

    def that_lhsT(D_off, n):
        sig = xscale ** (D_off / SCALE_BASE)
        th = D_off * inv_freq
        g = float(np.float32(gammas[n] ** D_off))
        M = np.zeros((D, D))
        for i in range(half):
            cth, sth = np.cos(th[i]), np.sin(th[i])
            M[2 * i, 2 * i] = cth * sig[i] * g
            M[2 * i, 2 * i + 1] = sth * sig[i] * g
            M[2 * i + 1, 2 * i] = -sth * sig[i] * g
            M[2 * i + 1, 2 * i + 1] = cth * sig[i] * g
        return M.T

    that_by_core = []
    for core in range(N_CORES):
        bq, qq = core // NQ, core % NQ
        that = np.zeros((N_CORES, NH, D, D), np.float64)
        for r in range(N_CORES):
            rb, rq = r // NQ, r % NQ
            if rb == bq and rq < qq:
                for n in range(NH):
                    that[r, n] = that_lhsT(T * (qq - rq), n)
        that_by_core.append(np.ascontiguousarray(that.astype(bfdt)))

    return dict(cq=cq_t, sq=sq_t, ck=ck_t, sk=sk_t, gq=gq, gk=gk,
                causal=causal_rep, ident=ident, that_by_core=that_by_core)



class Buf:
    __slots__ = ("writer", "readers", "sem", "group", "writers")

    def __init__(self, sem=None, group=False):
        self.writer = None
        self.writers = []
        self.readers = []
        self.sem = sem
        self.group = group


class Sched:
    ENGINES = ("te", "act", "dve", "pool", "sp")

    def __init__(self):
        self.ops = []
        self.last_dma = {}

    def add(self, eng, fn, reads=(), writes=(), deps=(), dma=False, dsem=None,
            chain=True):
        idx = len(self.ops)
        dep = set(deps)
        if SERIALIZE and idx > 0:
            dep.add(idx - 1)
        for b in reads:
            if b.group:
                dep.update(b.writers)
            elif b.writer is not None:
                dep.add(b.writer)
        for b in writes:
            if b.group:
                dep.update(b.readers)
                b.writers.append(idx)
            else:
                if b.writer is not None:
                    dep.add(b.writer)
                dep.update(b.readers)
                b.writer = idx
                b.readers = []
        dep.discard(idx)
        for d in dep:
            assert d < idx
        if dma:
            if dsem is None and writes:
                dsem = writes[0].sem
            assert dsem is not None, "dma op needs a dsem"
            if chain and dsem in self.last_dma:
                dep.add(self.last_dma[dsem])
            self.last_dma[dsem] = idx
        self.ops.append(dict(i=idx, eng=eng, fn=fn, deps=sorted(dep), dma=dma,
                             dsem=dsem))
        for b in reads:
            b.readers.append(idx)
        for b in writes:
            if not b.group:
                pass
        return idx

    def finalize(self):
        needed = set()
        for op in self.ops:
            needed.update(op["deps"])
        counters = {}
        pos = {}
        for op in self.ops:
            op["needed"] = op["i"] in needed or op["dma"]
            if op["needed"]:
                key = ("d", op["dsem"]) if op["dma"] else ("c", op["eng"])
                inc = 16 if op["dma"] else 1
                counters[key] = counters.get(key, 0) + inc
                pos[op["i"]] = (key, counters[key])
                op["inc"] = inc
        clocks = {e: {} for e in self.ENGINES}
        for op in self.ops:
            wmap = {}
            for d in op["deps"]:
                dop = self.ops[d]
                key, tgt = pos[d]
                if (not dop["dma"]) and dop["eng"] == op["eng"] == "te":
                    continue
                if wmap.get(key, 0) < tgt:
                    wmap[key] = tgt
            clk = clocks[op["eng"]]
            ww = []
            for key in sorted(wmap):
                tgt = wmap[key]
                if clk.get(key, 0) >= tgt:
                    continue
                clk[key] = tgt
                ww.append((key, tgt))
            op["waits"] = ww

    def sem_keys(self):
        ks = set()
        for op in self.ops:
            if op.get("needed"):
                ks.add(("d", op["dsem"]) if op["dma"] else ("c", op["eng"]))
        return sorted(ks)

    def emit_engine(self, eng_name, eng_obj, sems):
        for op in self.ops:
            if op["eng"] != eng_name:
                continue
            for key, tgt in op["waits"]:
                eng_obj.wait_ge(sems[key], tgt)
            inst = op["fn"](eng_obj)
            if op["needed"]:
                assert inst is not None, f"op {op['i']} on {eng_name} returned None"
                key = ("d", op["dsem"]) if op["dma"] else ("c", op["eng"])
                inst.then_inc(sems[key], op["inc"])



def build_program(stop_after=None):
    nc = bass.Bass("TRN2", target_bir_lowering=False)

    dr = {}
    ei = "ExternalInput"
    dr["x"] = nc.dram_tensor("x", [T, H], F32, kind=ei)
    for nm in ("wq", "wk", "wv", "wg"):
        dr[nm] = nc.dram_tensor(nm, [16, C, 16, D], F32, kind=ei)
    dr["wo"] = nc.dram_tensor("wo", [H, H], F32, kind=ei)
    dr["gnw"] = nc.dram_tensor("gnw", [C, H], F32, kind=ei)
    dr["gnb"] = nc.dram_tensor("gnb", [C, H], F32, kind=ei)
    for nm in ("cq", "sq", "ck", "sk"):
        dr[nm] = nc.dram_tensor(nm, [C, T], BF16, kind=ei)
    dr["causal"] = nc.dram_tensor("causal", [C, C], BF16, kind=ei)
    dr["gq"] = nc.dram_tensor("gq", [C, NH * NCH], F32, kind=ei)
    dr["gk"] = nc.dram_tensor("gk", [C, NH * NCH], F32, kind=ei)
    dr["ident"] = nc.dram_tensor("ident", [C, C], BF16, kind=ei)
    dr["that"] = nc.dram_tensor("that", [N_CORES, NH, D, D], BF16, kind=ei)
    dr["wgt"] = nc.dram_tensor("wgt", [NTB, 2, C, 16, 128], I8, kind=ei)
    dr["wut"] = nc.dram_tensor("wut", [NTB, 2, C, 16, 128], I8, kind=ei)
    dr["wdt"] = nc.dram_tensor("wdt", [2, 16, C, 4, 1024], I8, kind=ei)
    dr["sgate"] = nc.dram_tensor("sgate", [C, INTER // C], F32, kind=ei)
    dr["sup"] = nc.dram_tensor("sup", [C, INTER // C], F32, kind=ei)
    dr["sdown"] = nc.dram_tensor("sdown", [C, H], F32, kind=ei)
    dr["y"] = nc.dram_tensor("y", [T, H], F32, kind="ExternalOutput")
    dr["vqg"] = nc.dram_tensor("vqg", [NH, 3, C, T], BF16, kind="Internal")
    dr["ztsp"] = nc.dram_tensor("ztsp", [NH, C, T], BF16, kind="Internal")
    dr["y1sp"] = nc.dram_tensor("y1sp", [NCH, C, H], F32, kind="Internal")
    dr["ccin"] = nc.dram_tensor("ccin", [C, NH * D + 0], BF16, kind="Internal")
    dr["ccout"] = nc.dram_tensor("ccout", [N_CORES * C, NH * D], BF16, kind="Internal")

    A = {k: v.ap() for k, v in dr.items()}

    XT = nc.alloc_sbuf_tensor("XTa", [128, 16 * T], BF16)
    KT = nc.alloc_sbuf_tensor("KTa", [128, 16 * T], BF16)
    COLS = nc.alloc_sbuf_tensor("COLSa", [128, 8 * T], BF16)
    F32A = nc.alloc_sbuf_tensor("F32Aa", [128, 3 * H], F32)
    TAB = nc.alloc_sbuf_tensor("TABa", [128, 2 * T], BF16)
    CAU = nc.alloc_sbuf_tensor("CAUa", [128, C], BF16)
    GNBB = nc.alloc_sbuf_tensor("GNBBa", [128, H], BF16)
    GAM = nc.alloc_sbuf_tensor("GAMa", [128, 2 * NH * NCH], F32)
    IDN = nc.alloc_sbuf_tensor("IDNa", [128, C], BF16)
    GUB = nc.alloc_sbuf_tensor("GUBa", [128, 2 * STRIP], BF16)
    ST = nc.alloc_sbuf_tensor("STa", [128, 180], F32)
    SNB = nc.alloc_sbuf_tensor("SNBa", [128, T], BF16)
    SGC = nc.alloc_sbuf_tensor("SGCa", [128, 2 * (INTER // C)], F32)

    xt = XT.ap(); kt = KT.ap(); cols = COLS.ap(); f32a = F32A.ap()
    tab = TAB.ap(); cau = CAU.ap(); gnbb = GNBB.ap(); gam = GAM.ap()
    idn = IDN.ap(); gub = GUB.ap(); st = ST.ap(); snb = SNB.ap(); sgc = SGC.ap()
    tabf = tab.bitcast(F32)

    PA = nc.alloc_psum_tensor("PAa", [128, 2048], F32)
    PB = nc.alloc_psum_tensor("PBa", [128, 2048], F32)
    pa = PA.ap(); pb = PB.ap()
    pbb = pb.bitcast(BF16)

    Sc = Sched()
    bXT = [Buf(sem="xt") for _ in range(16)]
    bKT = [Buf() for _ in range(16)]
    bCOL = [Buf(sem=f"cols{i}") for i in range(8)]
    bF32 = [Buf(sem=f"f32{i}") for i in range(3)]
    bTAB = Buf(sem="tab"); bCAU = Buf(sem="cau"); bGNB = Buf(sem="gnb")
    bGAM = Buf(sem="gam"); bIDN = Buf(sem="idn")
    bGUB = [Buf(), Buf()]; bST = Buf(); bSNB = Buf(sem="snb"); bSGC = Buf(sem="sgc")
    bKTT = [Buf() for _ in range(16)]
    bBank = [Buf() for _ in range(8)]
    bDR = {k: Buf(sem=k, group=True)
           for k in ("vqg", "ztsp", "y1sp")}
    bDR["ccin"] = Buf(sem="ccin")
    bDR["ccout"] = Buf()

    banksA = [bBank[i] for i in range(4)]
    banksB = [bBank[i] for i in range(4, 8)]
    bTRhi = [bBank[6], bBank[7]]
    bTRlo = [bBank[4], bBank[5]]

    cs = lambda i: cols[:, i * T:(i + 1) * T]
    xts = lambda i: xt[:, i * T:(i + 1) * T]
    pab = pa.bitcast(BF16)
    fs = lambda i: f32a[:, i * H:(i + 1) * H]
    sti = lambda i: st[:, i * 20:i * 20 + 16]
    st1 = lambda i: st[:, i * 20:i * 20 + 1]

    eps_col = st[:, 176:177]
    Sc_eps_done = []

    def dma(engq, out, in_, reads=(), writes=(), deps=(), dsem=None, chain=True):
        return Sc.add(engq, lambda e, o=out, i=in_: e.dma_start(out=o, in_=i),
                      reads=reads, writes=writes, deps=deps, dma=True, dsem=dsem,
                      chain=chain)

    def finish():
        from contextlib import ExitStack
        Sc.finalize()
        with ExitStack() as stack:
            sems = {}
            for key in Sc.sem_keys():
                nm = ("cs_" + key[1]) if key[0] == "c" else ("ds_" + str(key[1]))
                sems[key] = stack.enter_context(nc.semaphore(nm))
            block = stack.enter_context(nc.Block())

            @block.tensor
            def _(eng):
                Sc.emit_engine("te", eng, sems)

            @block.scalar
            def _(eng):
                Sc.emit_engine("act", eng, sems)

            @block.vector
            def _(eng):
                Sc.emit_engine("dve", eng, sems)

            @block.gpsimd
            def _(eng):
                Sc.emit_engine("pool", eng, sems)

            @block.sync
            def _(eng):
                Sc.emit_engine("sp", eng, sems)
        return nc

    def dbg_sbuf(nm, ap, bufs, dtype=BF16, width=16 * T):
        dt_ = nc.dram_tensor(nm, [128, width], dtype, kind="ExternalOutput")
        dd = dma("sp", dt_.ap()[:, :], ap, reads=bufs, writes=[], dsem="out")
        Sc.add("sp", lambda e: e.nop(), deps=[dd])

    def dbg_dram(nm, key, shape, dtype):
        dt_ = nc.dram_tensor(nm, shape, dtype, kind="ExternalOutput")
        idx = tuple(slice(None) for _ in shape)
        dd = dma("sp", dt_.ap()[idx], A[key][idx], reads=[bDR[key]], writes=[], dsem="out")
        Sc.add("sp", lambda e: e.nop(), deps=[dd])

    Sc.add("dve", lambda e: e.memset(eps_col, 1e-5), writes=[bST])
    dma("sp", cau[:, :], A["causal"][:, :], writes=[bCAU])
    dma("sp", idn[:, :], A["ident"][:, :], writes=[bIDN])
    dma("sp", gam[:, 0:NH * NCH], A["gq"][:, :], writes=[bGAM])
    dma("sp", gam[:, NH * NCH:2 * NH * NCH], A["gk"][:, :], writes=[bGAM])
    dma("sp", tab[:, 0:T], A["ck"][:, :], writes=[bTAB])
    dma("sp", tab[:, T:2 * T], A["sk"][:, :], writes=[bTAB])

    for c_ in range(NCH):
        pp = c_ % 2
        xst = fs(pp)
        dma("sp", xst, A["x"][c_ * C:(c_ + 1) * C, :], reads=[bF32[pp]],
            writes=[bF32[pp]], chain=False)
        xb = cs(pp)
        Sc.add("act", lambda e, o=xb, i=xst: e.copy(o, i),
               reads=[bF32[pp]], writes=[bCOL[pp]])

        def mk_tp(src=xb):
            def f(e):
                last = None
                for ht in range(16):
                    last = nc.tensor.matmul(
                        pbb[:, 2048 + ht * 128:2048 + (ht + 1) * 128],
                        src[:, ht * 128:(ht + 1) * 128], idn[:, :],
                        start=(ht % 8 == 0), stop=(ht % 8 == 7),
                        is_transpose=True, skip_group_check=True)
                return last
            return f
        Sc.add("te", mk_tp(), reads=[bCOL[pp], bIDN], writes=bTRhi)
        dst = xt.rearrange("p (ht t) -> p ht t", ht=16)[:, :, c_ * C:(c_ + 1) * C]
        src = pbb[:, 2048:4096].rearrange("p (ht a) -> p ht a", ht=16)
        Sc.add("act", lambda e, o=dst, i=src: e.copy(o, i),
               reads=bTRhi, writes=list(bXT))

    if stop_after == "p0":
        dbg_sbuf("d_xt", xt[:, :], list(bXT))
        return finish()

    def proj_tile(n, wap, wbuf, wapb, out_ap, out_buf):
        dma("pool", wapb.rearrange("p (ht d) -> p ht d", ht=16),
            wap[n], reads=[wbuf], writes=[wbuf], chain=False)
        for half in range(2):
            def mk_mm(half=half, wapb=wapb):
                def f(e):
                    last = None
                    for ht in range(16):
                        for j in range(2):
                            off = ht * T + half * 1024 + j * 512
                            last = nc.tensor.matmul(
                                pb[:, j * 512:(j + 1) * 512],
                                wapb[:, ht * D:(ht + 1) * D],
                                xt[:, off:off + 512],
                                start=(ht == 0), stop=(ht == 15),
                                skip_group_check=True)
                    return last
                return f
            Sc.add("te", mk_mm(), reads=[wbuf] + list(bXT), writes=bTRlo)
            Sc.add("act",
                   lambda e, o=out_ap[:, half * 1024:(half + 1) * 1024],
                   i=pb[:, 0:1024]: e.copy(o, i),
                   reads=bTRlo, writes=[out_buf])

    def transpose16(src_ap, src_buf, dst_ap, dst_bufs, region="hi"):
        reg_off = 2048 if region == "hi" else 0
        reg_bufs = bTRhi if region == "hi" else bTRlo

        def f(e, src_ap=src_ap, reg_off=reg_off):
            last = None
            for c_ in range(16):
                last = nc.tensor.matmul(
                    pbb[:, reg_off + c_ * 128:reg_off + (c_ + 1) * 128],
                    src_ap[:, c_ * 128:(c_ + 1) * 128], idn[:, :],
                    start=(c_ % 8 == 0), stop=(c_ % 8 == 7),
                    is_transpose=True, skip_group_check=True)
            return last
        Sc.add("te", f, reads=[src_buf, bIDN], writes=reg_bufs)
        Sc.add("act",
               lambda e, o=dst_ap, i=pbb[:, reg_off:reg_off + 2048]: e.copy(o, i),
               reads=reg_bufs, writes=list(dst_bufs))

    def xpos_apply(col_ap, col_buf, gam_off, out_ap, out_bufs):
        tmp = fs(0)
        sw = col_ap.rearrange("p (c k two) -> p c k two", c=NCH, two=2)[:, :, :, ::-1]
        stv = tab[:, T:2 * T].rearrange("p (c k two) -> p c k two", c=NCH, two=2)
        Sc.add("dve", lambda e, o=tmp, i0=sw, i1=stv:
               e.tensor_tensor(o.rearrange("p (c k two) -> p c k two", c=NCH, two=2),
                               i0, i1, ALU.mult),
               reads=[col_buf, bTAB], writes=[bF32[0]])
        Sc.add("dve", lambda e, o=col_ap, i1=tab[:, 0:T]:
               e.tensor_tensor(o, o, i1, ALU.mult),
               reads=[col_buf, bTAB], writes=[col_buf])
        Sc.add("dve", lambda e, o=col_ap, i1=tmp:
               e.tensor_tensor(o, o, i1, ALU.add),
               reads=[col_buf, bF32[0]], writes=[col_buf])
        gview = gam[:, gam_off:gam_off + NCH].unsqueeze(2).broadcast_to([128, NCH, D])
        Sc.add("dve", lambda e, o=out_ap, i0=col_ap, i1=gview:
               e.tensor_tensor(o.rearrange("p (c d) -> p c d", c=NCH),
                               i0.rearrange("p (c d) -> p c d", c=NCH),
                               i1, ALU.mult),
               reads=[col_buf, bGAM], writes=list(out_bufs))

    for n in range(NH):
        wkb, wka = bCOL[3 + (n % 2)], cs(3 + (n % 2))
        wvb, wva = bCOL[5 + (n % 2)], cs(5 + (n % 2))
        proj_tile(n, A["wk"], wkb, wka, cs(0), bCOL[0])
        transpose16(cs(0), bCOL[0], kt[:, n * T:(n + 1) * T], [bKT[n]])
        xpos_apply(kt[:, n * T:(n + 1) * T], bKT[n], NH * NCH + n * NCH,
                   kt[:, n * T:(n + 1) * T], [bKT[n]])
        proj_tile(n, A["wv"], wvb, wva, cs(1), bCOL[1])
        transpose16(cs(1), bCOL[1], cs(2), [bCOL[2]])
        dma("sp", A["vqg"][n, 0], cs(2), reads=[bCOL[2]], writes=[bDR["vqg"]])

        def mk_state(n=n):
            def f(e):
                last = None
                for c_ in range(16):
                    last = nc.tensor.matmul(
                        pa[:, n * D:(n + 1) * D],
                        kt[:, n * T + c_ * 128:n * T + (c_ + 1) * 128],
                        cs(2)[:, c_ * 128:(c_ + 1) * 128],
                        start=(n % 4 == 0 and c_ == 0),
                        stop=(n % 4 == 3 and c_ == 15),
                        skip_group_check=True)
                return last
            return f
        Sc.add("te", mk_state(), reads=[bKT[n], bCOL[2]], writes=[bBank[n // 4]])

    Sc.add("act", lambda e, o=cs(7), i=pa[:, :]: e.copy(o, i),
           reads=banksA, writes=[bCOL[7]])
    dma("sp", A["ccin"][:, :], cs(7), reads=[bCOL[7]], writes=[bDR["ccin"]])
    Sc.add("pool", lambda e: e.collective_compute(
        "AllGather", ALU.bypass, [list(range(N_CORES))],
        [A["ccin"][:, :]], [A["ccout"][:, :]]),
        reads=[bDR["ccin"]], writes=[bDR["ccout"]])

    if stop_after == "p1":
        dbg_sbuf("d_kt", kt[:, :], list(bKT))
        dbg_dram("d_vqg", "vqg", [NH, 3, C, T], BF16)
        dbg_dram("d_ccin", "ccin", [C, NH * D + 0], BF16)
        return finish()

    dma("sp", tab[:, 0:T], A["cq"][:, :], reads=[bTAB], writes=[bTAB])
    dma("sp", tab[:, T:2 * T], A["sq"][:, :], reads=[bTAB], writes=[bTAB])
    for n in range(NH):
        wqb, wqa = bCOL[3 + (n % 2)], cs(3 + (n % 2))
        wgb, wga = bCOL[5 + (n % 2)], cs(5 + (n % 2))
        proj_tile(n, A["wq"], wqb, wqa, cs(0), bCOL[0])
        transpose16(cs(0), bCOL[0], cs(1), [bCOL[1]])
        xpos_apply(cs(1), bCOL[1], n * NCH, cs(1), [bCOL[1]])
        dma("sp", A["vqg"][n, 1], cs(1), reads=[bCOL[1]], writes=[bDR["vqg"]])
        proj_tile(n, A["wg"], wgb, wga, cs(0), bCOL[0])
        transpose16(cs(0), bCOL[0], cs(2), [bCOL[2]])
        Sc.add("act", lambda e, o=cs(2): e.activation(o, o, AF.Silu),
               reads=[bCOL[2]], writes=[bCOL[2]])
        dma("sp", A["vqg"][n, 2], cs(2), reads=[bCOL[2]], writes=[bDR["vqg"]])

    for r in range(N_CORES):
        dma("sp", cs(r), A["ccout"][r * C:(r + 1) * C, :],
            reads=[bDR["ccout"], bCOL[r]], writes=[bCOL[r]], dsem="cco")
    for r in range(N_CORES):
        dma("sp", snb[:, :].rearrange("p (n d) -> p n d", n=NH),
            A["that"][r].rearrange("n p d -> p n d"),
            reads=[bSNB], writes=[bSNB], chain=False)

        def mk_that(r=r):
            def f(e):
                last = None
                for n in range(NH):
                    last = nc.tensor.matmul(
                        pa[:, n * D:(n + 1) * D],
                        snb[:, n * D:(n + 1) * D],
                        cs(r)[:, n * D:(n + 1) * D],
                        start=(r == 0 and n % 4 == 0),
                        stop=(r == N_CORES - 1 and n % 4 == 3),
                        skip_group_check=True)
                return last
            return f
        Sc.add("te", mk_that(), reads=[bSNB, bCOL[r]], writes=banksA)
    Sc.add("act", lambda e, o=xts(12), i=pa[:, :]: e.copy(o, i),
           reads=banksA + [bXT[12]], writes=[bXT[12]])

    if stop_after == "exch":
        dbg_sbuf("d_sin", xts(12), [bXT[12]], dtype=BF16, width=T)
        dbg_dram("d_vqg", "vqg", [NH, 3, C, T], BF16)
        return finish()

    dma("sp", tabf[:, :], A["gnw"][:, :], reads=[bTAB], writes=[bTAB])
    dma("pool", gnbb[:, :], A["gnb"][:, :], writes=[bGNB])
    bSTp = [Buf(), Buf()]

    for n in range(NH):
        P_ = n % 2
        reg = (pa, pb)[P_]
        regb = (pab, pbb)[P_]
        reg3 = reg[:, :].rearrange("p (c d) -> p c d", c=NCH)
        rbanks = [bBank[P_ * 4 + i] for i in range(4)]
        rb01 = rbanks[0:2]
        sb0 = P_ * 6
        ktT, qtT, atm = xts(sb0), xts(sb0 + 1), xts(sb0 + 2)
        zc, ztc, dsB = xts(sb0 + 3), xts(sb0 + 4), xts(sb0 + 5)
        bktT, bqtT, batm = bXT[sb0], bXT[sb0 + 1], bXT[sb0 + 2]
        bzc, bztc, bdsB = bXT[sb0 + 3], bXT[sb0 + 4], bXT[sb0 + 5]
        snbP, bsnbP = xts(13 + P_), bXT[13 + P_]
        vc, qc, gc = cs(3 * P_), cs(3 * P_ + 1), cs(3 * P_ + 2)
        bvqg = [bCOL[3 * P_], bCOL[3 * P_ + 1], bCOL[3 * P_ + 2]]
        syF, bsyF = fs(1 + P_), bF32[1 + P_]
        so = P_ * 4
        stS = lambda i: st[:, (so + i) * 20:(so + i) * 20 + 16]
        bst = bSTp[P_]

        for k, q_ in ((0, "sp"), (1, "act"), (2, "pool")):
            dma(q_, cs(3 * P_ + k), A["vqg"][n, k],
                reads=[bDR["vqg"], bvqg[k]], writes=[bvqg[k]],
                dsem=f"vqg{P_}{k}", chain=False)

        def tp16(src_ap, src_buf, dst_ap, dst_buf, regb=regb, rb01=rb01):
            def f(e, src_ap=src_ap, regb=regb):
                last = None
                for c_ in range(16):
                    last = nc.tensor.matmul(
                        regb[:, c_ * 128:(c_ + 1) * 128],
                        src_ap[:, c_ * 128:(c_ + 1) * 128], idn[:, :],
                        start=(c_ % 8 == 0), stop=(c_ % 8 == 7),
                        is_transpose=True, skip_group_check=True)
                return last
            Sc.add("te", f, reads=[src_buf, bIDN], writes=rb01)
            Sc.add("act", lambda e, o=dst_ap, i=regb[:, 0:2048]: e.copy(o, i),
                   reads=rb01, writes=[dst_buf])

        tp16(kt[:, n * T:(n + 1) * T], bKT[n], ktT, bktT)
        tp16(qc, bvqg[1], qtT, bqtT)

        def mk_amm(ktT=ktT, qtT=qtT, reg=reg):
            def f(e):
                last = None
                for c_ in range(16):
                    last = nc.tensor.matmul(
                        reg[:, c_ * 128:(c_ + 1) * 128],
                        ktT[:, c_ * 128:(c_ + 1) * 128],
                        qtT[:, c_ * 128:(c_ + 1) * 128],
                        start=(c_ % 4 == 0), stop=(c_ % 4 == 3),
                        skip_group_check=True)
                return last
            return f
        Sc.add("te", mk_amm(), reads=[bktT, bqtT], writes=rbanks)
        cau_b = cau[:, :].unsqueeze(1).broadcast_to([128, NCH, C])
        Sc.add("dve", lambda e, o=atm, i0=reg3, i1=cau_b: e.tensor_tensor(
            o.rearrange("p (c a) -> p c a", c=NCH), i0, i1, ALU.mult),
               reads=rbanks + [bCAU], writes=[batm])

        def mk_ds(n=n, vc=vc, reg=reg):
            def f(e):
                last = None
                for c_ in range(16):
                    last = nc.tensor.matmul(
                        reg[:, c_ * 128:(c_ + 1) * 128],
                        kt[:, n * T + c_ * 128:n * T + (c_ + 1) * 128],
                        vc[:, c_ * 128:(c_ + 1) * 128],
                        start=(c_ % 4 == 0), stop=(c_ % 4 == 3),
                        skip_group_check=True)
                return last
            return f
        Sc.add("te", mk_ds(), reads=[bKT[n], bvqg[0]], writes=rbanks)
        Sc.add("act", lambda e, o=dsB, i=reg[:, :]: e.copy(o, i),
               reads=rbanks, writes=[bdsB])
        Sc.add("dve", lambda e, o=reg[:, 0:128], i=xts(12)[:, n * D:(n + 1) * D]:
               e.tensor_copy(o, i),
               reads=[bXT[12], bdsB] + rbanks, writes=[rbanks[0]])
        for c_ in range(15):
            Sc.add("dve", lambda e, c_=c_, reg=reg, dsB=dsB: e.tensor_add(
                reg[:, (c_ + 1) * 128:(c_ + 2) * 128],
                reg[:, c_ * 128:(c_ + 1) * 128],
                dsB[:, c_ * 128:(c_ + 1) * 128]),
                reads=[bdsB, rbanks[c_ // 4], rbanks[(c_ + 1) // 4]],
                writes=[rbanks[(c_ + 1) // 4]])
        Sc.add("act", lambda e, o=snbP, i=reg[:, :]: e.copy(o, i),
               reads=rbanks, writes=[bsnbP])

        def mk_y(atm=atm, vc=vc, qtT=qtT, snbP=snbP, reg=reg):
            def f(e):
                last = None
                for c_ in range(16):
                    nc.tensor.matmul(
                        reg[:, c_ * 128:(c_ + 1) * 128],
                        atm[:, c_ * 128:(c_ + 1) * 128],
                        vc[:, c_ * 128:(c_ + 1) * 128],
                        start=(c_ % 4 == 0), stop=False, skip_group_check=True)
                    last = nc.tensor.matmul(
                        reg[:, c_ * 128:(c_ + 1) * 128],
                        qtT[:, c_ * 128:(c_ + 1) * 128],
                        snbP[:, c_ * 128:(c_ + 1) * 128],
                        start=False, stop=(c_ % 4 == 3), skip_group_check=True)
                return last
            return f
        Sc.add("te", mk_y(), reads=[batm, bvqg[0], bqtT, bsnbP], writes=rbanks)
        Sc.add("act", lambda e, o=syF, i=reg[:, :]: e.copy(o, i),
               reads=rbanks, writes=[bsyF])

        sy3 = syF.rearrange("p (c d) -> p c d", c=NCH)
        Sc.add("dve", lambda e, o=stS(0), i=sy3: e.reduce_sum(o, i, axis=AX.X),
               reads=[bsyF], writes=[bst])
        Sc.add("act", lambda e, o=reg[:, :], i=syF: e.activation(o, i, AF.Square),
               reads=[bsyF] + rbanks, writes=rbanks)
        Sc.add("dve", lambda e, o=stS(1), i=reg3: e.reduce_sum(o, i, axis=AX.X),
               reads=rbanks, writes=[bst])
        Sc.add("dve", lambda e, a=stS(0): e.tensor_scalar_mul(a, a, 1.0 / D),
               reads=[bst], writes=[bst])
        Sc.add("dve", lambda e, a=stS(1): e.tensor_scalar_mul(a, a, 1.0 / D),
               reads=[bst], writes=[bst])
        Sc.add("dve", lambda e, a=stS(2), b=stS(0): e.tensor_mul(a, b, b),
               reads=[bst], writes=[bst])
        Sc.add("dve", lambda e, a=stS(1), b=stS(2): e.tensor_sub(a, a, b),
               reads=[bst], writes=[bst])
        Sc.add("act", lambda e, a=stS(2), b=stS(1): e.activation(a, b, AF.Sqrt,
                                                                 bias=eps_col),
               reads=[bst], writes=[bst])
        Sc.add("dve", lambda e, a=stS(3), b=stS(2): e.reciprocal(a, b),
               reads=[bst], writes=[bst])
        mu_b = stS(0).unsqueeze(2).broadcast_to([128, NCH, D])
        rs_b = stS(3).unsqueeze(2).broadcast_to([128, NCH, D])
        gnw_b = tabf[:, n * D:(n + 1) * D].unsqueeze(1).broadcast_to([128, NCH, D])
        gnb_b = gnbb[:, n * D:(n + 1) * D].unsqueeze(1).broadcast_to([128, NCH, D])
        Sc.add("pool", lambda e, o=sy3, i1=mu_b:
               e.tensor_tensor(o, o, i1, ALU.subtract),
               reads=[bst, bsyF], writes=[bsyF])
        Sc.add("pool", lambda e, o=sy3, i1=rs_b: e.tensor_tensor(o, o, i1,
                                                                 ALU.mult),
               reads=[bst, bsyF], writes=[bsyF])
        Sc.add("pool", lambda e, o=sy3, i1=gnw_b: e.tensor_tensor(o, o, i1,
                                                                  ALU.mult),
               reads=[bTAB, bsyF], writes=[bsyF])
        Sc.add("pool", lambda e, o=sy3, i1=gnb_b: e.tensor_tensor(o, o, i1,
                                                                  ALU.add),
               reads=[bGNB, bsyF], writes=[bsyF])
        gc3 = gc.rearrange("p (c d) -> p c d", c=NCH)
        Sc.add("pool", lambda e, o=zc, i0=sy3, i1=gc3: e.tensor_tensor(
            o.rearrange("p (c d) -> p c d", c=NCH), i0, i1, ALU.mult),
               reads=[bvqg[2], bsyF], writes=[bzc])
        tp16(zc, bzc, ztc, bztc)
        dma("sp", A["ztsp"][n], ztc, reads=[bztc], writes=[bDR["ztsp"]])

    if stop_after == "p2b":
        dbg_dram("d_ztsp", "ztsp", [NH, C, T], BF16)
        return finish()

    dma("pool", xt.rearrange("p (ht t) -> p ht t", ht=16),
        A["wo"].rearrange("(ht p) o -> p ht o", p=128),
        reads=list(bXT), writes=list(bXT))
    for c_ in range(NCH):
        ztc = cs(c_ % 2)
        dma("sp", ztc.rearrange("p (n a) -> p n a", n=NH),
            A["ztsp"][:, :, c_ * C:(c_ + 1) * C].rearrange("n p a -> p n a"),
            reads=[bDR["ztsp"], bCOL[c_ % 2]], writes=[bCOL[c_ % 2]], chain=False)

        regP = (pa, pb)[c_ % 2]
        regPb = (pab, pbb)[c_ % 2]
        rbk = [bBank[(c_ % 2) * 4 + i] for i in range(4)]

        def mk_wo(ztc=ztc, regP=regP):
            def f(e):
                last = None
                for ht in range(16):
                    for j in range(4):
                        last = nc.tensor.matmul(
                            regP[:, j * 512:(j + 1) * 512],
                            ztc[:, ht * 128:(ht + 1) * 128],
                            xt[:, ht * T + j * 512:ht * T + (j + 1) * 512],
                            start=(ht == 0), stop=(ht == 15),
                            skip_group_check=True)
                return last
            return f
        Sc.add("te", mk_wo(), reads=[bCOL[c_ % 2]] + list(bXT), writes=rbk)
        xres = fs(0)
        dma("act", xres, A["x"][c_ * C:(c_ + 1) * C, :],
            reads=[bF32[0]], writes=[bF32[0]], chain=False)
        y1 = fs(1)
        Sc.add("dve", lambda e, o=y1, i1=xres, i0=regP: e.tensor_add(o, i0[:, :], i1),
               reads=rbk + [bF32[0]], writes=[bF32[1]])
        Sc.add("dve", lambda e, o=st1(0), i=y1: e.reduce_sum(o, i, axis=AX.X),
               reads=[bF32[1]], writes=[bST])
        Sc.add("act", lambda e, o=fs(2), i=y1: e.activation(o, i, AF.Square),
               reads=[bF32[1]], writes=[bF32[2]])
        Sc.add("dve", lambda e, o=st1(1), i=fs(2): e.reduce_sum(o, i, axis=AX.X),
               reads=[bF32[2]], writes=[bST])

        Sc.add("dve", lambda e: e.tensor_scalar_mul(st1(0), st1(0), 1.0 / H),
               reads=[bST], writes=[bST])
        Sc.add("dve", lambda e: e.tensor_scalar_mul(st1(1), st1(1), 1.0 / H),
               reads=[bST], writes=[bST])
        Sc.add("dve", lambda e: e.tensor_mul(st1(2), st1(0), st1(0)),
               reads=[bST], writes=[bST])
        Sc.add("dve", lambda e: e.tensor_sub(st1(1), st1(1), st1(2)),
               reads=[bST], writes=[bST])
        Sc.add("act", lambda e: e.activation(st1(2), st1(1), AF.Sqrt, bias=eps_col),
               reads=[bST], writes=[bST])
        Sc.add("dve", lambda e: e.reciprocal(st1(3), st1(2)), reads=[bST],
               writes=[bST])
        Sc.add("dve", lambda e, o=y1: e.tensor_scalar(o, o, st1(0), st1(3),
                                                      ALU.subtract, ALU.mult),
               reads=[bF32[1], bST], writes=[bF32[1]])
        dma("act", A["y1sp"][c_], y1, reads=[bF32[1]], writes=[bDR["y1sp"]])
        y1b = cs(2)
        Sc.add("act", lambda e, o=y1b, i=y1: e.copy(o, i),
               reads=[bF32[1]], writes=[bCOL[2]])

        def mk_y1tp(y1b=y1b, regPb=regPb):
            def f(e):
                last = None
                for ht in range(16):
                    last = nc.tensor.matmul(
                        regPb[:, ht * 128:(ht + 1) * 128],
                        y1b[:, ht * 128:(ht + 1) * 128], idn[:, :],
                        start=(ht % 8 == 0), stop=(ht % 8 == 7),
                        is_transpose=True, skip_group_check=True)
                return last
            return f
        Sc.add("te", mk_y1tp(), reads=[bCOL[2], bIDN], writes=rbk[0:2])
        dstk = kt.rearrange("p (ht t) -> p ht t", ht=16)[:, :, c_ * C:(c_ + 1) * C]
        srck = regPb[:, 0:2048].rearrange("p (ht a) -> p ht a", ht=16)
        Sc.add("act", lambda e, o=dstk, i=srck: e.copy(o, i),
               reads=rbk[0:2],
               writes=[bKTT[c_]] + (list(bKT) if c_ == 0 else []))

    if stop_after == "p3":
        dbg_dram("d_y1sp", "y1sp", [NCH, C, H], F32)
        dbg_sbuf("d_y1t", kt[:, :], list(bKT))
        return finish()

    dma("sp", tabf[:, :], A["sdown"][:, :], reads=[bTAB], writes=[bTAB])
    dma("sp", sgc[:, 0:INTER // C], A["sgate"][:, :], writes=[bSGC])
    dma("sp", sgc[:, INTER // C:], A["sup"][:, :], writes=[bSGC])

    xtf = xt.bitcast(F32)
    gnbbf = gnbb.bitcast(F32)
    snbf = snb.bitcast(F32)

    def acc_half(tg, hh):
        if tg < 3:
            return fs(tg)[:, hh * 1024:(hh + 1) * 1024]
        return gnbbf if hh == 0 else snbf

    def acc_bufs(tg):
        return [bF32[tg]] if tg < 3 else [bGNB, bSNB]

    stv = lambda tg, i: st[:, tg * 8 + i:tg * 8 + i + 1]

    out_dmas = []
    for s_ in range(NSTRIP):
        for it in range(64):
            tb, j = it // 2, it % 2
            par = it % 2
            wgs, wus = cs(4 + 2 * par), cs(5 + 2 * par)
            bwg, bwu = bCOL[4 + 2 * par], bCOL[5 + 2 * par]
            dma("pool", wgs.rearrange("p (h i) -> p h i", h=16),
                A["wgt"][tb, j], reads=[bwg], writes=[bwg], chain=False)
            dma("pool", wus.rearrange("p (h i) -> p h i", h=16),
                A["wut"][tb, j], reads=[bwu], writes=[bwu], chain=False)
            bank_g = bBank[par * 2]
            bank_u = bBank[par * 2 + 1]
            pg = pa[:, par * 1024:par * 1024 + 512]
            pu = pa[:, par * 1024 + 512:par * 1024 + 1024]

            def mk_gu(pg=pg, pu=pu, wgs=wgs, wus=wus, s_=s_):
                def f(e):
                    last = None
                    for ht in range(16):
                        rhs = kt[:, ht * T + s_ * STRIP:ht * T + (s_ + 1) * STRIP]
                        nc.tensor.matmul(
                            pg, wgs[:, ht * 128:(ht + 1) * 128], rhs,
                            start=(ht == 0), stop=(ht == 15),
                            skip_group_check=True)
                        last = nc.tensor.matmul(
                            pu, wus[:, ht * 128:(ht + 1) * 128], rhs,
                            start=(ht == 0), stop=(ht == 15),
                            skip_group_check=True)
                    return last
                return f
            Sc.add("te", mk_gu(), reads=[bwg, bwu] + [bKTT[4 * s_ + k]
                                                      for k in range(4)],
                   writes=[bank_g, bank_u])
            gb = gub[:, par * STRIP:(par + 1) * STRIP]
            Sc.add("act", lambda e, o=gb, i=pg, it=it:
                   e.activation(o, i, AF.Silu, scale=sgc[:, it:it + 1]),
                   reads=[bank_g, bSGC], writes=[bGUB[par]])
            fu = xt[:, it * STRIP:(it + 1) * STRIP]
            Sc.add("dve", lambda e, o=fu, i0=pu, it=it, gb=gb:
                   e.scalar_tensor_tensor(o, i0, sgc[:, 64 + it:64 + it + 1],
                                          gb, ALU.mult, ALU.mult),
                   reads=[bank_u, bGUB[par], bSGC], writes=[bXT[it // 4]])
        if DBG4 and s_ == 0:
            dbg_sbuf("d_fuse", xt[:, :], list(bXT), dtype=BF16, width=16 * T)
            dbg_sbuf("d_y1fm", kt[:, :], list(bKTT), dtype=BF16, width=16 * T)
            dbg_dram("d_ztsp", "ztsp", [NH, C, T], BF16)
        for hh in range(2):
            for grp in range(16):
                par = grp % 2
                wds = cols[:, (par * 2) * T:(par * 2 + 2) * T]
                bwd = [bCOL[par * 2], bCOL[par * 2 + 1]]
                dma("pool", wds.rearrange("p (g h) -> p g h", g=4),
                    A["wdt"][hh, grp], reads=bwd, writes=bwd,
                    dsem=f"dwn{par}", chain=False)

                def mk_down(grp=grp, wds=wds):
                    def f(e):
                        last = None
                        for g4 in range(4):
                            it = grp * 4 + g4
                            for tg in range(4):
                                psum = (pa, pb)[tg // 2]
                                po = (tg % 2) * 1024
                                for jn in range(2):
                                    last = nc.tensor.matmul(
                                        psum[:, po + jn * 512:po + (jn + 1) * 512],
                                        xt[:, it * STRIP + tg * 128:
                                           it * STRIP + tg * 128 + 128],
                                        wds[:, g4 * 1024 + jn * 512:
                                            g4 * 1024 + (jn + 1) * 512],
                                        start=(it == 0), stop=(it == 63),
                                        skip_group_check=True)
                        return last
                    return f
                Sc.add("te", mk_down(), reads=bwd + list(bXT),
                       writes=banksA + banksB)
            for tg in range(4):
                psum = (pa, pb)[tg // 2]
                po = (tg % 2) * 1024
                pbanks = [bBank[tg * 2], bBank[tg * 2 + 1]]
                dst = acc_half(tg, hh)
                Sc.add("dve", lambda e, o=dst, i0=psum[:, po:po + 1024],
                       i1=tabf[:, hh * 1024:(hh + 1) * 1024]:
                       e.tensor_tensor(o, i0, i1, ALU.mult),
                       reads=pbanks + [bTAB] + acc_bufs(tg), writes=acc_bufs(tg))
        for tg in range(4):
            c_ = s_ * 4 + tg
            babf = acc_bufs(tg)
            y1h = [xtf[:, tg * 2048 + hh * 1024:tg * 2048 + (hh + 1) * 1024]
                   for hh in range(2)]
            by1 = [bXT[2 * tg], bXT[2 * tg + 1]]
            dma("act", xtf[:, tg * 2048:(tg + 1) * 2048], A["y1sp"][c_],
                reads=[bDR["y1sp"]] + by1, writes=by1, dsem="y1r", chain=False)
            for hh in range(2):
                Sc.add("dve", lambda e, o=acc_half(tg, hh), i1=y1h[hh]:
                       e.tensor_add(o, o, i1),
                       reads=babf + by1, writes=babf)
            Sc.add("dve", lambda e, o=stv(tg, 0), i=acc_half(tg, 0):
                   e.reduce_sum(o, i, axis=AX.X), reads=babf, writes=[bST])
            Sc.add("dve", lambda e, o=stv(tg, 4), i=acc_half(tg, 1):
                   e.reduce_sum(o, i, axis=AX.X), reads=babf, writes=[bST])
            for hh in range(2):
                Sc.add("act", lambda e, o=y1h[hh], i=acc_half(tg, hh):
                       e.activation(o, i, AF.Square),
                       reads=babf + by1, writes=by1)
            Sc.add("dve", lambda e, o=stv(tg, 1), i=y1h[0]:
                   e.reduce_sum(o, i, axis=AX.X), reads=by1, writes=[bST])
            Sc.add("dve", lambda e, o=stv(tg, 5), i=y1h[1]:
                   e.reduce_sum(o, i, axis=AX.X), reads=by1, writes=[bST])
            Sc.add("dve", lambda e, t=tg: e.tensor_add(stv(t, 0), stv(t, 0),
                                                       stv(t, 4)),
                   reads=[bST], writes=[bST])
            Sc.add("dve", lambda e, t=tg: e.tensor_add(stv(t, 1), stv(t, 1),
                                                       stv(t, 5)),
                   reads=[bST], writes=[bST])
            Sc.add("dve", lambda e, t=tg: e.tensor_scalar_mul(stv(t, 0),
                                                              stv(t, 0), 1.0 / H),
                   reads=[bST], writes=[bST])
            Sc.add("dve", lambda e, t=tg: e.tensor_scalar_mul(stv(t, 1),
                                                              stv(t, 1), 1.0 / H),
                   reads=[bST], writes=[bST])
            Sc.add("dve", lambda e, t=tg: e.tensor_mul(stv(t, 2), stv(t, 0),
                                                       stv(t, 0)),
                   reads=[bST], writes=[bST])
            Sc.add("dve", lambda e, t=tg: e.tensor_sub(stv(t, 1), stv(t, 1),
                                                       stv(t, 2)),
                   reads=[bST], writes=[bST])
            Sc.add("act", lambda e, t=tg: e.activation(stv(t, 2), stv(t, 1),
                                                       AF.Sqrt, bias=eps_col),
                   reads=[bST], writes=[bST])
            Sc.add("dve", lambda e, t=tg: e.reciprocal(stv(t, 3), stv(t, 2)),
                   reads=[bST], writes=[bST])
            for hh in range(2):
                Sc.add("dve", lambda e, o=acc_half(tg, hh), t=tg:
                       e.tensor_scalar(o, o, stv(t, 0), stv(t, 3),
                                       ALU.subtract, ALU.mult),
                       reads=babf + [bST], writes=babf)
                out_dmas.append(
                    dma("act", A["y"][c_ * C:(c_ + 1) * C,
                                      hh * 1024:(hh + 1) * 1024],
                        acc_half(tg, hh), reads=babf, writes=[], dsem="out"))

    Sc.add("sp", lambda e: e.nop(), deps=out_dmas)

    return finish()



def kernel(L0_attn_X, L0_attn_W_Q, L0_attn_W_K, L0_attn_W_V, L0_attn_W_G,
           L0_attn_W_O, L0_attn_gn_weight, L0_attn_gn_bias, L0_ffn_w_gate,
           L0_ffn_w_up, L0_ffn_w_down, L0_ffn_s_gate, L0_ffn_s_up, L0_ffn_s_down):
    X = np.asarray(L0_attn_X)
    cst = _build_host_constants()

    def cat(w):
        m = np.asarray(w).transpose(1, 0, 2).reshape(H, H).astype(np.float32)
        return np.ascontiguousarray(
            m.reshape(16, C, 16, D).transpose(2, 1, 0, 3))
    wq, wk, wv = cat(L0_attn_W_Q), cat(L0_attn_W_K), cat(L0_attn_W_V)
    wg = np.ascontiguousarray(np.asarray(L0_attn_W_G, np.float32)
                              .reshape(16, C, 16, D).transpose(2, 1, 0, 3))
    wo = np.ascontiguousarray(np.asarray(L0_attn_W_O, np.float32))
    gnw = np.ascontiguousarray(np.broadcast_to(
        np.asarray(L0_attn_gn_weight, np.float32), (C, H)))
    gnb = np.ascontiguousarray(np.broadcast_to(
        np.asarray(L0_attn_gn_bias, np.float32), (C, H)))
    wgt = np.ascontiguousarray(
        np.asarray(L0_ffn_w_gate).reshape(16, C, NTB, 2, 128)
        .transpose(2, 3, 1, 0, 4))
    wut = np.ascontiguousarray(
        np.asarray(L0_ffn_w_up).reshape(16, C, NTB, 2, 128)
        .transpose(2, 3, 1, 0, 4))
    wdt = np.ascontiguousarray(
        np.asarray(L0_ffn_w_down).reshape(16, 4, C, 2, 1024)
        .transpose(3, 0, 2, 1, 4))
    sgate = np.ascontiguousarray(
        np.asarray(L0_ffn_s_gate, np.float32).reshape(INTER // C, C).T)
    sup = np.ascontiguousarray(
        np.asarray(L0_ffn_s_up, np.float32).reshape(INTER // C, C).T)
    sdown = np.ascontiguousarray(np.broadcast_to(
        np.asarray(L0_ffn_s_down, np.float32), (C, H)))

    shared = dict(wq=wq, wk=wk, wv=wv, wg=wg, wo=wo, gnw=gnw, gnb=gnb,
                  cq=cst["cq"], sq=cst["sq"], ck=cst["ck"], sk=cst["sk"],
                  gq=cst["gq"], gk=cst["gk"], causal=cst["causal"],
                  ident=cst["ident"], wgt=wgt, wut=wut, wdt=wdt,
                  sgate=sgate, sup=sup, sdown=sdown)

    in_maps = []
    for core in range(N_CORES):
        bq, qq = core // NQ, core % NQ
        m = dict(shared)
        m["x"] = np.ascontiguousarray(X[bq, qq * T:(qq + 1) * T, :].astype(np.float32))
        m["that"] = cst["that_by_core"][core]
        in_maps.append(m)

    nc = build_program()
    res = run_bass_kernel_spmd(nc, in_maps, core_ids=list(range(N_CORES)))

    out = np.empty((B, S, H), np.float32)
    for core in range(N_CORES):
        bq, qq = core // NQ, core % NQ
        out[bq, qq * T:(qq + 1) * T, :] = res.results[core]["y"]
    return out


# version 14
# speedup vs baseline: 1.0512x; 1.0227x over previous
import numpy as np
import ml_dtypes
import concourse.bass as bass
import concourse.mybir as mybir
from concourse.bass_utils import run_bass_kernel_spmd

F32 = mybir.dt.float32
BF16 = mybir.dt.bfloat16
I8 = mybir.dt.int8
AF = mybir.ActivationFunctionType
ALU = mybir.AluOpType
AX = mybir.AxisListType
bfdt = ml_dtypes.bfloat16

B, S, H = 2, 8192, 2048
NH, D = 16, 128
T = 2048
NCH = 16
C = 128
INTER = 8192
SCALE_BASE = 512.0
GN_EPS = 1e-5
LN_EPS = 1e-5
N_CORES = 8
NQ = 4
NTB = 32
NIB = 256
NSTRIP = 4
STRIP = 512
DBGH = 8
DBG4 = False
SERIALIZE = False



def _xpos_tables_f64(d, positions):
    half = d // 2
    xscale = (np.arange(0, d, 2, dtype=np.float64) + 0.4 * d) / (1.4 * d)
    pos = positions.astype(np.float64)
    scale = xscale[None, :] ** (pos[:, None] / SCALE_BASE)
    inv_freq = 1.0 / 10000.0 ** (np.arange(half, dtype=np.float64) / half)
    sinu = pos[:, None] * inv_freq[None, :]
    sin, cos = np.sin(sinu), np.cos(sinu)
    dup = lambda t: np.repeat(t, 2, axis=-1)
    return dup(cos * scale), dup(sin * scale), dup(cos / scale), dup(sin / scale)


def _build_host_constants():
    gammas = 1.0 - np.exp(np.linspace(np.log(1 / 32), np.log(1 / 512), NH))
    p = np.arange(T)
    cq, sq, ck, sk = _xpos_tables_f64(D, p)
    sgn = np.ones(D)
    sgn[0::2] = -1.0
    sq = sq * sgn[None, :]
    sk = sk * sgn[None, :]

    def to_col(tab):
        return np.ascontiguousarray(
            tab.reshape(NCH, C, D).transpose(1, 0, 2).reshape(C, NCH * D))

    cq_t = to_col(cq).astype(bfdt)
    sq_t = to_col(sq).astype(bfdt)
    ck_t = to_col(ck).astype(bfdt)
    sk_t = to_col(sk).astype(bfdt)

    pc = (np.arange(NCH)[None, :] * C + np.arange(C)[:, None] + 1.0)
    gq = np.empty((C, NH * NCH), np.float64)
    gk = np.empty((C, NH * NCH), np.float64)
    for n in range(NH):
        gq[:, n * NCH:(n + 1) * NCH] = gammas[n] ** pc
        gk[:, n * NCH:(n + 1) * NCH] = gammas[n] ** (-pc)
    gq = gq.astype(np.float32)
    gk = gk.astype(np.float32)

    causal = (np.arange(C)[None, :] >= np.arange(C)[:, None]).astype(np.float64)
    causal_rep = np.ascontiguousarray(causal.astype(bfdt))
    ident = np.ascontiguousarray(np.eye(C, dtype=bfdt))

    half = D // 2
    xscale = (np.arange(0, D, 2, dtype=np.float64) + 0.4 * D) / (1.4 * D)
    inv_freq = 1.0 / 10000.0 ** (np.arange(half, dtype=np.float64) / half)

    def that_lhsT(D_off, n):
        sig = xscale ** (D_off / SCALE_BASE)
        th = D_off * inv_freq
        g = float(np.float32(gammas[n] ** D_off))
        M = np.zeros((D, D))
        for i in range(half):
            cth, sth = np.cos(th[i]), np.sin(th[i])
            M[2 * i, 2 * i] = cth * sig[i] * g
            M[2 * i, 2 * i + 1] = sth * sig[i] * g
            M[2 * i + 1, 2 * i] = -sth * sig[i] * g
            M[2 * i + 1, 2 * i + 1] = cth * sig[i] * g
        return M.T

    that_by_core = []
    for core in range(N_CORES):
        bq, qq = core // NQ, core % NQ
        that = np.zeros((N_CORES, NH, D, D), np.float64)
        for r in range(N_CORES):
            rb, rq = r // NQ, r % NQ
            if rb == bq and rq < qq:
                for n in range(NH):
                    that[r, n] = that_lhsT(T * (qq - rq), n)
        that_by_core.append(np.ascontiguousarray(that.astype(bfdt)))

    return dict(cq=cq_t, sq=sq_t, ck=ck_t, sk=sk_t, gq=gq, gk=gk,
                causal=causal_rep, ident=ident, that_by_core=that_by_core)



class Buf:
    __slots__ = ("writer", "readers", "sem", "group", "writers")

    def __init__(self, sem=None, group=False):
        self.writer = None
        self.writers = []
        self.readers = []
        self.sem = sem
        self.group = group


class Sched:
    ENGINES = ("te", "act", "dve", "pool", "sp")

    def __init__(self):
        self.ops = []
        self.last_dma = {}

    def add(self, eng, fn, reads=(), writes=(), deps=(), dma=False, dsem=None,
            chain=True):
        idx = len(self.ops)
        dep = set(deps)
        if SERIALIZE and idx > 0:
            dep.add(idx - 1)
        for b in reads:
            if b.group:
                dep.update(b.writers)
            elif b.writer is not None:
                dep.add(b.writer)
        for b in writes:
            if b.group:
                dep.update(b.readers)
                b.writers.append(idx)
            else:
                if b.writer is not None:
                    dep.add(b.writer)
                dep.update(b.readers)
                b.writer = idx
                b.readers = []
        dep.discard(idx)
        for d in dep:
            assert d < idx
        if dma:
            if dsem is None and writes:
                dsem = writes[0].sem
            assert dsem is not None, "dma op needs a dsem"
            if chain and dsem in self.last_dma:
                dep.add(self.last_dma[dsem])
            self.last_dma[dsem] = idx
        self.ops.append(dict(i=idx, eng=eng, fn=fn, deps=sorted(dep), dma=dma,
                             dsem=dsem))
        for b in reads:
            b.readers.append(idx)
        for b in writes:
            if not b.group:
                pass
        return idx

    def finalize(self):
        needed = set()
        for op in self.ops:
            needed.update(op["deps"])
        counters = {}
        pos = {}
        for op in self.ops:
            op["needed"] = op["i"] in needed or op["dma"]
            if op["needed"]:
                key = ("d", op["dsem"]) if op["dma"] else ("c", op["eng"])
                inc = 16 if op["dma"] else 1
                counters[key] = counters.get(key, 0) + inc
                pos[op["i"]] = (key, counters[key])
                op["inc"] = inc
        clocks = {e: {} for e in self.ENGINES}
        for op in self.ops:
            wmap = {}
            for d in op["deps"]:
                dop = self.ops[d]
                key, tgt = pos[d]
                if (not dop["dma"]) and dop["eng"] == op["eng"] == "te":
                    continue
                if wmap.get(key, 0) < tgt:
                    wmap[key] = tgt
            clk = clocks[op["eng"]]
            ww = []
            for key in sorted(wmap):
                tgt = wmap[key]
                if clk.get(key, 0) >= tgt:
                    continue
                clk[key] = tgt
                ww.append((key, tgt))
            op["waits"] = ww

    def sem_keys(self):
        ks = set()
        for op in self.ops:
            if op.get("needed"):
                ks.add(("d", op["dsem"]) if op["dma"] else ("c", op["eng"]))
        return sorted(ks)

    def emit_engine(self, eng_name, eng_obj, sems):
        for op in self.ops:
            if op["eng"] != eng_name:
                continue
            for key, tgt in op["waits"]:
                eng_obj.wait_ge(sems[key], tgt)
            inst = op["fn"](eng_obj)
            if op["needed"]:
                assert inst is not None, f"op {op['i']} on {eng_name} returned None"
                key = ("d", op["dsem"]) if op["dma"] else ("c", op["eng"])
                inst.then_inc(sems[key], op["inc"])



def build_program(stop_after=None):
    nc = bass.Bass("TRN2", target_bir_lowering=False)

    dr = {}
    ei = "ExternalInput"
    dr["x"] = nc.dram_tensor("x", [T, H], F32, kind=ei)
    for nm in ("wq", "wk", "wv", "wg"):
        dr[nm] = nc.dram_tensor(nm, [16, C, 16, D], F32, kind=ei)
    dr["wo"] = nc.dram_tensor("wo", [H, H], F32, kind=ei)
    dr["gnw"] = nc.dram_tensor("gnw", [C, H], F32, kind=ei)
    dr["gnb"] = nc.dram_tensor("gnb", [C, H], F32, kind=ei)
    for nm in ("cq", "sq", "ck", "sk"):
        dr[nm] = nc.dram_tensor(nm, [C, T], BF16, kind=ei)
    dr["causal"] = nc.dram_tensor("causal", [C, C], BF16, kind=ei)
    dr["gq"] = nc.dram_tensor("gq", [C, NH * NCH], F32, kind=ei)
    dr["gk"] = nc.dram_tensor("gk", [C, NH * NCH], F32, kind=ei)
    dr["ident"] = nc.dram_tensor("ident", [C, C], BF16, kind=ei)
    dr["that"] = nc.dram_tensor("that", [N_CORES, NH, D, D], BF16, kind=ei)
    dr["wgt"] = nc.dram_tensor("wgt", [NTB, 2, C, 16, 128], I8, kind=ei)
    dr["wut"] = nc.dram_tensor("wut", [NTB, 2, C, 16, 128], I8, kind=ei)
    dr["wdt"] = nc.dram_tensor("wdt", [2, 16, C, 4, 1024], I8, kind=ei)
    dr["sgate"] = nc.dram_tensor("sgate", [C, INTER // C], F32, kind=ei)
    dr["sup"] = nc.dram_tensor("sup", [C, INTER // C], F32, kind=ei)
    dr["sdown"] = nc.dram_tensor("sdown", [C, H], F32, kind=ei)
    dr["y"] = nc.dram_tensor("y", [T, H], F32, kind="ExternalOutput")
    dr["vqg"] = nc.dram_tensor("vqg", [NH, 3, C, T], BF16, kind="Internal")
    dr["ztsp"] = nc.dram_tensor("ztsp", [NH, C, T], BF16, kind="Internal")
    dr["y1sp"] = nc.dram_tensor("y1sp", [NCH, C, H], F32, kind="Internal")
    dr["ccin"] = nc.dram_tensor("ccin", [C, NH * D + 0], BF16, kind="Internal")
    dr["ccout"] = nc.dram_tensor("ccout", [N_CORES * C, NH * D], BF16, kind="Internal")

    A = {k: v.ap() for k, v in dr.items()}

    XT = nc.alloc_sbuf_tensor("XTa", [128, 16 * T], BF16)
    KT = nc.alloc_sbuf_tensor("KTa", [128, 16 * T], BF16)
    COLS = nc.alloc_sbuf_tensor("COLSa", [128, 8 * T], BF16)
    F32A = nc.alloc_sbuf_tensor("F32Aa", [128, 3 * H], F32)
    TAB = nc.alloc_sbuf_tensor("TABa", [128, 2 * T], BF16)
    CAU = nc.alloc_sbuf_tensor("CAUa", [128, C], BF16)
    GNBB = nc.alloc_sbuf_tensor("GNBBa", [128, H], BF16)
    GAM = nc.alloc_sbuf_tensor("GAMa", [128, 2 * NH * NCH], F32)
    IDN = nc.alloc_sbuf_tensor("IDNa", [128, C], BF16)
    GUB = nc.alloc_sbuf_tensor("GUBa", [128, 2 * STRIP], BF16)
    ST = nc.alloc_sbuf_tensor("STa", [128, 180], F32)
    SNB = nc.alloc_sbuf_tensor("SNBa", [128, T], BF16)
    SGC = nc.alloc_sbuf_tensor("SGCa", [128, 2 * (INTER // C)], F32)

    xt = XT.ap(); kt = KT.ap(); cols = COLS.ap(); f32a = F32A.ap()
    tab = TAB.ap(); cau = CAU.ap(); gnbb = GNBB.ap(); gam = GAM.ap()
    idn = IDN.ap(); gub = GUB.ap(); st = ST.ap(); snb = SNB.ap(); sgc = SGC.ap()
    tabf = tab.bitcast(F32)

    PA = nc.alloc_psum_tensor("PAa", [128, 2048], F32)
    PB = nc.alloc_psum_tensor("PBa", [128, 2048], F32)
    pa = PA.ap(); pb = PB.ap()
    pbb = pb.bitcast(BF16)

    Sc = Sched()
    bXT = [Buf(sem="xt") for _ in range(16)]
    bKT = [Buf() for _ in range(16)]
    bCOL = [Buf(sem=f"cols{i}") for i in range(8)]
    bF32 = [Buf(sem=f"f32{i}") for i in range(3)]
    bTAB = Buf(sem="tab"); bCAU = Buf(sem="cau"); bGNB = Buf(sem="gnb")
    bGAM = Buf(sem="gam"); bIDN = Buf(sem="idn")
    bGUB = [Buf(), Buf()]; bST = Buf(); bSNB = Buf(sem="snb"); bSGC = Buf(sem="sgc")
    bKTT = [Buf() for _ in range(16)]
    bBank = [Buf() for _ in range(8)]
    bDR = {k: Buf(sem=k, group=True)
           for k in ("vqg", "ztsp", "y1sp")}
    bDR["ccin"] = Buf(sem="ccin")
    bDR["ccout"] = Buf()

    banksA = [bBank[i] for i in range(4)]
    banksB = [bBank[i] for i in range(4, 8)]
    bTRhi = [bBank[6], bBank[7]]
    bTRlo = [bBank[4], bBank[5]]

    cs = lambda i: cols[:, i * T:(i + 1) * T]
    xts = lambda i: xt[:, i * T:(i + 1) * T]
    pab = pa.bitcast(BF16)
    fs = lambda i: f32a[:, i * H:(i + 1) * H]
    sti = lambda i: st[:, i * 20:i * 20 + 16]
    st1 = lambda i: st[:, i * 20:i * 20 + 1]

    eps_col = st[:, 176:177]
    Sc_eps_done = []

    def dma(engq, out, in_, reads=(), writes=(), deps=(), dsem=None, chain=True):
        return Sc.add(engq, lambda e, o=out, i=in_: e.dma_start(out=o, in_=i),
                      reads=reads, writes=writes, deps=deps, dma=True, dsem=dsem,
                      chain=chain)

    def finish():
        from contextlib import ExitStack
        Sc.finalize()
        with ExitStack() as stack:
            sems = {}
            for key in Sc.sem_keys():
                nm = ("cs_" + key[1]) if key[0] == "c" else ("ds_" + str(key[1]))
                sems[key] = stack.enter_context(nc.semaphore(nm))
            block = stack.enter_context(nc.Block())

            @block.tensor
            def _(eng):
                Sc.emit_engine("te", eng, sems)

            @block.scalar
            def _(eng):
                Sc.emit_engine("act", eng, sems)

            @block.vector
            def _(eng):
                Sc.emit_engine("dve", eng, sems)

            @block.gpsimd
            def _(eng):
                Sc.emit_engine("pool", eng, sems)

            @block.sync
            def _(eng):
                Sc.emit_engine("sp", eng, sems)
        return nc

    def dbg_sbuf(nm, ap, bufs, dtype=BF16, width=16 * T):
        dt_ = nc.dram_tensor(nm, [128, width], dtype, kind="ExternalOutput")
        dd = dma("sp", dt_.ap()[:, :], ap, reads=bufs, writes=[], dsem="out")
        Sc.add("sp", lambda e: e.nop(), deps=[dd])

    def dbg_dram(nm, key, shape, dtype):
        dt_ = nc.dram_tensor(nm, shape, dtype, kind="ExternalOutput")
        idx = tuple(slice(None) for _ in shape)
        dd = dma("sp", dt_.ap()[idx], A[key][idx], reads=[bDR[key]], writes=[], dsem="out")
        Sc.add("sp", lambda e: e.nop(), deps=[dd])

    Sc.add("dve", lambda e: e.memset(eps_col, 1e-5), writes=[bST])
    dma("sp", cau[:, :], A["causal"][:, :], writes=[bCAU])
    dma("sp", idn[:, :], A["ident"][:, :], writes=[bIDN])
    dma("sp", gam[:, 0:NH * NCH], A["gq"][:, :], writes=[bGAM])
    dma("sp", gam[:, NH * NCH:2 * NH * NCH], A["gk"][:, :], writes=[bGAM])
    dma("sp", tab[:, 0:T], A["ck"][:, :], writes=[bTAB])
    dma("sp", tab[:, T:2 * T], A["sk"][:, :], writes=[bTAB])

    for c_ in range(NCH):
        pp = c_ % 2
        xst = fs(pp)
        dma("sp", xst, A["x"][c_ * C:(c_ + 1) * C, :], reads=[bF32[pp]],
            writes=[bF32[pp]], chain=False)
        xb = cs(pp)
        Sc.add("act", lambda e, o=xb, i=xst: e.copy(o, i),
               reads=[bF32[pp]], writes=[bCOL[pp]])

        def mk_tp(src=xb):
            def f(e):
                last = None
                for ht in range(16):
                    last = nc.tensor.matmul(
                        pbb[:, 2048 + ht * 128:2048 + (ht + 1) * 128],
                        src[:, ht * 128:(ht + 1) * 128], idn[:, :],
                        start=(ht % 8 == 0), stop=(ht % 8 == 7),
                        is_transpose=True, skip_group_check=True)
                return last
            return f
        Sc.add("te", mk_tp(), reads=[bCOL[pp], bIDN], writes=bTRhi)
        dst = xt.rearrange("p (ht t) -> p ht t", ht=16)[:, :, c_ * C:(c_ + 1) * C]
        src = pbb[:, 2048:4096].rearrange("p (ht a) -> p ht a", ht=16)
        Sc.add("act", lambda e, o=dst, i=src: e.copy(o, i),
               reads=bTRhi, writes=list(bXT))

    if stop_after == "p0":
        dbg_sbuf("d_xt", xt[:, :], list(bXT))
        return finish()

    def proj_tile(n, wap, wbuf, wapb, out_ap, out_buf):
        dma("pool", wapb.rearrange("p (ht d) -> p ht d", ht=16),
            wap[n], reads=[wbuf], writes=[wbuf], chain=False)
        for half in range(2):
            def mk_mm(half=half, wapb=wapb):
                def f(e):
                    last = None
                    for ht in range(16):
                        for j in range(2):
                            off = ht * T + half * 1024 + j * 512
                            last = nc.tensor.matmul(
                                pb[:, j * 512:(j + 1) * 512],
                                wapb[:, ht * D:(ht + 1) * D],
                                xt[:, off:off + 512],
                                start=(ht == 0), stop=(ht == 15),
                                skip_group_check=True)
                    return last
                return f
            Sc.add("te", mk_mm(), reads=[wbuf] + list(bXT), writes=bTRlo)
            Sc.add("act",
                   lambda e, o=out_ap[:, half * 1024:(half + 1) * 1024],
                   i=pb[:, 0:1024]: e.copy(o, i),
                   reads=bTRlo, writes=[out_buf])

    def transpose16(src_ap, src_buf, dst_ap, dst_bufs, region="hi"):
        reg_off = 2048 if region == "hi" else 0
        reg_bufs = bTRhi if region == "hi" else bTRlo

        def f(e, src_ap=src_ap, reg_off=reg_off):
            last = None
            for c_ in range(16):
                last = nc.tensor.matmul(
                    pbb[:, reg_off + c_ * 128:reg_off + (c_ + 1) * 128],
                    src_ap[:, c_ * 128:(c_ + 1) * 128], idn[:, :],
                    start=(c_ % 8 == 0), stop=(c_ % 8 == 7),
                    is_transpose=True, skip_group_check=True)
            return last
        Sc.add("te", f, reads=[src_buf, bIDN], writes=reg_bufs)
        Sc.add("act",
               lambda e, o=dst_ap, i=pbb[:, reg_off:reg_off + 2048]: e.copy(o, i),
               reads=reg_bufs, writes=list(dst_bufs))

    def xpos_apply(col_ap, col_buf, gam_off, out_ap, out_bufs):
        tmp = fs(0)
        sw = col_ap.rearrange("p (c k two) -> p c k two", c=NCH, two=2)[:, :, :, ::-1]
        stv = tab[:, T:2 * T].rearrange("p (c k two) -> p c k two", c=NCH, two=2)
        Sc.add("dve", lambda e, o=tmp, i0=sw, i1=stv:
               e.tensor_tensor(o.rearrange("p (c k two) -> p c k two", c=NCH, two=2),
                               i0, i1, ALU.mult),
               reads=[col_buf, bTAB], writes=[bF32[0]])
        Sc.add("dve", lambda e, o=col_ap, i1=tab[:, 0:T]:
               e.tensor_tensor(o, o, i1, ALU.mult),
               reads=[col_buf, bTAB], writes=[col_buf])
        Sc.add("dve", lambda e, o=col_ap, i1=tmp:
               e.tensor_tensor(o, o, i1, ALU.add),
               reads=[col_buf, bF32[0]], writes=[col_buf])
        gview = gam[:, gam_off:gam_off + NCH].unsqueeze(2).broadcast_to([128, NCH, D])
        Sc.add("dve", lambda e, o=out_ap, i0=col_ap, i1=gview:
               e.tensor_tensor(o.rearrange("p (c d) -> p c d", c=NCH),
                               i0.rearrange("p (c d) -> p c d", c=NCH),
                               i1, ALU.mult),
               reads=[col_buf, bGAM], writes=list(out_bufs))

    for n in range(NH):
        wkb, wka = bCOL[3 + (n % 2)], cs(3 + (n % 2))
        wvb, wva = bCOL[5 + (n % 2)], cs(5 + (n % 2))
        proj_tile(n, A["wk"], wkb, wka, cs(0), bCOL[0])
        transpose16(cs(0), bCOL[0], kt[:, n * T:(n + 1) * T], [bKT[n]])
        xpos_apply(kt[:, n * T:(n + 1) * T], bKT[n], NH * NCH + n * NCH,
                   kt[:, n * T:(n + 1) * T], [bKT[n]])
        proj_tile(n, A["wv"], wvb, wva, cs(1), bCOL[1])
        transpose16(cs(1), bCOL[1], cs(2), [bCOL[2]])
        dma("sp", A["vqg"][n, 0], cs(2), reads=[bCOL[2]], writes=[bDR["vqg"]])

        def mk_state(n=n):
            def f(e):
                last = None
                for c_ in range(16):
                    last = nc.tensor.matmul(
                        pa[:, n * D:(n + 1) * D],
                        kt[:, n * T + c_ * 128:n * T + (c_ + 1) * 128],
                        cs(2)[:, c_ * 128:(c_ + 1) * 128],
                        start=(n % 4 == 0 and c_ == 0),
                        stop=(n % 4 == 3 and c_ == 15),
                        skip_group_check=True)
                return last
            return f
        Sc.add("te", mk_state(), reads=[bKT[n], bCOL[2]], writes=[bBank[n // 4]])

    Sc.add("act", lambda e, o=cs(7), i=pa[:, :]: e.copy(o, i),
           reads=banksA, writes=[bCOL[7]])
    dma("sp", A["ccin"][:, :], cs(7), reads=[bCOL[7]], writes=[bDR["ccin"]])
    Sc.add("pool", lambda e: e.collective_compute(
        "AllGather", ALU.bypass, [list(range(N_CORES))],
        [A["ccin"][:, :]], [A["ccout"][:, :]]),
        reads=[bDR["ccin"]], writes=[bDR["ccout"]])

    if stop_after == "p1":
        dbg_sbuf("d_kt", kt[:, :], list(bKT))
        dbg_dram("d_vqg", "vqg", [NH, 3, C, T], BF16)
        dbg_dram("d_ccin", "ccin", [C, NH * D + 0], BF16)
        return finish()

    dma("sp", tab[:, 0:T], A["cq"][:, :], reads=[bTAB], writes=[bTAB])
    dma("sp", tab[:, T:2 * T], A["sq"][:, :], reads=[bTAB], writes=[bTAB])
    for n in range(NH):
        wqb, wqa = bCOL[3 + (n % 2)], cs(3 + (n % 2))
        wgb, wga = bCOL[5 + (n % 2)], cs(5 + (n % 2))
        proj_tile(n, A["wq"], wqb, wqa, cs(0), bCOL[0])
        transpose16(cs(0), bCOL[0], cs(1), [bCOL[1]])
        xpos_apply(cs(1), bCOL[1], n * NCH, cs(1), [bCOL[1]])
        dma("sp", A["vqg"][n, 1], cs(1), reads=[bCOL[1]], writes=[bDR["vqg"]])
        proj_tile(n, A["wg"], wgb, wga, cs(0), bCOL[0])
        transpose16(cs(0), bCOL[0], cs(2), [bCOL[2]])
        Sc.add("act", lambda e, o=cs(2): e.activation(o, o, AF.Silu),
               reads=[bCOL[2]], writes=[bCOL[2]])
        dma("sp", A["vqg"][n, 2], cs(2), reads=[bCOL[2]], writes=[bDR["vqg"]])

    for r in range(N_CORES):
        dma("sp", cs(r), A["ccout"][r * C:(r + 1) * C, :],
            reads=[bDR["ccout"], bCOL[r]], writes=[bCOL[r]], dsem="cco")
    for r in range(N_CORES):
        dma("sp", snb[:, :].rearrange("p (n d) -> p n d", n=NH),
            A["that"][r].rearrange("n p d -> p n d"),
            reads=[bSNB], writes=[bSNB], chain=False)

        def mk_that(r=r):
            def f(e):
                last = None
                for n in range(NH):
                    last = nc.tensor.matmul(
                        pa[:, n * D:(n + 1) * D],
                        snb[:, n * D:(n + 1) * D],
                        cs(r)[:, n * D:(n + 1) * D],
                        start=(r == 0 and n % 4 == 0),
                        stop=(r == N_CORES - 1 and n % 4 == 3),
                        skip_group_check=True)
                return last
            return f
        Sc.add("te", mk_that(), reads=[bSNB, bCOL[r]], writes=banksA)
    Sc.add("act", lambda e, o=xts(12), i=pa[:, :]: e.copy(o, i),
           reads=banksA + [bXT[12]], writes=[bXT[12]])

    if stop_after == "exch":
        dbg_sbuf("d_sin", xts(12), [bXT[12]], dtype=BF16, width=T)
        dbg_dram("d_vqg", "vqg", [NH, 3, C, T], BF16)
        return finish()

    dma("sp", tabf[:, :], A["gnw"][:, :], reads=[bTAB], writes=[bTAB])
    dma("pool", gnbb[:, :], A["gnb"][:, :], writes=[bGNB])
    bSTp = [Buf(), Buf()]

    for n in range(NH):
        P_ = n % 2
        reg = (pa, pb)[P_]
        regb = (pab, pbb)[P_]
        reg3 = reg[:, :].rearrange("p (c d) -> p c d", c=NCH)
        rbanks = [bBank[P_ * 4 + i] for i in range(4)]
        rb01 = rbanks[0:2]
        sb0 = P_ * 6
        ktT, qtT, atm = xts(sb0), xts(sb0 + 1), xts(sb0 + 2)
        zc, ztc, dsB = xts(sb0 + 3), xts(sb0 + 4), xts(sb0 + 5)
        bktT, bqtT, batm = bXT[sb0], bXT[sb0 + 1], bXT[sb0 + 2]
        bzc, bztc, bdsB = bXT[sb0 + 3], bXT[sb0 + 4], bXT[sb0 + 5]
        snbP, bsnbP = xts(13 + P_), bXT[13 + P_]
        vc, qc, gc = cs(3 * P_), cs(3 * P_ + 1), cs(3 * P_ + 2)
        bvqg = [bCOL[3 * P_], bCOL[3 * P_ + 1], bCOL[3 * P_ + 2]]
        syF, bsyF = fs(1 + P_), bF32[1 + P_]
        so = P_ * 4
        stS = lambda i: st[:, (so + i) * 20:(so + i) * 20 + 16]
        bst = bSTp[P_]

        for k, q_ in ((0, "sp"), (1, "act"), (2, "pool")):
            dma(q_, cs(3 * P_ + k), A["vqg"][n, k],
                reads=[bDR["vqg"], bvqg[k]], writes=[bvqg[k]],
                dsem=f"vqg{P_}{k}", chain=False)

        def tp16(src_ap, src_buf, dst_ap, dst_buf, regb=regb, rb01=rb01):
            def f(e, src_ap=src_ap, regb=regb):
                last = None
                for c_ in range(16):
                    last = nc.tensor.matmul(
                        regb[:, c_ * 128:(c_ + 1) * 128],
                        src_ap[:, c_ * 128:(c_ + 1) * 128], idn[:, :],
                        start=(c_ % 8 == 0), stop=(c_ % 8 == 7),
                        is_transpose=True, skip_group_check=True)
                return last
            Sc.add("te", f, reads=[src_buf, bIDN], writes=rb01)
            Sc.add("act", lambda e, o=dst_ap, i=regb[:, 0:2048]: e.copy(o, i),
                   reads=rb01, writes=[dst_buf])

        tp16(kt[:, n * T:(n + 1) * T], bKT[n], ktT, bktT)
        tp16(qc, bvqg[1], qtT, bqtT)

        def mk_amm(ktT=ktT, qtT=qtT, reg=reg):
            def f(e):
                last = None
                for c_ in range(16):
                    last = nc.tensor.matmul(
                        reg[:, c_ * 128:(c_ + 1) * 128],
                        ktT[:, c_ * 128:(c_ + 1) * 128],
                        qtT[:, c_ * 128:(c_ + 1) * 128],
                        start=(c_ % 4 == 0), stop=(c_ % 4 == 3),
                        skip_group_check=True)
                return last
            return f
        Sc.add("te", mk_amm(), reads=[bktT, bqtT], writes=rbanks)
        cau_b = cau[:, :].unsqueeze(1).broadcast_to([128, NCH, C])
        Sc.add("dve", lambda e, o=atm, i0=reg3, i1=cau_b: e.tensor_tensor(
            o.rearrange("p (c a) -> p c a", c=NCH), i0, i1, ALU.mult),
               reads=rbanks + [bCAU], writes=[batm])

        def mk_ds(n=n, vc=vc, reg=reg):
            def f(e):
                last = None
                for c_ in range(16):
                    last = nc.tensor.matmul(
                        reg[:, c_ * 128:(c_ + 1) * 128],
                        kt[:, n * T + c_ * 128:n * T + (c_ + 1) * 128],
                        vc[:, c_ * 128:(c_ + 1) * 128],
                        start=(c_ % 4 == 0), stop=(c_ % 4 == 3),
                        skip_group_check=True)
                return last
            return f
        Sc.add("te", mk_ds(), reads=[bKT[n], bvqg[0]], writes=rbanks)
        Sc.add("act", lambda e, o=dsB, i=reg[:, :]: e.copy(o, i),
               reads=rbanks, writes=[bdsB])
        Sc.add("dve", lambda e, o=reg[:, 0:128], i=xts(12)[:, n * D:(n + 1) * D]:
               e.tensor_copy(o, i),
               reads=[bXT[12], bdsB] + rbanks, writes=[rbanks[0]])
        for c_ in range(15):
            Sc.add("dve", lambda e, c_=c_, reg=reg, dsB=dsB: e.tensor_add(
                reg[:, (c_ + 1) * 128:(c_ + 2) * 128],
                reg[:, c_ * 128:(c_ + 1) * 128],
                dsB[:, c_ * 128:(c_ + 1) * 128]),
                reads=[bdsB, rbanks[c_ // 4], rbanks[(c_ + 1) // 4]],
                writes=[rbanks[(c_ + 1) // 4]])
        Sc.add("act", lambda e, o=snbP, i=reg[:, :]: e.copy(o, i),
               reads=rbanks, writes=[bsnbP])

        def mk_y(atm=atm, vc=vc, qtT=qtT, snbP=snbP, reg=reg):
            def f(e):
                last = None
                for c_ in range(16):
                    nc.tensor.matmul(
                        reg[:, c_ * 128:(c_ + 1) * 128],
                        atm[:, c_ * 128:(c_ + 1) * 128],
                        vc[:, c_ * 128:(c_ + 1) * 128],
                        start=(c_ % 4 == 0), stop=False, skip_group_check=True)
                    last = nc.tensor.matmul(
                        reg[:, c_ * 128:(c_ + 1) * 128],
                        qtT[:, c_ * 128:(c_ + 1) * 128],
                        snbP[:, c_ * 128:(c_ + 1) * 128],
                        start=False, stop=(c_ % 4 == 3), skip_group_check=True)
                return last
            return f
        Sc.add("te", mk_y(), reads=[batm, bvqg[0], bqtT, bsnbP], writes=rbanks)
        Sc.add("act", lambda e, o=syF, i=reg[:, :]: e.copy(o, i),
               reads=rbanks, writes=[bsyF])

        sy3 = syF.rearrange("p (c d) -> p c d", c=NCH)
        Sc.add("dve", lambda e, o=stS(0), i=sy3: e.reduce_sum(o, i, axis=AX.X),
               reads=[bsyF], writes=[bst])
        Sc.add("act", lambda e, o=reg[:, :], i=syF: e.activation(o, i, AF.Square),
               reads=[bsyF] + rbanks, writes=rbanks)
        Sc.add("dve", lambda e, o=stS(1), i=reg3: e.reduce_sum(o, i, axis=AX.X),
               reads=rbanks, writes=[bst])
        Sc.add("dve", lambda e, a=stS(0): e.tensor_scalar_mul(a, a, 1.0 / D),
               reads=[bst], writes=[bst])
        Sc.add("dve", lambda e, a=stS(1): e.tensor_scalar_mul(a, a, 1.0 / D),
               reads=[bst], writes=[bst])
        Sc.add("dve", lambda e, a=stS(2), b=stS(0): e.tensor_mul(a, b, b),
               reads=[bst], writes=[bst])
        Sc.add("dve", lambda e, a=stS(1), b=stS(2): e.tensor_sub(a, a, b),
               reads=[bst], writes=[bst])
        Sc.add("act", lambda e, a=stS(2), b=stS(1): e.activation(a, b, AF.Sqrt,
                                                                 bias=eps_col),
               reads=[bst], writes=[bst])
        Sc.add("dve", lambda e, a=stS(3), b=stS(2): e.reciprocal(a, b),
               reads=[bst], writes=[bst])
        mu_b = stS(0).unsqueeze(2).broadcast_to([128, NCH, D])
        rs_b = stS(3).unsqueeze(2).broadcast_to([128, NCH, D])
        gnw_b = tabf[:, n * D:(n + 1) * D].unsqueeze(1).broadcast_to([128, NCH, D])
        gnb_b = gnbb[:, n * D:(n + 1) * D].unsqueeze(1).broadcast_to([128, NCH, D])
        Sc.add("dve", lambda e, o=sy3, i1=mu_b:
               e.tensor_tensor(o, o, i1, ALU.subtract),
               reads=[bst, bsyF], writes=[bsyF])
        Sc.add("pool", lambda e, o=sy3, i1=rs_b: e.tensor_tensor(o, o, i1,
                                                                 ALU.mult),
               reads=[bst, bsyF], writes=[bsyF])
        Sc.add("dve", lambda e, o=sy3, i1=gnw_b: e.tensor_tensor(o, o, i1,
                                                                 ALU.mult),
               reads=[bTAB, bsyF], writes=[bsyF])
        Sc.add("pool", lambda e, o=sy3, i1=gnb_b: e.tensor_tensor(o, o, i1,
                                                                  ALU.add),
               reads=[bGNB, bsyF], writes=[bsyF])
        gc3 = gc.rearrange("p (c d) -> p c d", c=NCH)
        Sc.add("dve", lambda e, o=zc, i0=sy3, i1=gc3: e.tensor_tensor(
            o.rearrange("p (c d) -> p c d", c=NCH), i0, i1, ALU.mult),
               reads=[bvqg[2], bsyF], writes=[bzc])
        tp16(zc, bzc, ztc, bztc)
        dma("sp", A["ztsp"][n], ztc, reads=[bztc], writes=[bDR["ztsp"]])

    if stop_after == "p2b":
        dbg_dram("d_ztsp", "ztsp", [NH, C, T], BF16)
        return finish()

    dma("pool", xt.rearrange("p (ht t) -> p ht t", ht=16),
        A["wo"].rearrange("(ht p) o -> p ht o", p=128),
        reads=list(bXT), writes=list(bXT))
    for c_ in range(NCH):
        ztc = cs(c_ % 2)
        dma("sp", ztc.rearrange("p (n a) -> p n a", n=NH),
            A["ztsp"][:, :, c_ * C:(c_ + 1) * C].rearrange("n p a -> p n a"),
            reads=[bDR["ztsp"], bCOL[c_ % 2]], writes=[bCOL[c_ % 2]], chain=False)

        regP = (pa, pb)[c_ % 2]
        regPb = (pab, pbb)[c_ % 2]
        rbk = [bBank[(c_ % 2) * 4 + i] for i in range(4)]

        def mk_wo(ztc=ztc, regP=regP):
            def f(e):
                last = None
                for ht in range(16):
                    for j in range(4):
                        last = nc.tensor.matmul(
                            regP[:, j * 512:(j + 1) * 512],
                            ztc[:, ht * 128:(ht + 1) * 128],
                            xt[:, ht * T + j * 512:ht * T + (j + 1) * 512],
                            start=(ht == 0), stop=(ht == 15),
                            skip_group_check=True)
                return last
            return f
        Sc.add("te", mk_wo(), reads=[bCOL[c_ % 2]] + list(bXT), writes=rbk)
        xres = fs(0)
        dma("act", xres, A["x"][c_ * C:(c_ + 1) * C, :],
            reads=[bF32[0]], writes=[bF32[0]], chain=False)
        y1 = fs(1)
        Sc.add("dve", lambda e, o=y1, i1=xres, i0=regP: e.tensor_add(o, i0[:, :], i1),
               reads=rbk + [bF32[0]], writes=[bF32[1]])
        Sc.add("dve", lambda e, o=st1(0), i=y1: e.reduce_sum(o, i, axis=AX.X),
               reads=[bF32[1]], writes=[bST])
        Sc.add("act", lambda e, o=fs(2), i=y1: e.activation(o, i, AF.Square),
               reads=[bF32[1]], writes=[bF32[2]])
        Sc.add("dve", lambda e, o=st1(1), i=fs(2): e.reduce_sum(o, i, axis=AX.X),
               reads=[bF32[2]], writes=[bST])

        Sc.add("dve", lambda e: e.tensor_scalar_mul(st1(0), st1(0), 1.0 / H),
               reads=[bST], writes=[bST])
        Sc.add("dve", lambda e: e.tensor_scalar_mul(st1(1), st1(1), 1.0 / H),
               reads=[bST], writes=[bST])
        Sc.add("dve", lambda e: e.tensor_mul(st1(2), st1(0), st1(0)),
               reads=[bST], writes=[bST])
        Sc.add("dve", lambda e: e.tensor_sub(st1(1), st1(1), st1(2)),
               reads=[bST], writes=[bST])
        Sc.add("act", lambda e: e.activation(st1(2), st1(1), AF.Sqrt, bias=eps_col),
               reads=[bST], writes=[bST])
        Sc.add("dve", lambda e: e.reciprocal(st1(3), st1(2)), reads=[bST],
               writes=[bST])
        Sc.add("dve", lambda e, o=y1: e.tensor_scalar(o, o, st1(0), st1(3),
                                                      ALU.subtract, ALU.mult),
               reads=[bF32[1], bST], writes=[bF32[1]])
        dma("act", A["y1sp"][c_], y1, reads=[bF32[1]], writes=[bDR["y1sp"]])
        y1b = cs(2)
        Sc.add("act", lambda e, o=y1b, i=y1: e.copy(o, i),
               reads=[bF32[1]], writes=[bCOL[2]])

        def mk_y1tp(y1b=y1b, regPb=regPb):
            def f(e):
                last = None
                for ht in range(16):
                    last = nc.tensor.matmul(
                        regPb[:, ht * 128:(ht + 1) * 128],
                        y1b[:, ht * 128:(ht + 1) * 128], idn[:, :],
                        start=(ht % 8 == 0), stop=(ht % 8 == 7),
                        is_transpose=True, skip_group_check=True)
                return last
            return f
        Sc.add("te", mk_y1tp(), reads=[bCOL[2], bIDN], writes=rbk[0:2])
        dstk = kt.rearrange("p (ht t) -> p ht t", ht=16)[:, :, c_ * C:(c_ + 1) * C]
        srck = regPb[:, 0:2048].rearrange("p (ht a) -> p ht a", ht=16)
        Sc.add("act", lambda e, o=dstk, i=srck: e.copy(o, i),
               reads=rbk[0:2],
               writes=[bKTT[c_]] + (list(bKT) if c_ == 0 else []))

    if stop_after == "p3":
        dbg_dram("d_y1sp", "y1sp", [NCH, C, H], F32)
        dbg_sbuf("d_y1t", kt[:, :], list(bKT))
        return finish()

    dma("sp", tabf[:, :], A["sdown"][:, :], reads=[bTAB], writes=[bTAB])
    dma("sp", sgc[:, 0:INTER // C], A["sgate"][:, :], writes=[bSGC])
    dma("sp", sgc[:, INTER // C:], A["sup"][:, :], writes=[bSGC])

    xtf = xt.bitcast(F32)
    gnbbf = gnbb.bitcast(F32)
    snbf = snb.bitcast(F32)

    def acc_half(tg, hh):
        if tg < 3:
            return fs(tg)[:, hh * 1024:(hh + 1) * 1024]
        return gnbbf if hh == 0 else snbf

    def acc_bufs(tg):
        return [bF32[tg]] if tg < 3 else [bGNB, bSNB]

    stv = lambda tg, i: st[:, tg * 8 + i:tg * 8 + i + 1]

    out_dmas = []
    for s_ in range(NSTRIP):
        for it in range(64):
            tb, j = it // 2, it % 2
            par = it % 2
            wgs, wus = cs(4 + 2 * par), cs(5 + 2 * par)
            bwg, bwu = bCOL[4 + 2 * par], bCOL[5 + 2 * par]
            dma("pool", wgs.rearrange("p (h i) -> p h i", h=16),
                A["wgt"][tb, j], reads=[bwg], writes=[bwg], chain=False)
            dma("pool", wus.rearrange("p (h i) -> p h i", h=16),
                A["wut"][tb, j], reads=[bwu], writes=[bwu], chain=False)
            bank_g = bBank[par * 2]
            bank_u = bBank[par * 2 + 1]
            pg = pa[:, par * 1024:par * 1024 + 512]
            pu = pa[:, par * 1024 + 512:par * 1024 + 1024]

            def mk_gu(pg=pg, pu=pu, wgs=wgs, wus=wus, s_=s_):
                def f(e):
                    last = None
                    for ht in range(16):
                        rhs = kt[:, ht * T + s_ * STRIP:ht * T + (s_ + 1) * STRIP]
                        nc.tensor.matmul(
                            pg, wgs[:, ht * 128:(ht + 1) * 128], rhs,
                            start=(ht == 0), stop=(ht == 15),
                            skip_group_check=True)
                        last = nc.tensor.matmul(
                            pu, wus[:, ht * 128:(ht + 1) * 128], rhs,
                            start=(ht == 0), stop=(ht == 15),
                            skip_group_check=True)
                    return last
                return f
            Sc.add("te", mk_gu(), reads=[bwg, bwu] + [bKTT[4 * s_ + k]
                                                      for k in range(4)],
                   writes=[bank_g, bank_u])
            gb = gub[:, par * STRIP:(par + 1) * STRIP]
            Sc.add("act", lambda e, o=gb, i=pg, it=it:
                   e.activation(o, i, AF.Silu, scale=sgc[:, it:it + 1]),
                   reads=[bank_g, bSGC], writes=[bGUB[par]])
            fu = xt[:, it * STRIP:(it + 1) * STRIP]
            Sc.add("dve", lambda e, o=fu, i0=pu, it=it, gb=gb:
                   e.scalar_tensor_tensor(o, i0, sgc[:, 64 + it:64 + it + 1],
                                          gb, ALU.mult, ALU.mult),
                   reads=[bank_u, bGUB[par], bSGC], writes=[bXT[it // 4]])
        if DBG4 and s_ == 0:
            dbg_sbuf("d_fuse", xt[:, :], list(bXT), dtype=BF16, width=16 * T)
            dbg_sbuf("d_y1fm", kt[:, :], list(bKTT), dtype=BF16, width=16 * T)
            dbg_dram("d_ztsp", "ztsp", [NH, C, T], BF16)
        for hh in range(2):
            for grp in range(16):
                par = grp % 2
                wds = cols[:, (par * 2) * T:(par * 2 + 2) * T]
                bwd = [bCOL[par * 2], bCOL[par * 2 + 1]]
                dma("pool", wds.rearrange("p (g h) -> p g h", g=4),
                    A["wdt"][hh, grp], reads=bwd, writes=bwd,
                    dsem=f"dwn{par}", chain=False)

                def mk_down(grp=grp, wds=wds):
                    def f(e):
                        last = None
                        for g4 in range(4):
                            it = grp * 4 + g4
                            for tg in range(4):
                                psum = (pa, pb)[tg // 2]
                                po = (tg % 2) * 1024
                                for jn in range(2):
                                    last = nc.tensor.matmul(
                                        psum[:, po + jn * 512:po + (jn + 1) * 512],
                                        xt[:, it * STRIP + tg * 128:
                                           it * STRIP + tg * 128 + 128],
                                        wds[:, g4 * 1024 + jn * 512:
                                            g4 * 1024 + (jn + 1) * 512],
                                        start=(it == 0), stop=(it == 63),
                                        skip_group_check=True)
                        return last
                    return f
                Sc.add("te", mk_down(), reads=bwd + list(bXT),
                       writes=banksA + banksB)
            for tg in range(4):
                psum = (pa, pb)[tg // 2]
                po = (tg % 2) * 1024
                pbanks = [bBank[tg * 2], bBank[tg * 2 + 1]]
                dst = acc_half(tg, hh)
                Sc.add("dve", lambda e, o=dst, i0=psum[:, po:po + 1024],
                       i1=tabf[:, hh * 1024:(hh + 1) * 1024]:
                       e.tensor_tensor(o, i0, i1, ALU.mult),
                       reads=pbanks + [bTAB] + acc_bufs(tg), writes=acc_bufs(tg))
        for tg in range(4):
            c_ = s_ * 4 + tg
            babf = acc_bufs(tg)
            y1h = [xtf[:, tg * 2048 + hh * 1024:tg * 2048 + (hh + 1) * 1024]
                   for hh in range(2)]
            by1 = [bXT[2 * tg], bXT[2 * tg + 1]]
            dma("act", xtf[:, tg * 2048:(tg + 1) * 2048], A["y1sp"][c_],
                reads=[bDR["y1sp"]] + by1, writes=by1, dsem="y1r", chain=False)
            for hh in range(2):
                Sc.add("dve", lambda e, o=acc_half(tg, hh), i1=y1h[hh]:
                       e.tensor_add(o, o, i1),
                       reads=babf + by1, writes=babf)
            Sc.add("dve", lambda e, o=stv(tg, 0), i=acc_half(tg, 0):
                   e.reduce_sum(o, i, axis=AX.X), reads=babf, writes=[bST])
            Sc.add("dve", lambda e, o=stv(tg, 4), i=acc_half(tg, 1):
                   e.reduce_sum(o, i, axis=AX.X), reads=babf, writes=[bST])
            for hh in range(2):
                Sc.add("act", lambda e, o=y1h[hh], i=acc_half(tg, hh):
                       e.activation(o, i, AF.Square),
                       reads=babf + by1, writes=by1)
            Sc.add("dve", lambda e, o=stv(tg, 1), i=y1h[0]:
                   e.reduce_sum(o, i, axis=AX.X), reads=by1, writes=[bST])
            Sc.add("dve", lambda e, o=stv(tg, 5), i=y1h[1]:
                   e.reduce_sum(o, i, axis=AX.X), reads=by1, writes=[bST])
            Sc.add("dve", lambda e, t=tg: e.tensor_add(stv(t, 0), stv(t, 0),
                                                       stv(t, 4)),
                   reads=[bST], writes=[bST])
            Sc.add("dve", lambda e, t=tg: e.tensor_add(stv(t, 1), stv(t, 1),
                                                       stv(t, 5)),
                   reads=[bST], writes=[bST])
            Sc.add("dve", lambda e, t=tg: e.tensor_scalar_mul(stv(t, 0),
                                                              stv(t, 0), 1.0 / H),
                   reads=[bST], writes=[bST])
            Sc.add("dve", lambda e, t=tg: e.tensor_scalar_mul(stv(t, 1),
                                                              stv(t, 1), 1.0 / H),
                   reads=[bST], writes=[bST])
            Sc.add("dve", lambda e, t=tg: e.tensor_mul(stv(t, 2), stv(t, 0),
                                                       stv(t, 0)),
                   reads=[bST], writes=[bST])
            Sc.add("dve", lambda e, t=tg: e.tensor_sub(stv(t, 1), stv(t, 1),
                                                       stv(t, 2)),
                   reads=[bST], writes=[bST])
            Sc.add("act", lambda e, t=tg: e.activation(stv(t, 2), stv(t, 1),
                                                       AF.Sqrt, bias=eps_col),
                   reads=[bST], writes=[bST])
            Sc.add("dve", lambda e, t=tg: e.reciprocal(stv(t, 3), stv(t, 2)),
                   reads=[bST], writes=[bST])
            for hh in range(2):
                Sc.add("dve", lambda e, o=acc_half(tg, hh), t=tg:
                       e.tensor_scalar(o, o, stv(t, 0), stv(t, 3),
                                       ALU.subtract, ALU.mult),
                       reads=babf + [bST], writes=babf)
                out_dmas.append(
                    dma("act", A["y"][c_ * C:(c_ + 1) * C,
                                      hh * 1024:(hh + 1) * 1024],
                        acc_half(tg, hh), reads=babf, writes=[], dsem="out"))

    Sc.add("sp", lambda e: e.nop(), deps=out_dmas)

    return finish()



def kernel(L0_attn_X, L0_attn_W_Q, L0_attn_W_K, L0_attn_W_V, L0_attn_W_G,
           L0_attn_W_O, L0_attn_gn_weight, L0_attn_gn_bias, L0_ffn_w_gate,
           L0_ffn_w_up, L0_ffn_w_down, L0_ffn_s_gate, L0_ffn_s_up, L0_ffn_s_down):
    X = np.asarray(L0_attn_X)
    cst = _build_host_constants()

    def cat(w):
        m = np.asarray(w).transpose(1, 0, 2).reshape(H, H).astype(np.float32)
        return np.ascontiguousarray(
            m.reshape(16, C, 16, D).transpose(2, 1, 0, 3))
    wq, wk, wv = cat(L0_attn_W_Q), cat(L0_attn_W_K), cat(L0_attn_W_V)
    wg = np.ascontiguousarray(np.asarray(L0_attn_W_G, np.float32)
                              .reshape(16, C, 16, D).transpose(2, 1, 0, 3))
    wo = np.ascontiguousarray(np.asarray(L0_attn_W_O, np.float32))
    gnw = np.ascontiguousarray(np.broadcast_to(
        np.asarray(L0_attn_gn_weight, np.float32), (C, H)))
    gnb = np.ascontiguousarray(np.broadcast_to(
        np.asarray(L0_attn_gn_bias, np.float32), (C, H)))
    wgt = np.ascontiguousarray(
        np.asarray(L0_ffn_w_gate).reshape(16, C, NTB, 2, 128)
        .transpose(2, 3, 1, 0, 4))
    wut = np.ascontiguousarray(
        np.asarray(L0_ffn_w_up).reshape(16, C, NTB, 2, 128)
        .transpose(2, 3, 1, 0, 4))
    wdt = np.ascontiguousarray(
        np.asarray(L0_ffn_w_down).reshape(16, 4, C, 2, 1024)
        .transpose(3, 0, 2, 1, 4))
    sgate = np.ascontiguousarray(
        np.asarray(L0_ffn_s_gate, np.float32).reshape(INTER // C, C).T)
    sup = np.ascontiguousarray(
        np.asarray(L0_ffn_s_up, np.float32).reshape(INTER // C, C).T)
    sdown = np.ascontiguousarray(np.broadcast_to(
        np.asarray(L0_ffn_s_down, np.float32), (C, H)))

    shared = dict(wq=wq, wk=wk, wv=wv, wg=wg, wo=wo, gnw=gnw, gnb=gnb,
                  cq=cst["cq"], sq=cst["sq"], ck=cst["ck"], sk=cst["sk"],
                  gq=cst["gq"], gk=cst["gk"], causal=cst["causal"],
                  ident=cst["ident"], wgt=wgt, wut=wut, wdt=wdt,
                  sgate=sgate, sup=sup, sdown=sdown)

    in_maps = []
    for core in range(N_CORES):
        bq, qq = core // NQ, core % NQ
        m = dict(shared)
        m["x"] = np.ascontiguousarray(X[bq, qq * T:(qq + 1) * T, :].astype(np.float32))
        m["that"] = cst["that_by_core"][core]
        in_maps.append(m)

    nc = build_program()
    res = run_bass_kernel_spmd(nc, in_maps, core_ids=list(range(N_CORES)))

    out = np.empty((B, S, H), np.float32)
    for core in range(N_CORES):
        bq, qq = core // NQ, core % NQ
        out[bq, qq * T:(qq + 1) * T, :] = res.results[core]["y"]
    return out


# version 15
# speedup vs baseline: 1.0701x; 1.0180x over previous
import numpy as np
import ml_dtypes
import concourse.bass as bass
import concourse.mybir as mybir
from concourse.bass_utils import run_bass_kernel_spmd

F32 = mybir.dt.float32
BF16 = mybir.dt.bfloat16
I8 = mybir.dt.int8
AF = mybir.ActivationFunctionType
ALU = mybir.AluOpType
AX = mybir.AxisListType
bfdt = ml_dtypes.bfloat16

B, S, H = 2, 8192, 2048
NH, D = 16, 128
T = 2048
NCH = 16
C = 128
INTER = 8192
SCALE_BASE = 512.0
GN_EPS = 1e-5
LN_EPS = 1e-5
N_CORES = 8
NQ = 4
NTB = 32
NIB = 256
NSTRIP = 4
STRIP = 512
DBGH = 8
DBG4 = False
SERIALIZE = False



def _xpos_tables_f64(d, positions):
    half = d // 2
    xscale = (np.arange(0, d, 2, dtype=np.float64) + 0.4 * d) / (1.4 * d)
    pos = positions.astype(np.float64)
    scale = xscale[None, :] ** (pos[:, None] / SCALE_BASE)
    inv_freq = 1.0 / 10000.0 ** (np.arange(half, dtype=np.float64) / half)
    sinu = pos[:, None] * inv_freq[None, :]
    sin, cos = np.sin(sinu), np.cos(sinu)
    dup = lambda t: np.repeat(t, 2, axis=-1)
    return dup(cos * scale), dup(sin * scale), dup(cos / scale), dup(sin / scale)


def _build_host_constants():
    gammas = 1.0 - np.exp(np.linspace(np.log(1 / 32), np.log(1 / 512), NH))
    p = np.arange(T)
    cq, sq, ck, sk = _xpos_tables_f64(D, p)
    sgn = np.ones(D)
    sgn[0::2] = -1.0
    sq = sq * sgn[None, :]
    sk = sk * sgn[None, :]

    def to_col(tab):
        return np.ascontiguousarray(
            tab.reshape(NCH, C, D).transpose(1, 0, 2).reshape(C, NCH * D))

    cq_t = to_col(cq).astype(bfdt)
    sq_t = to_col(sq).astype(bfdt)
    ck_t = to_col(ck).astype(bfdt)
    sk_t = to_col(sk).astype(bfdt)

    pc = (np.arange(NCH)[None, :] * C + np.arange(C)[:, None] + 1.0)
    gq = np.empty((C, NH * NCH), np.float64)
    gk = np.empty((C, NH * NCH), np.float64)
    for n in range(NH):
        gq[:, n * NCH:(n + 1) * NCH] = gammas[n] ** pc
        gk[:, n * NCH:(n + 1) * NCH] = gammas[n] ** (-pc)
    gq = gq.astype(np.float32)
    gk = gk.astype(np.float32)

    causal = (np.arange(C)[None, :] >= np.arange(C)[:, None]).astype(np.float64)
    causal_rep = np.ascontiguousarray(causal.astype(bfdt))
    ident = np.ascontiguousarray(np.eye(C, dtype=bfdt))

    half = D // 2
    xscale = (np.arange(0, D, 2, dtype=np.float64) + 0.4 * D) / (1.4 * D)
    inv_freq = 1.0 / 10000.0 ** (np.arange(half, dtype=np.float64) / half)

    def that_lhsT(D_off, n):
        sig = xscale ** (D_off / SCALE_BASE)
        th = D_off * inv_freq
        g = float(np.float32(gammas[n] ** D_off))
        M = np.zeros((D, D))
        for i in range(half):
            cth, sth = np.cos(th[i]), np.sin(th[i])
            M[2 * i, 2 * i] = cth * sig[i] * g
            M[2 * i, 2 * i + 1] = sth * sig[i] * g
            M[2 * i + 1, 2 * i] = -sth * sig[i] * g
            M[2 * i + 1, 2 * i + 1] = cth * sig[i] * g
        return M.T

    that_by_core = []
    for core in range(N_CORES):
        bq, qq = core // NQ, core % NQ
        that = np.zeros((N_CORES, NH, D, D), np.float64)
        for r in range(N_CORES):
            rb, rq = r // NQ, r % NQ
            if rb == bq and rq < qq:
                for n in range(NH):
                    that[r, n] = that_lhsT(T * (qq - rq), n)
        that_by_core.append(np.ascontiguousarray(that.astype(bfdt)))

    return dict(cq=cq_t, sq=sq_t, ck=ck_t, sk=sk_t, gq=gq, gk=gk,
                causal=causal_rep, ident=ident, that_by_core=that_by_core)



class Buf:
    __slots__ = ("writer", "readers", "sem", "group", "writers")

    def __init__(self, sem=None, group=False):
        self.writer = None
        self.writers = []
        self.readers = []
        self.sem = sem
        self.group = group


class Sched:
    ENGINES = ("te", "act", "dve", "pool", "sp")

    def __init__(self):
        self.ops = []
        self.last_dma = {}

    def add(self, eng, fn, reads=(), writes=(), deps=(), dma=False, dsem=None,
            chain=True):
        idx = len(self.ops)
        dep = set(deps)
        if SERIALIZE and idx > 0:
            dep.add(idx - 1)
        for b in reads:
            if b.group:
                dep.update(b.writers)
            elif b.writer is not None:
                dep.add(b.writer)
        for b in writes:
            if b.group:
                dep.update(b.readers)
                b.writers.append(idx)
            else:
                if b.writer is not None:
                    dep.add(b.writer)
                dep.update(b.readers)
                b.writer = idx
                b.readers = []
        dep.discard(idx)
        for d in dep:
            assert d < idx
        if dma:
            if dsem is None and writes:
                dsem = writes[0].sem
            assert dsem is not None, "dma op needs a dsem"
            if chain and dsem in self.last_dma:
                dep.add(self.last_dma[dsem])
            self.last_dma[dsem] = idx
        self.ops.append(dict(i=idx, eng=eng, fn=fn, deps=sorted(dep), dma=dma,
                             dsem=dsem))
        for b in reads:
            b.readers.append(idx)
        for b in writes:
            if not b.group:
                pass
        return idx

    def finalize(self):
        needed = set()
        for op in self.ops:
            needed.update(op["deps"])
        counters = {}
        pos = {}
        for op in self.ops:
            op["needed"] = op["i"] in needed or op["dma"]
            if op["needed"]:
                key = ("d", op["dsem"]) if op["dma"] else ("c", op["eng"])
                inc = 16 if op["dma"] else 1
                counters[key] = counters.get(key, 0) + inc
                pos[op["i"]] = (key, counters[key])
                op["inc"] = inc
        clocks = {e: {} for e in self.ENGINES}
        for op in self.ops:
            wmap = {}
            for d in op["deps"]:
                dop = self.ops[d]
                key, tgt = pos[d]
                if (not dop["dma"]) and dop["eng"] == op["eng"] == "te":
                    continue
                if wmap.get(key, 0) < tgt:
                    wmap[key] = tgt
            clk = clocks[op["eng"]]
            ww = []
            for key in sorted(wmap):
                tgt = wmap[key]
                if clk.get(key, 0) >= tgt:
                    continue
                clk[key] = tgt
                ww.append((key, tgt))
            op["waits"] = ww

    def sem_keys(self):
        ks = set()
        for op in self.ops:
            if op.get("needed"):
                ks.add(("d", op["dsem"]) if op["dma"] else ("c", op["eng"]))
        return sorted(ks)

    def emit_engine(self, eng_name, eng_obj, sems):
        for op in self.ops:
            if op["eng"] != eng_name:
                continue
            for key, tgt in op["waits"]:
                eng_obj.wait_ge(sems[key], tgt)
            inst = op["fn"](eng_obj)
            if op["needed"]:
                assert inst is not None, f"op {op['i']} on {eng_name} returned None"
                key = ("d", op["dsem"]) if op["dma"] else ("c", op["eng"])
                inst.then_inc(sems[key], op["inc"])



def build_program(stop_after=None):
    nc = bass.Bass("TRN2", target_bir_lowering=False)

    dr = {}
    ei = "ExternalInput"
    dr["x"] = nc.dram_tensor("x", [T, H], F32, kind=ei)
    for nm in ("wq", "wk", "wv", "wg"):
        dr[nm] = nc.dram_tensor(nm, [16, C, 16, D], F32, kind=ei)
    dr["wo"] = nc.dram_tensor("wo", [H, H], F32, kind=ei)
    dr["gnw"] = nc.dram_tensor("gnw", [C, H], F32, kind=ei)
    dr["gnb"] = nc.dram_tensor("gnb", [C, H], F32, kind=ei)
    for nm in ("cq", "sq", "ck", "sk"):
        dr[nm] = nc.dram_tensor(nm, [C, T], BF16, kind=ei)
    dr["causal"] = nc.dram_tensor("causal", [C, C], BF16, kind=ei)
    dr["gq"] = nc.dram_tensor("gq", [C, NH * NCH], F32, kind=ei)
    dr["gk"] = nc.dram_tensor("gk", [C, NH * NCH], F32, kind=ei)
    dr["ident"] = nc.dram_tensor("ident", [C, C], BF16, kind=ei)
    dr["that"] = nc.dram_tensor("that", [N_CORES, NH, D, D], BF16, kind=ei)
    dr["wgt"] = nc.dram_tensor("wgt", [NTB, 2, C, 16, 128], I8, kind=ei)
    dr["wut"] = nc.dram_tensor("wut", [NTB, 2, C, 16, 128], I8, kind=ei)
    dr["wdt"] = nc.dram_tensor("wdt", [2, 16, C, 4, 1024], I8, kind=ei)
    dr["sgate"] = nc.dram_tensor("sgate", [C, INTER // C], F32, kind=ei)
    dr["sup"] = nc.dram_tensor("sup", [C, INTER // C], F32, kind=ei)
    dr["sdown"] = nc.dram_tensor("sdown", [C, H], F32, kind=ei)
    dr["y"] = nc.dram_tensor("y", [T, H], F32, kind="ExternalOutput")
    dr["vqg"] = nc.dram_tensor("vqg", [NH, 3, C, T], BF16, kind="Internal")
    dr["ztsp"] = nc.dram_tensor("ztsp", [NH, C, T], BF16, kind="Internal")
    dr["y1sp"] = nc.dram_tensor("y1sp", [NCH, C, H], F32, kind="Internal")
    dr["ccin"] = nc.dram_tensor("ccin", [C, NH * D + 0], BF16, kind="Internal")
    dr["ccout"] = nc.dram_tensor("ccout", [N_CORES * C, NH * D], BF16, kind="Internal")

    A = {k: v.ap() for k, v in dr.items()}

    XT = nc.alloc_sbuf_tensor("XTa", [128, 16 * T], BF16)
    KT = nc.alloc_sbuf_tensor("KTa", [128, 16 * T], BF16)
    COLS = nc.alloc_sbuf_tensor("COLSa", [128, 8 * T], BF16)
    F32A = nc.alloc_sbuf_tensor("F32Aa", [128, 3 * H], F32)
    TAB = nc.alloc_sbuf_tensor("TABa", [128, 2 * T], BF16)
    CAU = nc.alloc_sbuf_tensor("CAUa", [128, C], BF16)
    GNBB = nc.alloc_sbuf_tensor("GNBBa", [128, H], BF16)
    GAM = nc.alloc_sbuf_tensor("GAMa", [128, 2 * NH * NCH], F32)
    IDN = nc.alloc_sbuf_tensor("IDNa", [128, C], BF16)
    GUB = nc.alloc_sbuf_tensor("GUBa", [128, 2 * STRIP], BF16)
    ST = nc.alloc_sbuf_tensor("STa", [128, 180], F32)
    SNB = nc.alloc_sbuf_tensor("SNBa", [128, T], BF16)
    SGC = nc.alloc_sbuf_tensor("SGCa", [128, 2 * (INTER // C)], F32)

    xt = XT.ap(); kt = KT.ap(); cols = COLS.ap(); f32a = F32A.ap()
    tab = TAB.ap(); cau = CAU.ap(); gnbb = GNBB.ap(); gam = GAM.ap()
    idn = IDN.ap(); gub = GUB.ap(); st = ST.ap(); snb = SNB.ap(); sgc = SGC.ap()
    tabf = tab.bitcast(F32)

    PA = nc.alloc_psum_tensor("PAa", [128, 2048], F32)
    PB = nc.alloc_psum_tensor("PBa", [128, 2048], F32)
    pa = PA.ap(); pb = PB.ap()
    pbb = pb.bitcast(BF16)

    Sc = Sched()
    bXT = [Buf(sem="xt") for _ in range(16)]
    bKT = [Buf() for _ in range(16)]
    bCOL = [Buf(sem=f"cols{i}") for i in range(8)]
    bF32 = [Buf(sem=f"f32{i}") for i in range(3)]
    bTAB = Buf(sem="tab"); bCAU = Buf(sem="cau"); bGNB = Buf(sem="gnb")
    bGAM = Buf(sem="gam"); bIDN = Buf(sem="idn")
    bGUB = [Buf(), Buf()]; bST = Buf(); bSNB = Buf(sem="snb"); bSGC = Buf(sem="sgc")
    bKTT = [Buf() for _ in range(16)]
    bBank = [Buf() for _ in range(8)]
    bDR = {k: Buf(sem=k, group=True)
           for k in ("vqg", "ztsp", "y1sp")}
    bDR["ccin"] = Buf(sem="ccin")
    bDR["ccout"] = Buf()

    banksA = [bBank[i] for i in range(4)]
    banksB = [bBank[i] for i in range(4, 8)]
    bTRhi = [bBank[6], bBank[7]]
    bTRlo = [bBank[4], bBank[5]]

    cs = lambda i: cols[:, i * T:(i + 1) * T]
    xts = lambda i: xt[:, i * T:(i + 1) * T]
    pab = pa.bitcast(BF16)
    fs = lambda i: f32a[:, i * H:(i + 1) * H]
    sti = lambda i: st[:, i * 20:i * 20 + 16]
    st1 = lambda i: st[:, i * 20:i * 20 + 1]

    eps_col = st[:, 176:177]
    Sc_eps_done = []

    def dma(engq, out, in_, reads=(), writes=(), deps=(), dsem=None, chain=True):
        return Sc.add(engq, lambda e, o=out, i=in_: e.dma_start(out=o, in_=i),
                      reads=reads, writes=writes, deps=deps, dma=True, dsem=dsem,
                      chain=chain)

    def finish():
        from contextlib import ExitStack
        Sc.finalize()
        with ExitStack() as stack:
            sems = {}
            for key in Sc.sem_keys():
                nm = ("cs_" + key[1]) if key[0] == "c" else ("ds_" + str(key[1]))
                sems[key] = stack.enter_context(nc.semaphore(nm))
            block = stack.enter_context(nc.Block())

            @block.tensor
            def _(eng):
                Sc.emit_engine("te", eng, sems)

            @block.scalar
            def _(eng):
                Sc.emit_engine("act", eng, sems)

            @block.vector
            def _(eng):
                Sc.emit_engine("dve", eng, sems)

            @block.gpsimd
            def _(eng):
                Sc.emit_engine("pool", eng, sems)

            @block.sync
            def _(eng):
                Sc.emit_engine("sp", eng, sems)
        return nc

    def dbg_sbuf(nm, ap, bufs, dtype=BF16, width=16 * T):
        dt_ = nc.dram_tensor(nm, [128, width], dtype, kind="ExternalOutput")
        dd = dma("sp", dt_.ap()[:, :], ap, reads=bufs, writes=[], dsem="out")
        Sc.add("sp", lambda e: e.nop(), deps=[dd])

    def dbg_dram(nm, key, shape, dtype):
        dt_ = nc.dram_tensor(nm, shape, dtype, kind="ExternalOutput")
        idx = tuple(slice(None) for _ in shape)
        dd = dma("sp", dt_.ap()[idx], A[key][idx], reads=[bDR[key]], writes=[], dsem="out")
        Sc.add("sp", lambda e: e.nop(), deps=[dd])

    Sc.add("dve", lambda e: e.memset(eps_col, 1e-5), writes=[bST])
    dma("sp", cau[:, :], A["causal"][:, :], writes=[bCAU])
    dma("sp", idn[:, :], A["ident"][:, :], writes=[bIDN])
    dma("sp", gam[:, 0:NH * NCH], A["gq"][:, :], writes=[bGAM])
    dma("sp", gam[:, NH * NCH:2 * NH * NCH], A["gk"][:, :], writes=[bGAM])
    dma("sp", tab[:, 0:T], A["ck"][:, :], writes=[bTAB])
    dma("sp", tab[:, T:2 * T], A["sk"][:, :], writes=[bTAB])

    for c_ in range(NCH):
        pp = c_ % 2
        xst = fs(pp)
        dma("sp", xst, A["x"][c_ * C:(c_ + 1) * C, :], reads=[bF32[pp]],
            writes=[bF32[pp]], chain=False)
        xb = cs(pp)
        Sc.add("act", lambda e, o=xb, i=xst: e.copy(o, i),
               reads=[bF32[pp]], writes=[bCOL[pp]])

        def mk_tp(src=xb):
            def f(e):
                last = None
                for ht in range(16):
                    last = nc.tensor.matmul(
                        pbb[:, 2048 + ht * 128:2048 + (ht + 1) * 128],
                        src[:, ht * 128:(ht + 1) * 128], idn[:, :],
                        start=(ht % 8 == 0), stop=(ht % 8 == 7),
                        is_transpose=True, skip_group_check=True)
                return last
            return f
        Sc.add("te", mk_tp(), reads=[bCOL[pp], bIDN], writes=bTRhi)
        dst = xt.rearrange("p (ht t) -> p ht t", ht=16)[:, :, c_ * C:(c_ + 1) * C]
        src = pbb[:, 2048:4096].rearrange("p (ht a) -> p ht a", ht=16)
        Sc.add("act", lambda e, o=dst, i=src: e.copy(o, i),
               reads=bTRhi, writes=list(bXT))

    if stop_after == "p0":
        dbg_sbuf("d_xt", xt[:, :], list(bXT))
        return finish()

    def proj_tile(n, wap, wbuf, wapb, out_ap, out_buf):
        dma("pool", wapb.rearrange("p (ht d) -> p ht d", ht=16),
            wap[n], reads=[wbuf], writes=[wbuf], chain=False)
        for half in range(2):
            regp = (pb, pa)[half]
            rbufs = ([bBank[4], bBank[5]], [bBank[0], bBank[1]])[half]
            def mk_mm(half=half, wapb=wapb, regp=regp):
                def f(e):
                    last = None
                    for ht in range(16):
                        for j in range(2):
                            off = ht * T + half * 1024 + j * 512
                            last = nc.tensor.matmul(
                                regp[:, j * 512:(j + 1) * 512],
                                wapb[:, ht * D:(ht + 1) * D],
                                xt[:, off:off + 512],
                                start=(ht == 0), stop=(ht == 15),
                                skip_group_check=True)
                    return last
                return f
            Sc.add("te", mk_mm(), reads=[wbuf] + list(bXT), writes=rbufs)
            Sc.add("act",
                   lambda e, o=out_ap[:, half * 1024:(half + 1) * 1024],
                   i=regp[:, 0:1024]: e.copy(o, i),
                   reads=rbufs, writes=[out_buf])

    def transpose16(src_ap, src_buf, dst_ap, dst_bufs, region="hi"):
        reg_off = 2048 if region == "hi" else 0
        reg_bufs = bTRhi if region == "hi" else bTRlo

        def f(e, src_ap=src_ap, reg_off=reg_off):
            last = None
            for c_ in range(16):
                last = nc.tensor.matmul(
                    pbb[:, reg_off + c_ * 128:reg_off + (c_ + 1) * 128],
                    src_ap[:, c_ * 128:(c_ + 1) * 128], idn[:, :],
                    start=(c_ % 8 == 0), stop=(c_ % 8 == 7),
                    is_transpose=True, skip_group_check=True)
            return last
        Sc.add("te", f, reads=[src_buf, bIDN], writes=reg_bufs)
        Sc.add("act",
               lambda e, o=dst_ap, i=pbb[:, reg_off:reg_off + 2048]: e.copy(o, i),
               reads=reg_bufs, writes=list(dst_bufs))

    def xpos_apply(col_ap, col_buf, gam_off, out_ap, out_bufs):
        tmp = fs(0)
        sw = col_ap.rearrange("p (c k two) -> p c k two", c=NCH, two=2)[:, :, :, ::-1]
        stv = tab[:, T:2 * T].rearrange("p (c k two) -> p c k two", c=NCH, two=2)
        Sc.add("dve", lambda e, o=tmp, i0=sw, i1=stv:
               e.tensor_tensor(o.rearrange("p (c k two) -> p c k two", c=NCH, two=2),
                               i0, i1, ALU.mult),
               reads=[col_buf, bTAB], writes=[bF32[0]])
        Sc.add("dve", lambda e, o=col_ap, i1=tab[:, 0:T]:
               e.tensor_tensor(o, o, i1, ALU.mult),
               reads=[col_buf, bTAB], writes=[col_buf])
        Sc.add("dve", lambda e, o=col_ap, i1=tmp:
               e.tensor_tensor(o, o, i1, ALU.add),
               reads=[col_buf, bF32[0]], writes=[col_buf])
        gview = gam[:, gam_off:gam_off + NCH].unsqueeze(2).broadcast_to([128, NCH, D])
        Sc.add("dve", lambda e, o=out_ap, i0=col_ap, i1=gview:
               e.tensor_tensor(o.rearrange("p (c d) -> p c d", c=NCH),
                               i0.rearrange("p (c d) -> p c d", c=NCH),
                               i1, ALU.mult),
               reads=[col_buf, bGAM], writes=list(out_bufs))

    for n in range(NH):
        wkb, wka = bCOL[3 + (n % 2)], cs(3 + (n % 2))
        wvb, wva = bCOL[5 + (n % 2)], cs(5 + (n % 2))
        proj_tile(n, A["wk"], wkb, wka, cs(0), bCOL[0])
        transpose16(cs(0), bCOL[0], kt[:, n * T:(n + 1) * T], [bKT[n]])
        xpos_apply(kt[:, n * T:(n + 1) * T], bKT[n], NH * NCH + n * NCH,
                   kt[:, n * T:(n + 1) * T], [bKT[n]])
        proj_tile(n, A["wv"], wvb, wva, cs(1), bCOL[1])
        transpose16(cs(1), bCOL[1], cs(2), [bCOL[2]])
        dma("sp", A["vqg"][n, 0], cs(2), reads=[bCOL[2]], writes=[bDR["vqg"]])

        stp = pa[:, 1024 + (n % 8) * D:1024 + (n % 8 + 1) * D]
        stpb = bBank[2 + (n % 8) // 4]

        def mk_state(n=n, stp=stp):
            def f(e):
                last = None
                for c_ in range(16):
                    last = nc.tensor.matmul(
                        stp,
                        kt[:, n * T + c_ * 128:n * T + (c_ + 1) * 128],
                        cs(2)[:, c_ * 128:(c_ + 1) * 128],
                        start=(c_ == 0), stop=(c_ == 15),
                        skip_group_check=True)
                return last
            return f
        Sc.add("te", mk_state(), reads=[bKT[n], bCOL[2]], writes=[stpb])
        Sc.add("dve", lambda e, o=fs(2)[:, n * D:(n + 1) * D], i=stp:
               e.tensor_copy(o, i),
               reads=[stpb, bF32[2]], writes=[bF32[2]])

    Sc.add("act", lambda e, o=cs(7), i=fs(2): e.copy(o, i),
           reads=[bF32[2]], writes=[bCOL[7]])
    dma("sp", A["ccin"][:, :], cs(7), reads=[bCOL[7]], writes=[bDR["ccin"]])
    Sc.add("pool", lambda e: e.collective_compute(
        "AllGather", ALU.bypass, [list(range(N_CORES))],
        [A["ccin"][:, :]], [A["ccout"][:, :]]),
        reads=[bDR["ccin"]], writes=[bDR["ccout"]])

    if stop_after == "p1":
        dbg_sbuf("d_kt", kt[:, :], list(bKT))
        dbg_dram("d_vqg", "vqg", [NH, 3, C, T], BF16)
        dbg_dram("d_ccin", "ccin", [C, NH * D + 0], BF16)
        return finish()

    dma("sp", tab[:, 0:T], A["cq"][:, :], reads=[bTAB], writes=[bTAB])
    dma("sp", tab[:, T:2 * T], A["sq"][:, :], reads=[bTAB], writes=[bTAB])
    for n in range(NH):
        wqb, wqa = bCOL[3 + (n % 2)], cs(3 + (n % 2))
        wgb, wga = bCOL[5 + (n % 2)], cs(5 + (n % 2))
        proj_tile(n, A["wq"], wqb, wqa, cs(0), bCOL[0])
        transpose16(cs(0), bCOL[0], cs(1), [bCOL[1]])
        xpos_apply(cs(1), bCOL[1], n * NCH, cs(1), [bCOL[1]])
        dma("sp", A["vqg"][n, 1], cs(1), reads=[bCOL[1]], writes=[bDR["vqg"]])
        proj_tile(n, A["wg"], wgb, wga, cs(0), bCOL[0])
        transpose16(cs(0), bCOL[0], cs(2), [bCOL[2]])
        Sc.add("act", lambda e, o=cs(2): e.activation(o, o, AF.Silu),
               reads=[bCOL[2]], writes=[bCOL[2]])
        dma("sp", A["vqg"][n, 2], cs(2), reads=[bCOL[2]], writes=[bDR["vqg"]])

    for r in range(N_CORES):
        dma("sp", cs(r), A["ccout"][r * C:(r + 1) * C, :],
            reads=[bDR["ccout"], bCOL[r]], writes=[bCOL[r]], dsem="cco")
    for r in range(N_CORES):
        dma("sp", snb[:, :].rearrange("p (n d) -> p n d", n=NH),
            A["that"][r].rearrange("n p d -> p n d"),
            reads=[bSNB], writes=[bSNB], chain=False)

        def mk_that(r=r):
            def f(e):
                last = None
                for n in range(NH):
                    last = nc.tensor.matmul(
                        pa[:, n * D:(n + 1) * D],
                        snb[:, n * D:(n + 1) * D],
                        cs(r)[:, n * D:(n + 1) * D],
                        start=(r == 0 and n % 4 == 0),
                        stop=(r == N_CORES - 1 and n % 4 == 3),
                        skip_group_check=True)
                return last
            return f
        Sc.add("te", mk_that(), reads=[bSNB, bCOL[r]], writes=banksA)
    Sc.add("act", lambda e, o=xts(12), i=pa[:, :]: e.copy(o, i),
           reads=banksA + [bXT[12]], writes=[bXT[12]])

    if stop_after == "exch":
        dbg_sbuf("d_sin", xts(12), [bXT[12]], dtype=BF16, width=T)
        dbg_dram("d_vqg", "vqg", [NH, 3, C, T], BF16)
        return finish()

    dma("sp", tabf[:, :], A["gnw"][:, :], reads=[bTAB], writes=[bTAB])
    dma("pool", gnbb[:, :], A["gnb"][:, :], writes=[bGNB])
    bSTp = [Buf(), Buf()]

    for n in range(NH):
        P_ = n % 2
        reg = (pa, pb)[P_]
        regb = (pab, pbb)[P_]
        reg3 = reg[:, :].rearrange("p (c d) -> p c d", c=NCH)
        rbanks = [bBank[P_ * 4 + i] for i in range(4)]
        rb01 = rbanks[0:2]
        sb0 = P_ * 6
        ktT, qtT, atm = xts(sb0), xts(sb0 + 1), xts(sb0 + 2)
        zc, ztc, dsB = xts(sb0 + 3), xts(sb0 + 4), xts(sb0 + 5)
        bktT, bqtT, batm = bXT[sb0], bXT[sb0 + 1], bXT[sb0 + 2]
        bzc, bztc, bdsB = bXT[sb0 + 3], bXT[sb0 + 4], bXT[sb0 + 5]
        snbP, bsnbP = xts(13 + P_), bXT[13 + P_]
        vc, qc, gc = cs(3 * P_), cs(3 * P_ + 1), cs(3 * P_ + 2)
        bvqg = [bCOL[3 * P_], bCOL[3 * P_ + 1], bCOL[3 * P_ + 2]]
        syF, bsyF = fs(1 + P_), bF32[1 + P_]
        so = P_ * 4
        stS = lambda i: st[:, (so + i) * 20:(so + i) * 20 + 16]
        bst = bSTp[P_]

        for k, q_ in ((0, "sp"), (1, "act"), (2, "pool")):
            dma(q_, cs(3 * P_ + k), A["vqg"][n, k],
                reads=[bDR["vqg"], bvqg[k]], writes=[bvqg[k]],
                dsem=f"vqg{P_}{k}", chain=False)

        def tp16(src_ap, src_buf, dst_ap, dst_buf, regb=regb, rb01=rb01):
            def f(e, src_ap=src_ap, regb=regb):
                last = None
                for c_ in range(16):
                    last = nc.tensor.matmul(
                        regb[:, c_ * 128:(c_ + 1) * 128],
                        src_ap[:, c_ * 128:(c_ + 1) * 128], idn[:, :],
                        start=(c_ % 8 == 0), stop=(c_ % 8 == 7),
                        is_transpose=True, skip_group_check=True)
                return last
            Sc.add("te", f, reads=[src_buf, bIDN], writes=rb01)
            Sc.add("act", lambda e, o=dst_ap, i=regb[:, 0:2048]: e.copy(o, i),
                   reads=rb01, writes=[dst_buf])

        tp16(kt[:, n * T:(n + 1) * T], bKT[n], ktT, bktT)
        tp16(qc, bvqg[1], qtT, bqtT)

        def mk_amm(ktT=ktT, qtT=qtT, reg=reg):
            def f(e):
                last = None
                for c_ in range(16):
                    last = nc.tensor.matmul(
                        reg[:, c_ * 128:(c_ + 1) * 128],
                        ktT[:, c_ * 128:(c_ + 1) * 128],
                        qtT[:, c_ * 128:(c_ + 1) * 128],
                        start=(c_ % 4 == 0), stop=(c_ % 4 == 3),
                        skip_group_check=True)
                return last
            return f
        Sc.add("te", mk_amm(), reads=[bktT, bqtT], writes=rbanks)
        cau_b = cau[:, :].unsqueeze(1).broadcast_to([128, NCH, C])
        Sc.add("dve", lambda e, o=atm, i0=reg3, i1=cau_b: e.tensor_tensor(
            o.rearrange("p (c a) -> p c a", c=NCH), i0, i1, ALU.mult),
               reads=rbanks + [bCAU], writes=[batm])

        def mk_ds(n=n, vc=vc, reg=reg):
            def f(e):
                last = None
                for c_ in range(16):
                    last = nc.tensor.matmul(
                        reg[:, c_ * 128:(c_ + 1) * 128],
                        kt[:, n * T + c_ * 128:n * T + (c_ + 1) * 128],
                        vc[:, c_ * 128:(c_ + 1) * 128],
                        start=(c_ % 4 == 0), stop=(c_ % 4 == 3),
                        skip_group_check=True)
                return last
            return f
        Sc.add("te", mk_ds(), reads=[bKT[n], bvqg[0]], writes=rbanks)
        Sc.add("act", lambda e, o=dsB, i=reg[:, :]: e.copy(o, i),
               reads=rbanks, writes=[bdsB])
        Sc.add("dve", lambda e, o=reg[:, 0:128], i=xts(12)[:, n * D:(n + 1) * D]:
               e.tensor_copy(o, i),
               reads=[bXT[12], bdsB] + rbanks, writes=[rbanks[0]])
        for c_ in range(15):
            Sc.add("dve", lambda e, c_=c_, reg=reg, dsB=dsB: e.tensor_add(
                reg[:, (c_ + 1) * 128:(c_ + 2) * 128],
                reg[:, c_ * 128:(c_ + 1) * 128],
                dsB[:, c_ * 128:(c_ + 1) * 128]),
                reads=[bdsB, rbanks[c_ // 4], rbanks[(c_ + 1) // 4]],
                writes=[rbanks[(c_ + 1) // 4]])
        Sc.add("act", lambda e, o=snbP, i=reg[:, :]: e.copy(o, i),
               reads=rbanks, writes=[bsnbP])

        def mk_y(atm=atm, vc=vc, qtT=qtT, snbP=snbP, reg=reg):
            def f(e):
                last = None
                for c_ in range(16):
                    nc.tensor.matmul(
                        reg[:, c_ * 128:(c_ + 1) * 128],
                        atm[:, c_ * 128:(c_ + 1) * 128],
                        vc[:, c_ * 128:(c_ + 1) * 128],
                        start=(c_ % 4 == 0), stop=False, skip_group_check=True)
                    last = nc.tensor.matmul(
                        reg[:, c_ * 128:(c_ + 1) * 128],
                        qtT[:, c_ * 128:(c_ + 1) * 128],
                        snbP[:, c_ * 128:(c_ + 1) * 128],
                        start=False, stop=(c_ % 4 == 3), skip_group_check=True)
                return last
            return f
        Sc.add("te", mk_y(), reads=[batm, bvqg[0], bqtT, bsnbP], writes=rbanks)
        Sc.add("act", lambda e, o=syF, i=reg[:, :]: e.copy(o, i),
               reads=rbanks, writes=[bsyF])

        sy3 = syF.rearrange("p (c d) -> p c d", c=NCH)
        Sc.add("dve", lambda e, o=stS(0), i=sy3: e.reduce_sum(o, i, axis=AX.X),
               reads=[bsyF], writes=[bst])
        Sc.add("act", lambda e, o=reg[:, :], i=syF: e.activation(o, i, AF.Square),
               reads=[bsyF] + rbanks, writes=rbanks)
        Sc.add("dve", lambda e, o=stS(1), i=reg3: e.reduce_sum(o, i, axis=AX.X),
               reads=rbanks, writes=[bst])
        Sc.add("dve", lambda e, a=stS(0): e.tensor_scalar_mul(a, a, 1.0 / D),
               reads=[bst], writes=[bst])
        Sc.add("dve", lambda e, a=stS(1): e.tensor_scalar_mul(a, a, 1.0 / D),
               reads=[bst], writes=[bst])
        Sc.add("dve", lambda e, a=stS(2), b=stS(0): e.tensor_mul(a, b, b),
               reads=[bst], writes=[bst])
        Sc.add("dve", lambda e, a=stS(1), b=stS(2): e.tensor_sub(a, a, b),
               reads=[bst], writes=[bst])
        Sc.add("act", lambda e, a=stS(2), b=stS(1): e.activation(a, b, AF.Sqrt,
                                                                 bias=eps_col),
               reads=[bst], writes=[bst])
        Sc.add("dve", lambda e, a=stS(3), b=stS(2): e.reciprocal(a, b),
               reads=[bst], writes=[bst])
        mu_b = stS(0).unsqueeze(2).broadcast_to([128, NCH, D])
        rs_b = stS(3).unsqueeze(2).broadcast_to([128, NCH, D])
        gnw_b = tabf[:, n * D:(n + 1) * D].unsqueeze(1).broadcast_to([128, NCH, D])
        gnb_b = gnbb[:, n * D:(n + 1) * D].unsqueeze(1).broadcast_to([128, NCH, D])
        Sc.add("dve", lambda e, o=reg3, i0=sy3, i1=mu_b:
               e.tensor_tensor(o, i0, i1, ALU.subtract),
               reads=[bsyF, bst] + rbanks, writes=rbanks)
        Sc.add("dve", lambda e, o=reg3, i1=rs_b: e.tensor_tensor(o, o, i1,
                                                                 ALU.mult),
               reads=[bst] + rbanks, writes=rbanks)
        Sc.add("dve", lambda e, o=reg3, i1=gnw_b: e.tensor_tensor(o, o, i1,
                                                                  ALU.mult),
               reads=[bTAB] + rbanks, writes=rbanks)
        Sc.add("dve", lambda e, o=reg3, i1=gnb_b: e.tensor_tensor(o, o, i1,
                                                                  ALU.add),
               reads=[bGNB] + rbanks, writes=rbanks)
        gc3 = gc.rearrange("p (c d) -> p c d", c=NCH)
        Sc.add("dve", lambda e, o=zc, i0=reg3, i1=gc3: e.tensor_tensor(
            o.rearrange("p (c d) -> p c d", c=NCH), i0, i1, ALU.mult),
               reads=[bvqg[2]] + rbanks, writes=[bzc])
        tp16(zc, bzc, ztc, bztc)
        dma("sp", A["ztsp"][n], ztc, reads=[bztc], writes=[bDR["ztsp"]])

    if stop_after == "p2b":
        dbg_dram("d_ztsp", "ztsp", [NH, C, T], BF16)
        return finish()

    dma("pool", xt.rearrange("p (ht t) -> p ht t", ht=16),
        A["wo"].rearrange("(ht p) o -> p ht o", p=128),
        reads=list(bXT), writes=list(bXT))
    for c_ in range(NCH):
        ztc = cs(c_ % 2)
        dma("sp", ztc.rearrange("p (n a) -> p n a", n=NH),
            A["ztsp"][:, :, c_ * C:(c_ + 1) * C].rearrange("n p a -> p n a"),
            reads=[bDR["ztsp"], bCOL[c_ % 2]], writes=[bCOL[c_ % 2]], chain=False)

        regP = (pa, pb)[c_ % 2]
        regPb = (pab, pbb)[c_ % 2]
        rbk = [bBank[(c_ % 2) * 4 + i] for i in range(4)]

        def mk_wo(ztc=ztc, regP=regP):
            def f(e):
                last = None
                for ht in range(16):
                    for j in range(4):
                        last = nc.tensor.matmul(
                            regP[:, j * 512:(j + 1) * 512],
                            ztc[:, ht * 128:(ht + 1) * 128],
                            xt[:, ht * T + j * 512:ht * T + (j + 1) * 512],
                            start=(ht == 0), stop=(ht == 15),
                            skip_group_check=True)
                return last
            return f
        Sc.add("te", mk_wo(), reads=[bCOL[c_ % 2]] + list(bXT), writes=rbk)
        xres = fs(0)
        dma("act", xres, A["x"][c_ * C:(c_ + 1) * C, :],
            reads=[bF32[0]], writes=[bF32[0]], chain=False)
        y1 = fs(1)
        Sc.add("dve", lambda e, o=y1, i1=xres, i0=regP: e.tensor_add(o, i0[:, :], i1),
               reads=rbk + [bF32[0]], writes=[bF32[1]])
        Sc.add("dve", lambda e, o=st1(0), i=y1: e.reduce_sum(o, i, axis=AX.X),
               reads=[bF32[1]], writes=[bST])
        Sc.add("act", lambda e, o=fs(2), i=y1: e.activation(o, i, AF.Square),
               reads=[bF32[1]], writes=[bF32[2]])
        Sc.add("dve", lambda e, o=st1(1), i=fs(2): e.reduce_sum(o, i, axis=AX.X),
               reads=[bF32[2]], writes=[bST])

        Sc.add("dve", lambda e: e.tensor_scalar_mul(st1(0), st1(0), 1.0 / H),
               reads=[bST], writes=[bST])
        Sc.add("dve", lambda e: e.tensor_scalar_mul(st1(1), st1(1), 1.0 / H),
               reads=[bST], writes=[bST])
        Sc.add("dve", lambda e: e.tensor_mul(st1(2), st1(0), st1(0)),
               reads=[bST], writes=[bST])
        Sc.add("dve", lambda e: e.tensor_sub(st1(1), st1(1), st1(2)),
               reads=[bST], writes=[bST])
        Sc.add("act", lambda e: e.activation(st1(2), st1(1), AF.Sqrt, bias=eps_col),
               reads=[bST], writes=[bST])
        Sc.add("dve", lambda e: e.reciprocal(st1(3), st1(2)), reads=[bST],
               writes=[bST])
        Sc.add("dve", lambda e, o=y1: e.tensor_scalar(o, o, st1(0), st1(3),
                                                      ALU.subtract, ALU.mult),
               reads=[bF32[1], bST], writes=[bF32[1]])
        dma("act", A["y1sp"][c_], y1, reads=[bF32[1]], writes=[bDR["y1sp"]])
        y1b = cs(2)
        Sc.add("act", lambda e, o=y1b, i=y1: e.copy(o, i),
               reads=[bF32[1]], writes=[bCOL[2]])

        def mk_y1tp(y1b=y1b, regPb=regPb):
            def f(e):
                last = None
                for ht in range(16):
                    last = nc.tensor.matmul(
                        regPb[:, ht * 128:(ht + 1) * 128],
                        y1b[:, ht * 128:(ht + 1) * 128], idn[:, :],
                        start=(ht % 8 == 0), stop=(ht % 8 == 7),
                        is_transpose=True, skip_group_check=True)
                return last
            return f
        Sc.add("te", mk_y1tp(), reads=[bCOL[2], bIDN], writes=rbk[0:2])
        dstk = kt.rearrange("p (ht t) -> p ht t", ht=16)[:, :, c_ * C:(c_ + 1) * C]
        srck = regPb[:, 0:2048].rearrange("p (ht a) -> p ht a", ht=16)
        Sc.add("act", lambda e, o=dstk, i=srck: e.copy(o, i),
               reads=rbk[0:2],
               writes=[bKTT[c_]] + (list(bKT) if c_ == 0 else []))

    if stop_after == "p3":
        dbg_dram("d_y1sp", "y1sp", [NCH, C, H], F32)
        dbg_sbuf("d_y1t", kt[:, :], list(bKT))
        return finish()

    dma("sp", tabf[:, :], A["sdown"][:, :], reads=[bTAB], writes=[bTAB])
    dma("sp", sgc[:, 0:INTER // C], A["sgate"][:, :], writes=[bSGC])
    dma("sp", sgc[:, INTER // C:], A["sup"][:, :], writes=[bSGC])

    xtf = xt.bitcast(F32)
    gnbbf = gnbb.bitcast(F32)
    snbf = snb.bitcast(F32)

    def acc_half(tg, hh):
        if tg < 3:
            return fs(tg)[:, hh * 1024:(hh + 1) * 1024]
        return gnbbf if hh == 0 else snbf

    def acc_bufs(tg):
        return [bF32[tg]] if tg < 3 else [bGNB, bSNB]

    stv = lambda tg, i: st[:, tg * 8 + i:tg * 8 + i + 1]

    out_dmas = []
    for s_ in range(NSTRIP):
        for it in range(64):
            tb, j = it // 2, it % 2
            par = it % 2
            wgs, wus = cs(4 + 2 * par), cs(5 + 2 * par)
            bwg, bwu = bCOL[4 + 2 * par], bCOL[5 + 2 * par]
            dma("pool", wgs.rearrange("p (h i) -> p h i", h=16),
                A["wgt"][tb, j], reads=[bwg], writes=[bwg], chain=False)
            dma("pool", wus.rearrange("p (h i) -> p h i", h=16),
                A["wut"][tb, j], reads=[bwu], writes=[bwu], chain=False)
            bank_g = bBank[par * 2]
            bank_u = bBank[par * 2 + 1]
            pg = pa[:, par * 1024:par * 1024 + 512]
            pu = pa[:, par * 1024 + 512:par * 1024 + 1024]

            def mk_gu(pg=pg, pu=pu, wgs=wgs, wus=wus, s_=s_):
                def f(e):
                    last = None
                    for ht in range(16):
                        rhs = kt[:, ht * T + s_ * STRIP:ht * T + (s_ + 1) * STRIP]
                        nc.tensor.matmul(
                            pg, wgs[:, ht * 128:(ht + 1) * 128], rhs,
                            start=(ht == 0), stop=(ht == 15),
                            skip_group_check=True)
                        last = nc.tensor.matmul(
                            pu, wus[:, ht * 128:(ht + 1) * 128], rhs,
                            start=(ht == 0), stop=(ht == 15),
                            skip_group_check=True)
                    return last
                return f
            Sc.add("te", mk_gu(), reads=[bwg, bwu] + [bKTT[4 * s_ + k]
                                                      for k in range(4)],
                   writes=[bank_g, bank_u])
            gb = gub[:, par * STRIP:(par + 1) * STRIP]
            Sc.add("act", lambda e, o=gb, i=pg, it=it:
                   e.activation(o, i, AF.Silu, scale=sgc[:, it:it + 1]),
                   reads=[bank_g, bSGC], writes=[bGUB[par]])
            fu = xt[:, it * STRIP:(it + 1) * STRIP]
            Sc.add("dve", lambda e, o=fu, i0=pu, it=it, gb=gb:
                   e.scalar_tensor_tensor(o, i0, sgc[:, 64 + it:64 + it + 1],
                                          gb, ALU.mult, ALU.mult),
                   reads=[bank_u, bGUB[par], bSGC], writes=[bXT[it // 4]])
        if DBG4 and s_ == 0:
            dbg_sbuf("d_fuse", xt[:, :], list(bXT), dtype=BF16, width=16 * T)
            dbg_sbuf("d_y1fm", kt[:, :], list(bKTT), dtype=BF16, width=16 * T)
            dbg_dram("d_ztsp", "ztsp", [NH, C, T], BF16)
        for hh in range(2):
            for grp in range(16):
                par = grp % 2
                wds = cols[:, (par * 2) * T:(par * 2 + 2) * T]
                bwd = [bCOL[par * 2], bCOL[par * 2 + 1]]
                dma("pool", wds.rearrange("p (g h) -> p g h", g=4),
                    A["wdt"][hh, grp], reads=bwd, writes=bwd,
                    dsem=f"dwn{par}", chain=False)

                def mk_down(grp=grp, wds=wds):
                    def f(e):
                        last = None
                        for g4 in range(4):
                            it = grp * 4 + g4
                            for tg in range(4):
                                psum = (pa, pb)[tg // 2]
                                po = (tg % 2) * 1024
                                for jn in range(2):
                                    last = nc.tensor.matmul(
                                        psum[:, po + jn * 512:po + (jn + 1) * 512],
                                        xt[:, it * STRIP + tg * 128:
                                           it * STRIP + tg * 128 + 128],
                                        wds[:, g4 * 1024 + jn * 512:
                                            g4 * 1024 + (jn + 1) * 512],
                                        start=(it == 0), stop=(it == 63),
                                        skip_group_check=True)
                        return last
                    return f
                Sc.add("te", mk_down(), reads=bwd + list(bXT),
                       writes=banksA + banksB)
            for tg in range(4):
                psum = (pa, pb)[tg // 2]
                po = (tg % 2) * 1024
                pbanks = [bBank[tg * 2], bBank[tg * 2 + 1]]
                dst = acc_half(tg, hh)
                Sc.add("dve", lambda e, o=dst, i0=psum[:, po:po + 1024],
                       i1=tabf[:, hh * 1024:(hh + 1) * 1024]:
                       e.tensor_tensor(o, i0, i1, ALU.mult),
                       reads=pbanks + [bTAB] + acc_bufs(tg), writes=acc_bufs(tg))
        for tg in range(4):
            c_ = s_ * 4 + tg
            babf = acc_bufs(tg)
            y1h = [xtf[:, tg * 2048 + hh * 1024:tg * 2048 + (hh + 1) * 1024]
                   for hh in range(2)]
            by1 = [bXT[2 * tg], bXT[2 * tg + 1]]
            dma("act", xtf[:, tg * 2048:(tg + 1) * 2048], A["y1sp"][c_],
                reads=[bDR["y1sp"]] + by1, writes=by1, dsem="y1r", chain=False)
            for hh in range(2):
                Sc.add("dve", lambda e, o=acc_half(tg, hh), i1=y1h[hh]:
                       e.tensor_add(o, o, i1),
                       reads=babf + by1, writes=babf)
            Sc.add("dve", lambda e, o=stv(tg, 0), i=acc_half(tg, 0):
                   e.reduce_sum(o, i, axis=AX.X), reads=babf, writes=[bST])
            Sc.add("dve", lambda e, o=stv(tg, 4), i=acc_half(tg, 1):
                   e.reduce_sum(o, i, axis=AX.X), reads=babf, writes=[bST])
            for hh in range(2):
                Sc.add("act", lambda e, o=y1h[hh], i=acc_half(tg, hh):
                       e.activation(o, i, AF.Square),
                       reads=babf + by1, writes=by1)
            Sc.add("dve", lambda e, o=stv(tg, 1), i=y1h[0]:
                   e.reduce_sum(o, i, axis=AX.X), reads=by1, writes=[bST])
            Sc.add("dve", lambda e, o=stv(tg, 5), i=y1h[1]:
                   e.reduce_sum(o, i, axis=AX.X), reads=by1, writes=[bST])
            Sc.add("dve", lambda e, t=tg: e.tensor_add(stv(t, 0), stv(t, 0),
                                                       stv(t, 4)),
                   reads=[bST], writes=[bST])
            Sc.add("dve", lambda e, t=tg: e.tensor_add(stv(t, 1), stv(t, 1),
                                                       stv(t, 5)),
                   reads=[bST], writes=[bST])
            Sc.add("dve", lambda e, t=tg: e.tensor_scalar_mul(stv(t, 0),
                                                              stv(t, 0), 1.0 / H),
                   reads=[bST], writes=[bST])
            Sc.add("dve", lambda e, t=tg: e.tensor_scalar_mul(stv(t, 1),
                                                              stv(t, 1), 1.0 / H),
                   reads=[bST], writes=[bST])
            Sc.add("dve", lambda e, t=tg: e.tensor_mul(stv(t, 2), stv(t, 0),
                                                       stv(t, 0)),
                   reads=[bST], writes=[bST])
            Sc.add("dve", lambda e, t=tg: e.tensor_sub(stv(t, 1), stv(t, 1),
                                                       stv(t, 2)),
                   reads=[bST], writes=[bST])
            Sc.add("act", lambda e, t=tg: e.activation(stv(t, 2), stv(t, 1),
                                                       AF.Sqrt, bias=eps_col),
                   reads=[bST], writes=[bST])
            Sc.add("dve", lambda e, t=tg: e.reciprocal(stv(t, 3), stv(t, 2)),
                   reads=[bST], writes=[bST])
            for hh in range(2):
                Sc.add("dve", lambda e, o=acc_half(tg, hh), t=tg:
                       e.tensor_scalar(o, o, stv(t, 0), stv(t, 3),
                                       ALU.subtract, ALU.mult),
                       reads=babf + [bST], writes=babf)
                out_dmas.append(
                    dma("act", A["y"][c_ * C:(c_ + 1) * C,
                                      hh * 1024:(hh + 1) * 1024],
                        acc_half(tg, hh), reads=babf, writes=[], dsem="out"))

    Sc.add("sp", lambda e: e.nop(), deps=out_dmas)

    return finish()



def kernel(L0_attn_X, L0_attn_W_Q, L0_attn_W_K, L0_attn_W_V, L0_attn_W_G,
           L0_attn_W_O, L0_attn_gn_weight, L0_attn_gn_bias, L0_ffn_w_gate,
           L0_ffn_w_up, L0_ffn_w_down, L0_ffn_s_gate, L0_ffn_s_up, L0_ffn_s_down):
    X = np.asarray(L0_attn_X)
    cst = _build_host_constants()

    def cat(w):
        m = np.asarray(w).transpose(1, 0, 2).reshape(H, H).astype(np.float32)
        return np.ascontiguousarray(
            m.reshape(16, C, 16, D).transpose(2, 1, 0, 3))
    wq, wk, wv = cat(L0_attn_W_Q), cat(L0_attn_W_K), cat(L0_attn_W_V)
    wg = np.ascontiguousarray(np.asarray(L0_attn_W_G, np.float32)
                              .reshape(16, C, 16, D).transpose(2, 1, 0, 3))
    wo = np.ascontiguousarray(np.asarray(L0_attn_W_O, np.float32))
    gnw = np.ascontiguousarray(np.broadcast_to(
        np.asarray(L0_attn_gn_weight, np.float32), (C, H)))
    gnb = np.ascontiguousarray(np.broadcast_to(
        np.asarray(L0_attn_gn_bias, np.float32), (C, H)))
    wgt = np.ascontiguousarray(
        np.asarray(L0_ffn_w_gate).reshape(16, C, NTB, 2, 128)
        .transpose(2, 3, 1, 0, 4))
    wut = np.ascontiguousarray(
        np.asarray(L0_ffn_w_up).reshape(16, C, NTB, 2, 128)
        .transpose(2, 3, 1, 0, 4))
    wdt = np.ascontiguousarray(
        np.asarray(L0_ffn_w_down).reshape(16, 4, C, 2, 1024)
        .transpose(3, 0, 2, 1, 4))
    sgate = np.ascontiguousarray(
        np.asarray(L0_ffn_s_gate, np.float32).reshape(INTER // C, C).T)
    sup = np.ascontiguousarray(
        np.asarray(L0_ffn_s_up, np.float32).reshape(INTER // C, C).T)
    sdown = np.ascontiguousarray(np.broadcast_to(
        np.asarray(L0_ffn_s_down, np.float32), (C, H)))

    shared = dict(wq=wq, wk=wk, wv=wv, wg=wg, wo=wo, gnw=gnw, gnb=gnb,
                  cq=cst["cq"], sq=cst["sq"], ck=cst["ck"], sk=cst["sk"],
                  gq=cst["gq"], gk=cst["gk"], causal=cst["causal"],
                  ident=cst["ident"], wgt=wgt, wut=wut, wdt=wdt,
                  sgate=sgate, sup=sup, sdown=sdown)

    in_maps = []
    for core in range(N_CORES):
        bq, qq = core // NQ, core % NQ
        m = dict(shared)
        m["x"] = np.ascontiguousarray(X[bq, qq * T:(qq + 1) * T, :].astype(np.float32))
        m["that"] = cst["that_by_core"][core]
        in_maps.append(m)

    nc = build_program()
    res = run_bass_kernel_spmd(nc, in_maps, core_ids=list(range(N_CORES)))

    out = np.empty((B, S, H), np.float32)
    for core in range(N_CORES):
        bq, qq = core // NQ, core % NQ
        out[bq, qq * T:(qq + 1) * T, :] = res.results[core]["y"]
    return out


# version 16 (final)
# speedup vs baseline: 1.0712x; 1.0010x over previous
import numpy as np
import ml_dtypes
import concourse.bass as bass
import concourse.mybir as mybir
from concourse.bass_utils import run_bass_kernel_spmd

F32 = mybir.dt.float32
BF16 = mybir.dt.bfloat16
I8 = mybir.dt.int8
AF = mybir.ActivationFunctionType
ALU = mybir.AluOpType
AX = mybir.AxisListType
bfdt = ml_dtypes.bfloat16

B, S, H = 2, 8192, 2048
NH, D = 16, 128
T = 2048
NCH = 16
C = 128
INTER = 8192
SCALE_BASE = 512.0
GN_EPS = 1e-5
LN_EPS = 1e-5
N_CORES = 8
NQ = 4
NTB = 32
NIB = 256
NSTRIP = 4
STRIP = 512
DBGH = 8
DBG4 = False
SERIALIZE = False



def _xpos_tables_f64(d, positions):
    half = d // 2
    xscale = (np.arange(0, d, 2, dtype=np.float64) + 0.4 * d) / (1.4 * d)
    pos = positions.astype(np.float64)
    scale = xscale[None, :] ** (pos[:, None] / SCALE_BASE)
    inv_freq = 1.0 / 10000.0 ** (np.arange(half, dtype=np.float64) / half)
    sinu = pos[:, None] * inv_freq[None, :]
    sin, cos = np.sin(sinu), np.cos(sinu)
    dup = lambda t: np.repeat(t, 2, axis=-1)
    return dup(cos * scale), dup(sin * scale), dup(cos / scale), dup(sin / scale)


def _build_host_constants():
    gammas = 1.0 - np.exp(np.linspace(np.log(1 / 32), np.log(1 / 512), NH))
    p = np.arange(T)
    cq, sq, ck, sk = _xpos_tables_f64(D, p)
    sgn = np.ones(D)
    sgn[0::2] = -1.0
    sq = sq * sgn[None, :]
    sk = sk * sgn[None, :]

    def to_col(tab):
        return np.ascontiguousarray(
            tab.reshape(NCH, C, D).transpose(1, 0, 2).reshape(C, NCH * D))

    cq_t = to_col(cq).astype(bfdt)
    sq_t = to_col(sq).astype(bfdt)
    ck_t = to_col(ck).astype(bfdt)
    sk_t = to_col(sk).astype(bfdt)

    pc = (np.arange(NCH)[None, :] * C + np.arange(C)[:, None] + 1.0)
    gq = np.empty((C, NH * NCH), np.float64)
    gk = np.empty((C, NH * NCH), np.float64)
    for n in range(NH):
        gq[:, n * NCH:(n + 1) * NCH] = gammas[n] ** pc
        gk[:, n * NCH:(n + 1) * NCH] = gammas[n] ** (-pc)
    gq = gq.astype(np.float32)
    gk = gk.astype(np.float32)

    causal = (np.arange(C)[None, :] >= np.arange(C)[:, None]).astype(np.float64)
    causal_rep = np.ascontiguousarray(causal.astype(bfdt))
    ident = np.ascontiguousarray(np.eye(C, dtype=bfdt))

    half = D // 2
    xscale = (np.arange(0, D, 2, dtype=np.float64) + 0.4 * D) / (1.4 * D)
    inv_freq = 1.0 / 10000.0 ** (np.arange(half, dtype=np.float64) / half)

    def that_lhsT(D_off, n):
        sig = xscale ** (D_off / SCALE_BASE)
        th = D_off * inv_freq
        g = float(np.float32(gammas[n] ** D_off))
        M = np.zeros((D, D))
        for i in range(half):
            cth, sth = np.cos(th[i]), np.sin(th[i])
            M[2 * i, 2 * i] = cth * sig[i] * g
            M[2 * i, 2 * i + 1] = sth * sig[i] * g
            M[2 * i + 1, 2 * i] = -sth * sig[i] * g
            M[2 * i + 1, 2 * i + 1] = cth * sig[i] * g
        return M.T

    that_by_core = []
    for core in range(N_CORES):
        bq, qq = core // NQ, core % NQ
        that = np.zeros((N_CORES, NH, D, D), np.float64)
        for r in range(N_CORES):
            rb, rq = r // NQ, r % NQ
            if rb == bq and rq < qq:
                for n in range(NH):
                    that[r, n] = that_lhsT(T * (qq - rq), n)
        that_by_core.append(np.ascontiguousarray(that.astype(bfdt)))

    return dict(cq=cq_t, sq=sq_t, ck=ck_t, sk=sk_t, gq=gq, gk=gk,
                causal=causal_rep, ident=ident, that_by_core=that_by_core)



class Buf:
    __slots__ = ("writer", "readers", "sem", "group", "writers")

    def __init__(self, sem=None, group=False):
        self.writer = None
        self.writers = []
        self.readers = []
        self.sem = sem
        self.group = group


class Sched:
    ENGINES = ("te", "act", "dve", "pool", "sp")

    def __init__(self):
        self.ops = []
        self.last_dma = {}

    def add(self, eng, fn, reads=(), writes=(), deps=(), dma=False, dsem=None,
            chain=True):
        idx = len(self.ops)
        dep = set(deps)
        if SERIALIZE and idx > 0:
            dep.add(idx - 1)
        for b in reads:
            if b.group:
                dep.update(b.writers)
            elif b.writer is not None:
                dep.add(b.writer)
        for b in writes:
            if b.group:
                dep.update(b.readers)
                b.writers.append(idx)
            else:
                if b.writer is not None:
                    dep.add(b.writer)
                dep.update(b.readers)
                b.writer = idx
                b.readers = []
        dep.discard(idx)
        for d in dep:
            assert d < idx
        if dma:
            if dsem is None and writes:
                dsem = writes[0].sem
            assert dsem is not None, "dma op needs a dsem"
            if chain and dsem in self.last_dma:
                dep.add(self.last_dma[dsem])
            self.last_dma[dsem] = idx
        self.ops.append(dict(i=idx, eng=eng, fn=fn, deps=sorted(dep), dma=dma,
                             dsem=dsem))
        for b in reads:
            b.readers.append(idx)
        for b in writes:
            if not b.group:
                pass
        return idx

    def finalize(self):
        needed = set()
        for op in self.ops:
            needed.update(op["deps"])
        counters = {}
        pos = {}
        for op in self.ops:
            op["needed"] = op["i"] in needed or op["dma"]
            if op["needed"]:
                key = ("d", op["dsem"]) if op["dma"] else ("c", op["eng"])
                inc = 16 if op["dma"] else 1
                counters[key] = counters.get(key, 0) + inc
                pos[op["i"]] = (key, counters[key])
                op["inc"] = inc
        clocks = {e: {} for e in self.ENGINES}
        for op in self.ops:
            wmap = {}
            for d in op["deps"]:
                dop = self.ops[d]
                key, tgt = pos[d]
                if (not dop["dma"]) and dop["eng"] == op["eng"] == "te":
                    continue
                if wmap.get(key, 0) < tgt:
                    wmap[key] = tgt
            clk = clocks[op["eng"]]
            ww = []
            for key in sorted(wmap):
                tgt = wmap[key]
                if clk.get(key, 0) >= tgt:
                    continue
                clk[key] = tgt
                ww.append((key, tgt))
            op["waits"] = ww

    def sem_keys(self):
        ks = set()
        for op in self.ops:
            if op.get("needed"):
                ks.add(("d", op["dsem"]) if op["dma"] else ("c", op["eng"]))
        return sorted(ks)

    def emit_engine(self, eng_name, eng_obj, sems):
        for op in self.ops:
            if op["eng"] != eng_name:
                continue
            for key, tgt in op["waits"]:
                eng_obj.wait_ge(sems[key], tgt)
            inst = op["fn"](eng_obj)
            if op["needed"]:
                assert inst is not None, f"op {op['i']} on {eng_name} returned None"
                key = ("d", op["dsem"]) if op["dma"] else ("c", op["eng"])
                inst.then_inc(sems[key], op["inc"])



def build_program(stop_after=None):
    nc = bass.Bass("TRN2", target_bir_lowering=False)

    dr = {}
    ei = "ExternalInput"
    dr["x"] = nc.dram_tensor("x", [T, H], F32, kind=ei)
    for nm in ("wq", "wk", "wv", "wg"):
        dr[nm] = nc.dram_tensor(nm, [16, C, 16, D], F32, kind=ei)
    dr["wo"] = nc.dram_tensor("wo", [H, H], F32, kind=ei)
    dr["gnw"] = nc.dram_tensor("gnw", [C, H], F32, kind=ei)
    dr["gnb"] = nc.dram_tensor("gnb", [C, H], F32, kind=ei)
    for nm in ("cq", "sq", "ck", "sk"):
        dr[nm] = nc.dram_tensor(nm, [C, T], BF16, kind=ei)
    dr["causal"] = nc.dram_tensor("causal", [C, C], BF16, kind=ei)
    dr["gq"] = nc.dram_tensor("gq", [C, NH * NCH], F32, kind=ei)
    dr["gk"] = nc.dram_tensor("gk", [C, NH * NCH], F32, kind=ei)
    dr["ident"] = nc.dram_tensor("ident", [C, C], BF16, kind=ei)
    dr["that"] = nc.dram_tensor("that", [N_CORES, NH, D, D], BF16, kind=ei)
    dr["wgt"] = nc.dram_tensor("wgt", [NTB, 2, C, 16, 128], I8, kind=ei)
    dr["wut"] = nc.dram_tensor("wut", [NTB, 2, C, 16, 128], I8, kind=ei)
    dr["wdt"] = nc.dram_tensor("wdt", [2, 16, C, 4, 1024], I8, kind=ei)
    dr["sgate"] = nc.dram_tensor("sgate", [C, INTER // C], F32, kind=ei)
    dr["sup"] = nc.dram_tensor("sup", [C, INTER // C], F32, kind=ei)
    dr["sdown"] = nc.dram_tensor("sdown", [C, H], F32, kind=ei)
    dr["y"] = nc.dram_tensor("y", [T, H], F32, kind="ExternalOutput")
    dr["vqg"] = nc.dram_tensor("vqg", [NH, 3, C, T], BF16, kind="Internal")
    dr["ztsp"] = nc.dram_tensor("ztsp", [NH, C, T], BF16, kind="Internal")
    dr["y1sp"] = nc.dram_tensor("y1sp", [NCH, C, H], F32, kind="Internal")
    dr["ccin"] = nc.dram_tensor("ccin", [C, NH * D + 0], BF16, kind="Internal")
    dr["ccout"] = nc.dram_tensor("ccout", [N_CORES * C, NH * D], BF16, kind="Internal")

    A = {k: v.ap() for k, v in dr.items()}

    XT = nc.alloc_sbuf_tensor("XTa", [128, 16 * T], BF16)
    KT = nc.alloc_sbuf_tensor("KTa", [128, 16 * T], BF16)
    COLS = nc.alloc_sbuf_tensor("COLSa", [128, 8 * T], BF16)
    F32A = nc.alloc_sbuf_tensor("F32Aa", [128, 3 * H], F32)
    TAB = nc.alloc_sbuf_tensor("TABa", [128, 2 * T], BF16)
    CAU = nc.alloc_sbuf_tensor("CAUa", [128, C], BF16)
    GNBB = nc.alloc_sbuf_tensor("GNBBa", [128, H], BF16)
    GAM = nc.alloc_sbuf_tensor("GAMa", [128, 2 * NH * NCH], F32)
    IDN = nc.alloc_sbuf_tensor("IDNa", [128, C], BF16)
    GUB = nc.alloc_sbuf_tensor("GUBa", [128, 2 * STRIP], BF16)
    ST = nc.alloc_sbuf_tensor("STa", [128, 180], F32)
    SNB = nc.alloc_sbuf_tensor("SNBa", [128, T], BF16)
    SGC = nc.alloc_sbuf_tensor("SGCa", [128, 2 * (INTER // C)], F32)

    xt = XT.ap(); kt = KT.ap(); cols = COLS.ap(); f32a = F32A.ap()
    tab = TAB.ap(); cau = CAU.ap(); gnbb = GNBB.ap(); gam = GAM.ap()
    idn = IDN.ap(); gub = GUB.ap(); st = ST.ap(); snb = SNB.ap(); sgc = SGC.ap()
    tabf = tab.bitcast(F32)

    PA = nc.alloc_psum_tensor("PAa", [128, 2048], F32)
    PB = nc.alloc_psum_tensor("PBa", [128, 2048], F32)
    pa = PA.ap(); pb = PB.ap()
    pbb = pb.bitcast(BF16)

    Sc = Sched()
    bXT = [Buf(sem="xt") for _ in range(16)]
    bKT = [Buf() for _ in range(16)]
    bCOL = [Buf(sem=f"cols{i}") for i in range(8)]
    bF32 = [Buf(sem=f"f32{i}") for i in range(3)]
    bTAB = Buf(sem="tab"); bCAU = Buf(sem="cau"); bGNB = Buf(sem="gnb")
    bGAM = Buf(sem="gam"); bIDN = Buf(sem="idn")
    bGUB = [Buf(), Buf()]; bST = Buf(); bSNB = Buf(sem="snb"); bSGC = Buf(sem="sgc")
    bKTT = [Buf() for _ in range(16)]
    bBank = [Buf() for _ in range(8)]
    bDR = {k: Buf(sem=k, group=True)
           for k in ("vqg", "ztsp", "y1sp")}
    bDR["ccin"] = Buf(sem="ccin")
    bDR["ccout"] = Buf()

    banksA = [bBank[i] for i in range(4)]
    banksB = [bBank[i] for i in range(4, 8)]
    bTRhi = [bBank[6], bBank[7]]
    bTRlo = [bBank[4], bBank[5]]

    cs = lambda i: cols[:, i * T:(i + 1) * T]
    xts = lambda i: xt[:, i * T:(i + 1) * T]
    pab = pa.bitcast(BF16)
    fs = lambda i: f32a[:, i * H:(i + 1) * H]
    sti = lambda i: st[:, i * 20:i * 20 + 16]
    st1 = lambda i: st[:, i * 20:i * 20 + 1]

    eps_col = st[:, 176:177]
    Sc_eps_done = []

    def dma(engq, out, in_, reads=(), writes=(), deps=(), dsem=None, chain=True):
        return Sc.add(engq, lambda e, o=out, i=in_: e.dma_start(out=o, in_=i),
                      reads=reads, writes=writes, deps=deps, dma=True, dsem=dsem,
                      chain=chain)

    def finish():
        from contextlib import ExitStack
        Sc.finalize()
        with ExitStack() as stack:
            sems = {}
            for key in Sc.sem_keys():
                nm = ("cs_" + key[1]) if key[0] == "c" else ("ds_" + str(key[1]))
                sems[key] = stack.enter_context(nc.semaphore(nm))
            block = stack.enter_context(nc.Block())

            @block.tensor
            def _(eng):
                Sc.emit_engine("te", eng, sems)

            @block.scalar
            def _(eng):
                Sc.emit_engine("act", eng, sems)

            @block.vector
            def _(eng):
                Sc.emit_engine("dve", eng, sems)

            @block.gpsimd
            def _(eng):
                Sc.emit_engine("pool", eng, sems)

            @block.sync
            def _(eng):
                Sc.emit_engine("sp", eng, sems)
        return nc

    def dbg_sbuf(nm, ap, bufs, dtype=BF16, width=16 * T):
        dt_ = nc.dram_tensor(nm, [128, width], dtype, kind="ExternalOutput")
        dd = dma("sp", dt_.ap()[:, :], ap, reads=bufs, writes=[], dsem="out")
        Sc.add("sp", lambda e: e.nop(), deps=[dd])

    def dbg_dram(nm, key, shape, dtype):
        dt_ = nc.dram_tensor(nm, shape, dtype, kind="ExternalOutput")
        idx = tuple(slice(None) for _ in shape)
        dd = dma("sp", dt_.ap()[idx], A[key][idx], reads=[bDR[key]], writes=[], dsem="out")
        Sc.add("sp", lambda e: e.nop(), deps=[dd])

    Sc.add("dve", lambda e: e.memset(eps_col, 1e-5), writes=[bST])
    dma("sp", cau[:, :], A["causal"][:, :], writes=[bCAU])
    dma("sp", idn[:, :], A["ident"][:, :], writes=[bIDN])
    dma("sp", gam[:, 0:NH * NCH], A["gq"][:, :], writes=[bGAM])
    dma("sp", gam[:, NH * NCH:2 * NH * NCH], A["gk"][:, :], writes=[bGAM])
    dma("sp", tab[:, 0:T], A["ck"][:, :], writes=[bTAB])
    dma("sp", tab[:, T:2 * T], A["sk"][:, :], writes=[bTAB])

    for c_ in range(NCH):
        pp = c_ % 2
        xst = fs(pp)
        dma("sp", xst, A["x"][c_ * C:(c_ + 1) * C, :], reads=[bF32[pp]],
            writes=[bF32[pp]], chain=False)
        xb = cs(pp)
        Sc.add("act", lambda e, o=xb, i=xst: e.copy(o, i),
               reads=[bF32[pp]], writes=[bCOL[pp]])

        regv = (pbb, pab)[pp]
        rbu = (bTRhi, [bBank[2], bBank[3]])[pp]

        def mk_tp(src=xb, regv=regv):
            def f(e):
                last = None
                for ht in range(16):
                    last = nc.tensor.matmul(
                        regv[:, 2048 + ht * 128:2048 + (ht + 1) * 128],
                        src[:, ht * 128:(ht + 1) * 128], idn[:, :],
                        start=(ht % 8 == 0), stop=(ht % 8 == 7),
                        is_transpose=True, skip_group_check=True)
                return last
            return f
        Sc.add("te", mk_tp(), reads=[bCOL[pp], bIDN], writes=rbu)
        dst = xt.rearrange("p (ht t) -> p ht t", ht=16)[:, :, c_ * C:(c_ + 1) * C]
        src = regv[:, 2048:4096].rearrange("p (ht a) -> p ht a", ht=16)
        Sc.add("act", lambda e, o=dst, i=src: e.copy(o, i),
               reads=rbu, writes=list(bXT))

    if stop_after == "p0":
        dbg_sbuf("d_xt", xt[:, :], list(bXT))
        return finish()

    def proj_tile(n, wap, wbuf, wapb, out_ap, out_buf):
        dma("pool", wapb.rearrange("p (ht d) -> p ht d", ht=16),
            wap[n], reads=[wbuf], writes=[wbuf], chain=False)
        for half in range(2):
            regp = (pb, pa)[half]
            rbufs = ([bBank[4], bBank[5]], [bBank[0], bBank[1]])[half]
            def mk_mm(half=half, wapb=wapb, regp=regp):
                def f(e):
                    last = None
                    for ht in range(16):
                        for j in range(2):
                            off = ht * T + half * 1024 + j * 512
                            last = nc.tensor.matmul(
                                regp[:, j * 512:(j + 1) * 512],
                                wapb[:, ht * D:(ht + 1) * D],
                                xt[:, off:off + 512],
                                start=(ht == 0), stop=(ht == 15),
                                skip_group_check=True)
                    return last
                return f
            Sc.add("te", mk_mm(), reads=[wbuf] + list(bXT), writes=rbufs)
            Sc.add("act",
                   lambda e, o=out_ap[:, half * 1024:(half + 1) * 1024],
                   i=regp[:, 0:1024]: e.copy(o, i),
                   reads=rbufs, writes=[out_buf])

    def transpose16(src_ap, src_buf, dst_ap, dst_bufs, region="hi"):
        reg_off = 2048 if region == "hi" else 0
        reg_bufs = bTRhi if region == "hi" else bTRlo

        def f(e, src_ap=src_ap, reg_off=reg_off):
            last = None
            for c_ in range(16):
                last = nc.tensor.matmul(
                    pbb[:, reg_off + c_ * 128:reg_off + (c_ + 1) * 128],
                    src_ap[:, c_ * 128:(c_ + 1) * 128], idn[:, :],
                    start=(c_ % 8 == 0), stop=(c_ % 8 == 7),
                    is_transpose=True, skip_group_check=True)
            return last
        Sc.add("te", f, reads=[src_buf, bIDN], writes=reg_bufs)
        Sc.add("act",
               lambda e, o=dst_ap, i=pbb[:, reg_off:reg_off + 2048]: e.copy(o, i),
               reads=reg_bufs, writes=list(dst_bufs))

    def xpos_apply(col_ap, col_buf, gam_off, out_ap, out_bufs):
        tmp = fs(0)
        sw = col_ap.rearrange("p (c k two) -> p c k two", c=NCH, two=2)[:, :, :, ::-1]
        stv = tab[:, T:2 * T].rearrange("p (c k two) -> p c k two", c=NCH, two=2)
        Sc.add("dve", lambda e, o=tmp, i0=sw, i1=stv:
               e.tensor_tensor(o.rearrange("p (c k two) -> p c k two", c=NCH, two=2),
                               i0, i1, ALU.mult),
               reads=[col_buf, bTAB], writes=[bF32[0]])
        Sc.add("dve", lambda e, o=col_ap, i1=tab[:, 0:T]:
               e.tensor_tensor(o, o, i1, ALU.mult),
               reads=[col_buf, bTAB], writes=[col_buf])
        Sc.add("dve", lambda e, o=col_ap, i1=tmp:
               e.tensor_tensor(o, o, i1, ALU.add),
               reads=[col_buf, bF32[0]], writes=[col_buf])
        gview = gam[:, gam_off:gam_off + NCH].unsqueeze(2).broadcast_to([128, NCH, D])
        Sc.add("dve", lambda e, o=out_ap, i0=col_ap, i1=gview:
               e.tensor_tensor(o.rearrange("p (c d) -> p c d", c=NCH),
                               i0.rearrange("p (c d) -> p c d", c=NCH),
                               i1, ALU.mult),
               reads=[col_buf, bGAM], writes=list(out_bufs))

    for n in range(NH):
        wkb, wka = bCOL[3 + (n % 2)], cs(3 + (n % 2))
        wvb, wva = bCOL[5 + (n % 2)], cs(5 + (n % 2))
        proj_tile(n, A["wk"], wkb, wka, cs(0), bCOL[0])
        transpose16(cs(0), bCOL[0], kt[:, n * T:(n + 1) * T], [bKT[n]])
        xpos_apply(kt[:, n * T:(n + 1) * T], bKT[n], NH * NCH + n * NCH,
                   kt[:, n * T:(n + 1) * T], [bKT[n]])
        proj_tile(n, A["wv"], wvb, wva, cs(1), bCOL[1])
        transpose16(cs(1), bCOL[1], cs(2), [bCOL[2]])
        dma("sp", A["vqg"][n, 0], cs(2), reads=[bCOL[2]], writes=[bDR["vqg"]])

        stp = pa[:, 1024 + (n % 8) * D:1024 + (n % 8 + 1) * D]
        stpb = bBank[2 + (n % 8) // 4]

        def mk_state(n=n, stp=stp):
            def f(e):
                last = None
                for c_ in range(16):
                    last = nc.tensor.matmul(
                        stp,
                        kt[:, n * T + c_ * 128:n * T + (c_ + 1) * 128],
                        cs(2)[:, c_ * 128:(c_ + 1) * 128],
                        start=(c_ == 0), stop=(c_ == 15),
                        skip_group_check=True)
                return last
            return f
        Sc.add("te", mk_state(), reads=[bKT[n], bCOL[2]], writes=[stpb])
        Sc.add("dve", lambda e, o=fs(2)[:, n * D:(n + 1) * D], i=stp:
               e.tensor_copy(o, i),
               reads=[stpb, bF32[2]], writes=[bF32[2]])

    Sc.add("act", lambda e, o=cs(7), i=fs(2): e.copy(o, i),
           reads=[bF32[2]], writes=[bCOL[7]])
    dma("sp", A["ccin"][:, :], cs(7), reads=[bCOL[7]], writes=[bDR["ccin"]])
    Sc.add("pool", lambda e: e.collective_compute(
        "AllGather", ALU.bypass, [list(range(N_CORES))],
        [A["ccin"][:, :]], [A["ccout"][:, :]]),
        reads=[bDR["ccin"]], writes=[bDR["ccout"]])

    if stop_after == "p1":
        dbg_sbuf("d_kt", kt[:, :], list(bKT))
        dbg_dram("d_vqg", "vqg", [NH, 3, C, T], BF16)
        dbg_dram("d_ccin", "ccin", [C, NH * D + 0], BF16)
        return finish()

    dma("sp", tab[:, 0:T], A["cq"][:, :], reads=[bTAB], writes=[bTAB])
    dma("sp", tab[:, T:2 * T], A["sq"][:, :], reads=[bTAB], writes=[bTAB])
    for n in range(NH):
        wqb, wqa = bCOL[3 + (n % 2)], cs(3 + (n % 2))
        wgb, wga = bCOL[5 + (n % 2)], cs(5 + (n % 2))
        proj_tile(n, A["wq"], wqb, wqa, cs(0), bCOL[0])
        transpose16(cs(0), bCOL[0], cs(1), [bCOL[1]])
        xpos_apply(cs(1), bCOL[1], n * NCH, cs(1), [bCOL[1]])
        dma("sp", A["vqg"][n, 1], cs(1), reads=[bCOL[1]], writes=[bDR["vqg"]])
        proj_tile(n, A["wg"], wgb, wga, cs(0), bCOL[0])
        transpose16(cs(0), bCOL[0], cs(2), [bCOL[2]])
        Sc.add("act", lambda e, o=cs(2): e.activation(o, o, AF.Silu),
               reads=[bCOL[2]], writes=[bCOL[2]])
        dma("sp", A["vqg"][n, 2], cs(2), reads=[bCOL[2]], writes=[bDR["vqg"]])

    for r in range(N_CORES):
        dma("sp", cs(r), A["ccout"][r * C:(r + 1) * C, :],
            reads=[bDR["ccout"], bCOL[r]], writes=[bCOL[r]], dsem="cco")
    for r in range(N_CORES):
        dma("sp", snb[:, :].rearrange("p (n d) -> p n d", n=NH),
            A["that"][r].rearrange("n p d -> p n d"),
            reads=[bSNB], writes=[bSNB], chain=False)

        def mk_that(r=r):
            def f(e):
                last = None
                for n in range(NH):
                    last = nc.tensor.matmul(
                        pa[:, n * D:(n + 1) * D],
                        snb[:, n * D:(n + 1) * D],
                        cs(r)[:, n * D:(n + 1) * D],
                        start=(r == 0 and n % 4 == 0),
                        stop=(r == N_CORES - 1 and n % 4 == 3),
                        skip_group_check=True)
                return last
            return f
        Sc.add("te", mk_that(), reads=[bSNB, bCOL[r]], writes=banksA)
    Sc.add("act", lambda e, o=xts(12), i=pa[:, :]: e.copy(o, i),
           reads=banksA + [bXT[12]], writes=[bXT[12]])

    if stop_after == "exch":
        dbg_sbuf("d_sin", xts(12), [bXT[12]], dtype=BF16, width=T)
        dbg_dram("d_vqg", "vqg", [NH, 3, C, T], BF16)
        return finish()

    dma("sp", tabf[:, :], A["gnw"][:, :], reads=[bTAB], writes=[bTAB])
    dma("pool", gnbb[:, :], A["gnb"][:, :], writes=[bGNB])
    bSTp = [Buf(), Buf()]

    for n in range(NH):
        P_ = n % 2
        reg = (pa, pb)[P_]
        regb = (pab, pbb)[P_]
        reg3 = reg[:, :].rearrange("p (c d) -> p c d", c=NCH)
        rbanks = [bBank[P_ * 4 + i] for i in range(4)]
        rb01 = rbanks[0:2]
        sb0 = P_ * 6
        ktT, qtT, atm = xts(sb0), xts(sb0 + 1), xts(sb0 + 2)
        zc, ztc, dsB = xts(sb0 + 3), xts(sb0 + 4), xts(sb0 + 5)
        bktT, bqtT, batm = bXT[sb0], bXT[sb0 + 1], bXT[sb0 + 2]
        bzc, bztc, bdsB = bXT[sb0 + 3], bXT[sb0 + 4], bXT[sb0 + 5]
        snbP, bsnbP = xts(13 + P_), bXT[13 + P_]
        vc, qc, gc = cs(3 * P_), cs(3 * P_ + 1), cs(3 * P_ + 2)
        bvqg = [bCOL[3 * P_], bCOL[3 * P_ + 1], bCOL[3 * P_ + 2]]
        syF, bsyF = fs(1 + P_), bF32[1 + P_]
        so = P_ * 4
        stS = lambda i: st[:, (so + i) * 20:(so + i) * 20 + 16]
        bst = bSTp[P_]

        for k, q_ in ((0, "sp"), (1, "act"), (2, "pool")):
            dma(q_, cs(3 * P_ + k), A["vqg"][n, k],
                reads=[bDR["vqg"], bvqg[k]], writes=[bvqg[k]],
                dsem=f"vqg{P_}{k}", chain=False)

        def tp16(src_ap, src_buf, dst_ap, dst_buf, regb=regb, rb01=rb01):
            def f(e, src_ap=src_ap, regb=regb):
                last = None
                for c_ in range(16):
                    last = nc.tensor.matmul(
                        regb[:, c_ * 128:(c_ + 1) * 128],
                        src_ap[:, c_ * 128:(c_ + 1) * 128], idn[:, :],
                        start=(c_ % 8 == 0), stop=(c_ % 8 == 7),
                        is_transpose=True, skip_group_check=True)
                return last
            Sc.add("te", f, reads=[src_buf, bIDN], writes=rb01)
            Sc.add("act", lambda e, o=dst_ap, i=regb[:, 0:2048]: e.copy(o, i),
                   reads=rb01, writes=[dst_buf])

        tp16(kt[:, n * T:(n + 1) * T], bKT[n], ktT, bktT)
        tp16(qc, bvqg[1], qtT, bqtT)

        def mk_amm(ktT=ktT, qtT=qtT, reg=reg):
            def f(e):
                last = None
                for c_ in range(16):
                    last = nc.tensor.matmul(
                        reg[:, c_ * 128:(c_ + 1) * 128],
                        ktT[:, c_ * 128:(c_ + 1) * 128],
                        qtT[:, c_ * 128:(c_ + 1) * 128],
                        start=(c_ % 4 == 0), stop=(c_ % 4 == 3),
                        skip_group_check=True)
                return last
            return f
        Sc.add("te", mk_amm(), reads=[bktT, bqtT], writes=rbanks)
        cau_b = cau[:, :].unsqueeze(1).broadcast_to([128, NCH, C])
        Sc.add("dve", lambda e, o=atm, i0=reg3, i1=cau_b: e.tensor_tensor(
            o.rearrange("p (c a) -> p c a", c=NCH), i0, i1, ALU.mult),
               reads=rbanks + [bCAU], writes=[batm])

        def mk_ds(n=n, vc=vc, reg=reg):
            def f(e):
                last = None
                for c_ in range(16):
                    last = nc.tensor.matmul(
                        reg[:, c_ * 128:(c_ + 1) * 128],
                        kt[:, n * T + c_ * 128:n * T + (c_ + 1) * 128],
                        vc[:, c_ * 128:(c_ + 1) * 128],
                        start=(c_ % 4 == 0), stop=(c_ % 4 == 3),
                        skip_group_check=True)
                return last
            return f
        Sc.add("te", mk_ds(), reads=[bKT[n], bvqg[0]], writes=rbanks)
        Sc.add("act", lambda e, o=dsB, i=reg[:, :]: e.copy(o, i),
               reads=rbanks, writes=[bdsB])
        Sc.add("dve", lambda e, o=reg[:, 0:128], i=xts(12)[:, n * D:(n + 1) * D]:
               e.tensor_copy(o, i),
               reads=[bXT[12], bdsB] + rbanks, writes=[rbanks[0]])
        for c_ in range(15):
            Sc.add("dve", lambda e, c_=c_, reg=reg, dsB=dsB: e.tensor_add(
                reg[:, (c_ + 1) * 128:(c_ + 2) * 128],
                reg[:, c_ * 128:(c_ + 1) * 128],
                dsB[:, c_ * 128:(c_ + 1) * 128]),
                reads=[bdsB, rbanks[c_ // 4], rbanks[(c_ + 1) // 4]],
                writes=[rbanks[(c_ + 1) // 4]])
        Sc.add("act", lambda e, o=snbP, i=reg[:, :]: e.copy(o, i),
               reads=rbanks, writes=[bsnbP])

        def mk_y(atm=atm, vc=vc, qtT=qtT, snbP=snbP, reg=reg):
            def f(e):
                last = None
                for c_ in range(16):
                    nc.tensor.matmul(
                        reg[:, c_ * 128:(c_ + 1) * 128],
                        atm[:, c_ * 128:(c_ + 1) * 128],
                        vc[:, c_ * 128:(c_ + 1) * 128],
                        start=(c_ % 4 == 0), stop=False, skip_group_check=True)
                    last = nc.tensor.matmul(
                        reg[:, c_ * 128:(c_ + 1) * 128],
                        qtT[:, c_ * 128:(c_ + 1) * 128],
                        snbP[:, c_ * 128:(c_ + 1) * 128],
                        start=False, stop=(c_ % 4 == 3), skip_group_check=True)
                return last
            return f
        Sc.add("te", mk_y(), reads=[batm, bvqg[0], bqtT, bsnbP], writes=rbanks)
        Sc.add("act", lambda e, o=syF, i=reg[:, :]: e.copy(o, i),
               reads=rbanks, writes=[bsyF])

        sy3 = syF.rearrange("p (c d) -> p c d", c=NCH)
        Sc.add("dve", lambda e, o=stS(0), i=sy3: e.reduce_sum(o, i, axis=AX.X),
               reads=[bsyF], writes=[bst])
        Sc.add("act", lambda e, o=reg[:, :], i=syF: e.activation(o, i, AF.Square),
               reads=[bsyF] + rbanks, writes=rbanks)
        Sc.add("dve", lambda e, o=stS(1), i=reg3: e.reduce_sum(o, i, axis=AX.X),
               reads=rbanks, writes=[bst])
        Sc.add("dve", lambda e, a=stS(0): e.tensor_scalar_mul(a, a, 1.0 / D),
               reads=[bst], writes=[bst])
        Sc.add("dve", lambda e, a=stS(1): e.tensor_scalar_mul(a, a, 1.0 / D),
               reads=[bst], writes=[bst])
        Sc.add("dve", lambda e, a=stS(2), b=stS(0): e.tensor_mul(a, b, b),
               reads=[bst], writes=[bst])
        Sc.add("dve", lambda e, a=stS(1), b=stS(2): e.tensor_sub(a, a, b),
               reads=[bst], writes=[bst])
        Sc.add("act", lambda e, a=stS(2), b=stS(1): e.activation(a, b, AF.Sqrt,
                                                                 bias=eps_col),
               reads=[bst], writes=[bst])
        Sc.add("dve", lambda e, a=stS(3), b=stS(2): e.reciprocal(a, b),
               reads=[bst], writes=[bst])
        mu_b = stS(0).unsqueeze(2).broadcast_to([128, NCH, D])
        rs_b = stS(3).unsqueeze(2).broadcast_to([128, NCH, D])
        gnw_b = tabf[:, n * D:(n + 1) * D].unsqueeze(1).broadcast_to([128, NCH, D])
        gnb_b = gnbb[:, n * D:(n + 1) * D].unsqueeze(1).broadcast_to([128, NCH, D])
        Sc.add("dve", lambda e, o=reg3, i0=sy3, i1=mu_b:
               e.tensor_tensor(o, i0, i1, ALU.subtract),
               reads=[bsyF, bst] + rbanks, writes=rbanks)
        Sc.add("dve", lambda e, o=reg3, i1=rs_b: e.tensor_tensor(o, o, i1,
                                                                 ALU.mult),
               reads=[bst] + rbanks, writes=rbanks)
        Sc.add("dve", lambda e, o=reg3, i1=gnw_b: e.tensor_tensor(o, o, i1,
                                                                  ALU.mult),
               reads=[bTAB] + rbanks, writes=rbanks)
        Sc.add("dve", lambda e, o=reg3, i1=gnb_b: e.tensor_tensor(o, o, i1,
                                                                  ALU.add),
               reads=[bGNB] + rbanks, writes=rbanks)
        gc3 = gc.rearrange("p (c d) -> p c d", c=NCH)
        Sc.add("dve", lambda e, o=zc, i0=reg3, i1=gc3: e.tensor_tensor(
            o.rearrange("p (c d) -> p c d", c=NCH), i0, i1, ALU.mult),
               reads=[bvqg[2]] + rbanks, writes=[bzc])
        tp16(zc, bzc, ztc, bztc)
        dma("sp", A["ztsp"][n], ztc, reads=[bztc], writes=[bDR["ztsp"]])

    if stop_after == "p2b":
        dbg_dram("d_ztsp", "ztsp", [NH, C, T], BF16)
        return finish()

    dma("pool", xt.rearrange("p (ht t) -> p ht t", ht=16),
        A["wo"].rearrange("(ht p) o -> p ht o", p=128),
        reads=list(bXT), writes=list(bXT))
    for c_ in range(NCH):
        ztc = cs(c_ % 2)
        dma("sp", ztc.rearrange("p (n a) -> p n a", n=NH),
            A["ztsp"][:, :, c_ * C:(c_ + 1) * C].rearrange("n p a -> p n a"),
            reads=[bDR["ztsp"], bCOL[c_ % 2]], writes=[bCOL[c_ % 2]], chain=False)

        regP = (pa, pb)[c_ % 2]
        regPb = (pab, pbb)[c_ % 2]
        rbk = [bBank[(c_ % 2) * 4 + i] for i in range(4)]

        def mk_wo(ztc=ztc, regP=regP):
            def f(e):
                last = None
                for ht in range(16):
                    for j in range(4):
                        last = nc.tensor.matmul(
                            regP[:, j * 512:(j + 1) * 512],
                            ztc[:, ht * 128:(ht + 1) * 128],
                            xt[:, ht * T + j * 512:ht * T + (j + 1) * 512],
                            start=(ht == 0), stop=(ht == 15),
                            skip_group_check=True)
                return last
            return f
        Sc.add("te", mk_wo(), reads=[bCOL[c_ % 2]] + list(bXT), writes=rbk)
        xres = fs(0)
        dma("act", xres, A["x"][c_ * C:(c_ + 1) * C, :],
            reads=[bF32[0]], writes=[bF32[0]], chain=False)
        y1 = fs(1)
        Sc.add("dve", lambda e, o=y1, i1=xres, i0=regP: e.tensor_add(o, i0[:, :], i1),
               reads=rbk + [bF32[0]], writes=[bF32[1]])
        Sc.add("dve", lambda e, o=st1(0), i=y1: e.reduce_sum(o, i, axis=AX.X),
               reads=[bF32[1]], writes=[bST])
        Sc.add("act", lambda e, o=fs(2), i=y1: e.activation(o, i, AF.Square),
               reads=[bF32[1]], writes=[bF32[2]])
        Sc.add("dve", lambda e, o=st1(1), i=fs(2): e.reduce_sum(o, i, axis=AX.X),
               reads=[bF32[2]], writes=[bST])

        Sc.add("dve", lambda e: e.tensor_scalar_mul(st1(0), st1(0), 1.0 / H),
               reads=[bST], writes=[bST])
        Sc.add("dve", lambda e: e.tensor_scalar_mul(st1(1), st1(1), 1.0 / H),
               reads=[bST], writes=[bST])
        Sc.add("dve", lambda e: e.tensor_mul(st1(2), st1(0), st1(0)),
               reads=[bST], writes=[bST])
        Sc.add("dve", lambda e: e.tensor_sub(st1(1), st1(1), st1(2)),
               reads=[bST], writes=[bST])
        Sc.add("act", lambda e: e.activation(st1(2), st1(1), AF.Sqrt, bias=eps_col),
               reads=[bST], writes=[bST])
        Sc.add("dve", lambda e: e.reciprocal(st1(3), st1(2)), reads=[bST],
               writes=[bST])
        Sc.add("dve", lambda e, o=y1: e.tensor_scalar(o, o, st1(0), st1(3),
                                                      ALU.subtract, ALU.mult),
               reads=[bF32[1], bST], writes=[bF32[1]])
        dma("act", A["y1sp"][c_], y1, reads=[bF32[1]], writes=[bDR["y1sp"]])
        y1b = cs(2)
        Sc.add("act", lambda e, o=y1b, i=y1: e.copy(o, i),
               reads=[bF32[1]], writes=[bCOL[2]])

        def mk_y1tp(y1b=y1b, regPb=regPb):
            def f(e):
                last = None
                for ht in range(16):
                    last = nc.tensor.matmul(
                        regPb[:, ht * 128:(ht + 1) * 128],
                        y1b[:, ht * 128:(ht + 1) * 128], idn[:, :],
                        start=(ht % 8 == 0), stop=(ht % 8 == 7),
                        is_transpose=True, skip_group_check=True)
                return last
            return f
        Sc.add("te", mk_y1tp(), reads=[bCOL[2], bIDN], writes=rbk[0:2])
        dstk = kt.rearrange("p (ht t) -> p ht t", ht=16)[:, :, c_ * C:(c_ + 1) * C]
        srck = regPb[:, 0:2048].rearrange("p (ht a) -> p ht a", ht=16)
        Sc.add("act", lambda e, o=dstk, i=srck: e.copy(o, i),
               reads=rbk[0:2],
               writes=[bKTT[c_]] + (list(bKT) if c_ == 0 else []))

    if stop_after == "p3":
        dbg_dram("d_y1sp", "y1sp", [NCH, C, H], F32)
        dbg_sbuf("d_y1t", kt[:, :], list(bKT))
        return finish()

    dma("sp", tabf[:, :], A["sdown"][:, :], reads=[bTAB], writes=[bTAB])
    dma("sp", sgc[:, 0:INTER // C], A["sgate"][:, :], writes=[bSGC])
    dma("sp", sgc[:, INTER // C:], A["sup"][:, :], writes=[bSGC])

    xtf = xt.bitcast(F32)
    gnbbf = gnbb.bitcast(F32)
    snbf = snb.bitcast(F32)

    def acc_half(tg, hh):
        if tg < 3:
            return fs(tg)[:, hh * 1024:(hh + 1) * 1024]
        return gnbbf if hh == 0 else snbf

    def acc_bufs(tg):
        return [bF32[tg]] if tg < 3 else [bGNB, bSNB]

    stv = lambda tg, i: st[:, tg * 8 + i:tg * 8 + i + 1]

    out_dmas = []
    for s_ in range(NSTRIP):
        for it in range(64):
            tb, j = it // 2, it % 2
            par = it % 2
            wgs, wus = cs(4 + 2 * par), cs(5 + 2 * par)
            bwg, bwu = bCOL[4 + 2 * par], bCOL[5 + 2 * par]
            dma("pool", wgs.rearrange("p (h i) -> p h i", h=16),
                A["wgt"][tb, j], reads=[bwg], writes=[bwg], chain=False)
            dma("pool", wus.rearrange("p (h i) -> p h i", h=16),
                A["wut"][tb, j], reads=[bwu], writes=[bwu], chain=False)
            bank_g = bBank[par * 2]
            bank_u = bBank[par * 2 + 1]
            pg = pa[:, par * 1024:par * 1024 + 512]
            pu = pa[:, par * 1024 + 512:par * 1024 + 1024]

            def mk_gu(pg=pg, pu=pu, wgs=wgs, wus=wus, s_=s_):
                def f(e):
                    last = None
                    for ht in range(16):
                        rhs = kt[:, ht * T + s_ * STRIP:ht * T + (s_ + 1) * STRIP]
                        nc.tensor.matmul(
                            pg, wgs[:, ht * 128:(ht + 1) * 128], rhs,
                            start=(ht == 0), stop=(ht == 15),
                            skip_group_check=True)
                        last = nc.tensor.matmul(
                            pu, wus[:, ht * 128:(ht + 1) * 128], rhs,
                            start=(ht == 0), stop=(ht == 15),
                            skip_group_check=True)
                    return last
                return f
            Sc.add("te", mk_gu(), reads=[bwg, bwu] + [bKTT[4 * s_ + k]
                                                      for k in range(4)],
                   writes=[bank_g, bank_u])
            gb = gub[:, par * STRIP:(par + 1) * STRIP]
            Sc.add("act", lambda e, o=gb, i=pg, it=it:
                   e.activation(o, i, AF.Silu, scale=sgc[:, it:it + 1]),
                   reads=[bank_g, bSGC], writes=[bGUB[par]])
            fu = xt[:, it * STRIP:(it + 1) * STRIP]
            Sc.add("dve", lambda e, o=fu, i0=pu, it=it, gb=gb:
                   e.scalar_tensor_tensor(o, i0, sgc[:, 64 + it:64 + it + 1],
                                          gb, ALU.mult, ALU.mult),
                   reads=[bank_u, bGUB[par], bSGC], writes=[bXT[it // 4]])
        if DBG4 and s_ == 0:
            dbg_sbuf("d_fuse", xt[:, :], list(bXT), dtype=BF16, width=16 * T)
            dbg_sbuf("d_y1fm", kt[:, :], list(bKTT), dtype=BF16, width=16 * T)
            dbg_dram("d_ztsp", "ztsp", [NH, C, T], BF16)
        for hh in range(2):
            for grp in range(16):
                par = grp % 2
                wds = cols[:, (par * 2) * T:(par * 2 + 2) * T]
                bwd = [bCOL[par * 2], bCOL[par * 2 + 1]]
                dma("pool", wds.rearrange("p (g h) -> p g h", g=4),
                    A["wdt"][hh, grp], reads=bwd, writes=bwd,
                    dsem=f"dwn{par}", chain=False)

                def mk_down(grp=grp, wds=wds):
                    def f(e):
                        last = None
                        for g4 in range(4):
                            it = grp * 4 + g4
                            for tg in range(4):
                                psum = (pa, pb)[tg // 2]
                                po = (tg % 2) * 1024
                                for jn in range(2):
                                    last = nc.tensor.matmul(
                                        psum[:, po + jn * 512:po + (jn + 1) * 512],
                                        xt[:, it * STRIP + tg * 128:
                                           it * STRIP + tg * 128 + 128],
                                        wds[:, g4 * 1024 + jn * 512:
                                            g4 * 1024 + (jn + 1) * 512],
                                        start=(it == 0), stop=(it == 63),
                                        skip_group_check=True)
                        return last
                    return f
                Sc.add("te", mk_down(), reads=bwd + list(bXT),
                       writes=banksA + banksB)
            for tg in range(4):
                psum = (pa, pb)[tg // 2]
                po = (tg % 2) * 1024
                pbanks = [bBank[tg * 2], bBank[tg * 2 + 1]]
                dst = acc_half(tg, hh)
                Sc.add("dve", lambda e, o=dst, i0=psum[:, po:po + 1024],
                       i1=tabf[:, hh * 1024:(hh + 1) * 1024]:
                       e.tensor_tensor(o, i0, i1, ALU.mult),
                       reads=pbanks + [bTAB] + acc_bufs(tg), writes=acc_bufs(tg))
        for tg in range(4):
            c_ = s_ * 4 + tg
            babf = acc_bufs(tg)
            y1h = [xtf[:, tg * 2048 + hh * 1024:tg * 2048 + (hh + 1) * 1024]
                   for hh in range(2)]
            by1 = [bXT[2 * tg], bXT[2 * tg + 1]]
            dma("act", xtf[:, tg * 2048:(tg + 1) * 2048], A["y1sp"][c_],
                reads=[bDR["y1sp"]] + by1, writes=by1, dsem="y1r", chain=False)
            for hh in range(2):
                Sc.add("dve", lambda e, o=acc_half(tg, hh), i1=y1h[hh]:
                       e.tensor_add(o, o, i1),
                       reads=babf + by1, writes=babf)
            Sc.add("dve", lambda e, o=stv(tg, 0), i=acc_half(tg, 0):
                   e.reduce_sum(o, i, axis=AX.X), reads=babf, writes=[bST])
            Sc.add("dve", lambda e, o=stv(tg, 4), i=acc_half(tg, 1):
                   e.reduce_sum(o, i, axis=AX.X), reads=babf, writes=[bST])
            for hh in range(2):
                Sc.add("act", lambda e, o=y1h[hh], i=acc_half(tg, hh):
                       e.activation(o, i, AF.Square),
                       reads=babf + by1, writes=by1)
            Sc.add("dve", lambda e, o=stv(tg, 1), i=y1h[0]:
                   e.reduce_sum(o, i, axis=AX.X), reads=by1, writes=[bST])
            Sc.add("dve", lambda e, o=stv(tg, 5), i=y1h[1]:
                   e.reduce_sum(o, i, axis=AX.X), reads=by1, writes=[bST])
            Sc.add("dve", lambda e, t=tg: e.tensor_add(stv(t, 0), stv(t, 0),
                                                       stv(t, 4)),
                   reads=[bST], writes=[bST])
            Sc.add("dve", lambda e, t=tg: e.tensor_add(stv(t, 1), stv(t, 1),
                                                       stv(t, 5)),
                   reads=[bST], writes=[bST])
            Sc.add("dve", lambda e, t=tg: e.tensor_scalar_mul(stv(t, 0),
                                                              stv(t, 0), 1.0 / H),
                   reads=[bST], writes=[bST])
            Sc.add("dve", lambda e, t=tg: e.tensor_scalar_mul(stv(t, 1),
                                                              stv(t, 1), 1.0 / H),
                   reads=[bST], writes=[bST])
            Sc.add("dve", lambda e, t=tg: e.tensor_mul(stv(t, 2), stv(t, 0),
                                                       stv(t, 0)),
                   reads=[bST], writes=[bST])
            Sc.add("dve", lambda e, t=tg: e.tensor_sub(stv(t, 1), stv(t, 1),
                                                       stv(t, 2)),
                   reads=[bST], writes=[bST])
            Sc.add("act", lambda e, t=tg: e.activation(stv(t, 2), stv(t, 1),
                                                       AF.Sqrt, bias=eps_col),
                   reads=[bST], writes=[bST])
            Sc.add("dve", lambda e, t=tg: e.reciprocal(stv(t, 3), stv(t, 2)),
                   reads=[bST], writes=[bST])
            for hh in range(2):
                Sc.add("dve", lambda e, o=acc_half(tg, hh), t=tg:
                       e.tensor_scalar(o, o, stv(t, 0), stv(t, 3),
                                       ALU.subtract, ALU.mult),
                       reads=babf + [bST], writes=babf)
                out_dmas.append(
                    dma("act", A["y"][c_ * C:(c_ + 1) * C,
                                      hh * 1024:(hh + 1) * 1024],
                        acc_half(tg, hh), reads=babf, writes=[], dsem="out"))

    Sc.add("sp", lambda e: e.nop(), deps=out_dmas)

    return finish()



def kernel(L0_attn_X, L0_attn_W_Q, L0_attn_W_K, L0_attn_W_V, L0_attn_W_G,
           L0_attn_W_O, L0_attn_gn_weight, L0_attn_gn_bias, L0_ffn_w_gate,
           L0_ffn_w_up, L0_ffn_w_down, L0_ffn_s_gate, L0_ffn_s_up, L0_ffn_s_down):
    X = np.asarray(L0_attn_X)
    cst = _build_host_constants()

    def cat(w):
        m = np.asarray(w).transpose(1, 0, 2).reshape(H, H).astype(np.float32)
        return np.ascontiguousarray(
            m.reshape(16, C, 16, D).transpose(2, 1, 0, 3))
    wq, wk, wv = cat(L0_attn_W_Q), cat(L0_attn_W_K), cat(L0_attn_W_V)
    wg = np.ascontiguousarray(np.asarray(L0_attn_W_G, np.float32)
                              .reshape(16, C, 16, D).transpose(2, 1, 0, 3))
    wo = np.ascontiguousarray(np.asarray(L0_attn_W_O, np.float32))
    gnw = np.ascontiguousarray(np.broadcast_to(
        np.asarray(L0_attn_gn_weight, np.float32), (C, H)))
    gnb = np.ascontiguousarray(np.broadcast_to(
        np.asarray(L0_attn_gn_bias, np.float32), (C, H)))
    wgt = np.ascontiguousarray(
        np.asarray(L0_ffn_w_gate).reshape(16, C, NTB, 2, 128)
        .transpose(2, 3, 1, 0, 4))
    wut = np.ascontiguousarray(
        np.asarray(L0_ffn_w_up).reshape(16, C, NTB, 2, 128)
        .transpose(2, 3, 1, 0, 4))
    wdt = np.ascontiguousarray(
        np.asarray(L0_ffn_w_down).reshape(16, 4, C, 2, 1024)
        .transpose(3, 0, 2, 1, 4))
    sgate = np.ascontiguousarray(
        np.asarray(L0_ffn_s_gate, np.float32).reshape(INTER // C, C).T)
    sup = np.ascontiguousarray(
        np.asarray(L0_ffn_s_up, np.float32).reshape(INTER // C, C).T)
    sdown = np.ascontiguousarray(np.broadcast_to(
        np.asarray(L0_ffn_s_down, np.float32), (C, H)))

    shared = dict(wq=wq, wk=wk, wv=wv, wg=wg, wo=wo, gnw=gnw, gnb=gnb,
                  cq=cst["cq"], sq=cst["sq"], ck=cst["ck"], sk=cst["sk"],
                  gq=cst["gq"], gk=cst["gk"], causal=cst["causal"],
                  ident=cst["ident"], wgt=wgt, wut=wut, wdt=wdt,
                  sgate=sgate, sup=sup, sdown=sdown)

    in_maps = []
    for core in range(N_CORES):
        bq, qq = core // NQ, core % NQ
        m = dict(shared)
        m["x"] = np.ascontiguousarray(X[bq, qq * T:(qq + 1) * T, :].astype(np.float32))
        m["that"] = cst["that_by_core"][core]
        in_maps.append(m)

    nc = build_program()
    res = run_bass_kernel_spmd(nc, in_maps, core_ids=list(range(N_CORES)))

    out = np.empty((B, S, H), np.float32)
    for core in range(N_CORES):
        bq, qq = core // NQ, core % NQ
        out[bq, qq * T:(qq + 1) * T, :] = res.results[core]["y"]
    return out
